# version 04
# baseline (speedup 1.0000x reference)
import numpy as np
from contextlib import ExitStack

import concourse.bass as bass
import concourse.mybir as mybir
from concourse.bass_utils import run_bass_kernel_spmd

F32 = mybir.dt.float32
BF16 = mybir.dt.bfloat16
AF = mybir.ActivationFunctionType
OP = mybir.AluOpType
AX = mybir.AxisListType

D = 1024
KT = 8
EPS = 1e-5

_WRAP = {
    "tensor_copy", "tensor_tensor", "tensor_add", "tensor_sub", "tensor_mul",
    "tensor_max", "tensor_scalar", "tensor_single_scalar", "tensor_scalar_add",
    "tensor_scalar_sub", "tensor_scalar_mul", "tensor_scalar_max",
    "tensor_scalar_min", "tensor_relu", "tensor_reduce", "reduce_max",
    "reduce_sum", "scalar_tensor_tensor", "tensor_tensor_scan", "memset",
    "memzero", "activation", "sign", "square", "sqrt", "mul", "copy", "add",
    "max", "max_index", "match_replace", "select", "copy_predicated",
    "reciprocal", "bn_stats", "bn_aggr", "transpose", "pool", "pool_max",
    "pool_avg", "stream_shuffle", "iota", "affine_select",
}


class Sems:
    def __init__(self, nc, ctx, names):
        self.h = {n: ctx.enter_context(nc.semaphore(n)) for n in names}
        self.c = {n: 0 for n in names}

    def inc(self, inst, name, k=1):
        self.c[name] += k
        inst.then_inc(self.h[name], k)
        return self.c[name]

    def wait(self, eng, name, val=None):
        v = self.c[name] if val is None else val
        if v > 0:
            e = eng.e if isinstance(eng, Eng) else eng
            e.wait_ge(self.h[name], v)

    def mark(self, name):
        return self.c[name]


class Eng:

    def __init__(self, eng, s, ch):
        self.e = eng
        self.s = s
        self.ch = ch

    def __getattr__(self, k):
        f = getattr(self.e, k)
        if k in _WRAP:
            s, ch, e = self.s, self.ch, self.e

            def g(*a, **kw):
                n = s.c[ch]
                if n:
                    e.wait_ge(s.h[ch], n)
                inst = f(*a, **kw)
                s.c[ch] += 1
                inst.then_inc(s.h[ch], 1)
                return inst
            return g
        return f


class P:
    def __init__(self, name):
        self.nc = bass.Bass("TRN2", target_bir_lowering=False)
        self.ctx = ExitStack()
        self.name = name

    def sb(self, name, shape, dtype):
        return self.ctx.enter_context(self.nc.sbuf_tensor(name, shape, dtype))

    def ps(self, name, shape, dtype=F32):
        return self.ctx.enter_context(self.nc.psum_tensor(name, shape, dtype))

    def engines(self, sem_names):
        nc = self.nc
        s = Sems(nc, self.ctx, ["v", "a", "g", "t"] + sem_names)
        self.s = s
        self.blk = self.ctx.enter_context(nc.Block())
        self.sync = nc.sync
        self.gpsd = nc.gpsimd
        self.ten = nc.tensor
        self.vec = Eng(nc.vector, s, "v")
        self.act = Eng(nc.scalar, s, "a")
        self.gpc = Eng(nc.gpsimd, s, "g")
        return s


def _p_sbx(self, st, name, shape, dtype):
    return st.enter_context(self.nc.sbuf_tensor(name, shape, dtype))


def _p_psx(self, st, name, shape, dtype=F32):
    return st.enter_context(self.nc.psum_tensor(name, shape, dtype))


P.sbx = _p_sbx
P.psx = _p_psx


def emit_ln(p, s, T, x_hbm, xn, xch, xbf, xsq, stats, t1, ones_bf, ones_f,
            pb_s, pb_ss, pb_mb, pb_rb, xd="xd"):
    sync, ten, vec, act = p.sync, p.ten, p.vec, p.act
    for c in range(T // 512):
        sl = slice(c * 512, (c + 1) * 512)
        s.wait(sync, "v")
        last = sync.dma_start(out=xch[:, :, :],
                              in_=x_hbm[:, sl].rearrange("(k p) t -> p k t", p=128))
        s.inc(last, xd, 16)
        s.wait(vec, xd)
        s.wait(vec, "t")
        for kt in range(KT):
            vec.tensor_copy(xbf[:, :], xch[:, kt, :])
            vec.tensor_mul(xsq[:, :], xbf[:, :], xbf[:, :])
            s.wait(ten, "v")
            ten.matmul(pb_s[0:1, :], ones_bf[:, :], xbf[:, :],
                       start=(kt == 0), stop=(kt == KT - 1))
            last = ten.matmul(pb_ss[0:1, :], ones_bf[:, :], xsq[:, :],
                              start=(kt == 0), stop=(kt == KT - 1))
            s.inc(last, "t")
            s.wait(vec, "t")
        vec.tensor_scalar(stats[0:1, 2, :], pb_s[0:1, :], 1.0 / D, None, op0=OP.mult)
        vec.tensor_scalar(stats[0:1, 1, :], pb_ss[0:1, :], 1.0 / D, None, op0=OP.mult)
        vec.tensor_mul(stats[0:1, 3, :], stats[0:1, 2, :], stats[0:1, 2, :])
        vec.tensor_sub(stats[0:1, 3, :], stats[0:1, 1, :], stats[0:1, 3, :])
        vec.tensor_scalar(stats[0:1, 3, :], stats[0:1, 3, :], EPS, None, op0=OP.add)
        s.wait(act, "v")
        act.activation(stats[0:1, 4, :], stats[0:1, 3, :], AF.Ln)
        act.activation(stats[0:1, 4, :], stats[0:1, 4, :], AF.Exp, scale=-0.5)
        s.wait(ten, "a")
        s.wait(ten, "v")
        ten.matmul(pb_mb[:, :], ones_f[:, :], stats[0:1, 2, :], start=True, stop=True)
        last = ten.matmul(pb_rb[:, :], ones_f[:, :], stats[0:1, 4, :], start=True, stop=True)
        s.inc(last, "t")
        s.wait(vec, "t")
        for kt in range(KT):
            vec.tensor_sub(t1[:, :], xch[:, kt, :], pb_mb[:, :])
            vec.tensor_mul(xn[:, kt, sl], t1[:, :], pb_rb[:, :])


class FlatSrc:
    bf16 = False

    def __init__(self, t):
        self.t = t

    def ap(self, rows, cols):
        return self.t[rows, cols]


class AgSrc:
    bf16 = True

    def __init__(self, t, half):
        self.t = t
        self.half = half

    def ap(self, rows, cols):
        blk = cols.start // self.half
        off = blk * D
        return self.t[off + rows.start:off + rows.stop,
                      cols.start - blk * self.half:cols.stop - blk * self.half]


def emit_ln2(p, s, T, xsrc, xn, xch, xbf, xsq, stats, t1, ones_bf, ones_f,
             pb_s, pb_ss, pb_mb, pb_rb):
    nc = p.nc
    sync, gps, ten, vec, act = p.sync, p.gpsd, p.ten, p.vec, p.act
    ldeng, ldch = (gps, "xg") if xsrc.bf16 else (sync, "xd")
    for c in range(T // 512):
        sl = slice(c * 512, (c + 1) * 512)
        s.wait(ldeng, "v")
        last = ldeng.dma_start(
            out=xch[:, :, :],
            in_=xsrc.ap(slice(0, D), sl).rearrange("(k p) t -> p k t", p=128))
        s.inc(last, ldch, 16)
        s.wait(vec, ldch)
        s.wait(vec, "t")
        for kt in range(KT):
            vec.tensor_copy(xbf[:, :], xch[:, kt, :])
            vec.tensor_mul(xsq[:, :], xbf[:, :], xbf[:, :])
            s.wait(ten, "v")
            ten.matmul(pb_s[0:1, :], ones_bf[:, :], xbf[:, :],
                       start=(kt == 0), stop=(kt == KT - 1))
            last = ten.matmul(pb_ss[0:1, :], ones_bf[:, :], xsq[:, :],
                              start=(kt == 0), stop=(kt == KT - 1))
            s.inc(last, "t")
            s.wait(vec, "t")
        vec.tensor_scalar(stats[0:1, 2, :], pb_s[0:1, :], 1.0 / D, None, op0=OP.mult)
        vec.tensor_scalar(stats[0:1, 1, :], pb_ss[0:1, :], 1.0 / D, None, op0=OP.mult)
        vec.tensor_mul(stats[0:1, 3, :], stats[0:1, 2, :], stats[0:1, 2, :])
        vec.tensor_sub(stats[0:1, 3, :], stats[0:1, 1, :], stats[0:1, 3, :])
        vec.tensor_scalar(stats[0:1, 3, :], stats[0:1, 3, :], EPS, None, op0=OP.add)
        s.wait(act, "v")
        act.activation(stats[0:1, 4, :], stats[0:1, 3, :], AF.Ln)
        act.activation(stats[0:1, 4, :], stats[0:1, 4, :], AF.Exp, scale=-0.5)
        s.wait(ten, "a")
        s.wait(ten, "v")
        ten.matmul(pb_mb[:, :], ones_f[:, :], stats[0:1, 2, :], start=True, stop=True)
        last = ten.matmul(pb_rb[:, :], ones_f[:, :], stats[0:1, 4, :], start=True, stop=True)
        s.inc(last, "t")
        s.wait(vec, "t")
        for kt in range(KT):
            vec.tensor_sub(t1[:, :], xch[:, kt, :], pb_mb[:, :])
            vec.tensor_mul(xn[:, kt, sl], t1[:, :], pb_rb[:, :])


def stage_barrier(p, s):
    engs = [p.sync, p.gpsd, p.ten, p.vec, p.act]
    for e in engs:
        for ch in s.c:
            s.wait(e, ch)


def emit_ag(p, s, src_f32, agin, agout):
    gps = p.gpsd
    lw = gps.dma_start(out=agin, in_=src_f32)
    s.inc(lw, "xg", 16)
    s.wait(gps, "xg")
    cc = gps.collective_compute(
        "AllGather", OP.bypass,
        replica_groups=[[0, 1], [2, 3], [4, 5], [6, 7]],
        ins=[agin.opt()], outs=[agout.opt()])
    s.inc(cc, "cc", 1)
    s.wait(gps, "cc")


def emit_moe(p, s, io, pref, T=2048, TH=1024, E=9, I=2048):
    IT = I // 128
    NH = T // TH
    nc = p.nc
    from contextlib import ExitStack as _ES0
    with _ES0() as _stage:
        x = io["x"]
        wg = io["wg"]
        wu = io["wu"]
        wdr = io["wdr"]
        wr = io["wr"]
        rbv = io["rbv"]
        identf = io["identf"]
        y = io["y"]

        xn = p.sbx(_stage, pref + "xn", [128, KT, T], BF16)
        xch = p.sbx(_stage, pref + "xch", [128, KT, 512], F32)
        xr = p.sbx(_stage, pref + "xr", [128, 512], F32)
        xbf = p.sbx(_stage, pref + "xbf", [128, 512], BF16)
        xsq = p.sbx(_stage, pref + "xsq", [128, 512], BF16)
        stats = p.sbx(_stage, pref + "stats", [1, 5, 512], F32)
        t1 = p.sbx(_stage, pref + "t1", [128, 512], F32)
        ones_bf = p.sbx(_stage, pref + "ones_bf", [128, 1], BF16)
        ones_f = p.sbx(_stage, pref + "ones_f", [1, 128], F32)
        ones_bf1 = p.sbx(_stage, pref + "ones_bf1", [1, 128], BF16)
        ident = p.sbx(_stage, pref + "ident", [128, 128], F32)
        rbt = p.sbx(_stage, pref + "rbt", [128, 8], F32)
        etmp = p.sbx(_stage, pref + "etmp", [128, 8], F32)
        probs = p.sbx(_stage, pref + "probs", [128, 8], F32)
        m8 = p.sbx(_stage, pref + "m8", [128, 8], F32)
        cwt = p.sbx(_stage, pref + "cwt", [128, 8], F32)
        cwT = p.sbx(_stage, pref + "cwT", [1, 8, T], BF16)
        cwbc = p.sbx(_stage, pref + "cwbc", [128, 512], BF16)
        wrsb = p.sbx(_stage, pref + "wrsb", [128, KT, 8], BF16)
        xe = p.sbx(_stage, pref + "xe", [128, KT, TH], BF16)
        wgsb = [p.sbx(_stage, pref + f"wgsb{i}", [128, KT, 128], BF16) for i in range(2)]
        wusb = [p.sbx(_stage, pref + f"wusb{i}", [128, KT, 128], BF16) for i in range(2)]
        wdsb = [p.sbx(_stage, pref + f"wdsb{i}", [128, IT, 128], BF16) for i in range(2)]
        sg = p.sbx(_stage, pref + "sg", [128, 512], BF16)
        gu = p.sbx(_stage, pref + "gu", [128, IT, TH], BF16)
        yac = p.sbx(_stage, pref + "yac", [128, KT, TH], F32)

        pb = [p.psx(_stage, pref + f"pb{i}", [128, 512]) for i in range(6)]

        sync, gps, ten = p.sync, p.gpsd, p.ten
        vec, act = p.vec, p.act

        vec.memset(ones_bf[:, :], 1.0)
        vec.memset(ones_f[:, :], 1.0)
        vec.memset(ones_bf1[:, :], 1.0)
        last = sync.dma_start(out=ident[:, :], in_=identf)
        s.inc(last, "xd", 16)
        last = sync.dma_start(out=rbt[:, :], in_=rbv.partition_broadcast(128))
        s.inc(last, "xd", 16)
        last = gps.dma_start(out=wrsb[:, :, :],
                             in_=wr.rearrange("(k p) e -> p k e", p=128))
        s.inc(last, "wr0", 16)

        emit_ln2(p, s, T, FlatSrc(x), xn, xch, xbf, xsq, stats, t1, ones_bf, ones_f,
                pb[0][0:1, :], pb[3][0:1, :], pb[1], pb[2])

        s.wait(ten, "wr0")
        s.wait(ten, "xd")
        s.wait(ten, "v")
        for tt in range(T // 128):
            tsl = slice(tt * 128, (tt + 1) * 128)
            for kt in range(KT):
                lastm = ten.matmul(pb[3][:, 0:8], xn[:, kt, tsl], wrsb[:, kt, :],
                                   start=(kt == 0), stop=(kt == KT - 1))
            s.inc(lastm, "t")
            s.wait(vec, "t")
            vec.tensor_add(etmp[:, :], pb[3][:, 0:8], rbt[:, :])
            s.wait(act, "v")
            act.activation(etmp[:, :], etmp[:, :], AF.Exp, scale=-1.0)
            s.wait(vec, "a")
            vec.tensor_scalar(etmp[:, :], etmp[:, :], 1.0, None, op0=OP.add)
            vec.reciprocal(probs[:, :], etmp[:, :])
            vec.max(out=m8[:, :], in_=probs[:, :])
            vec.tensor_scalar(cwt[:, :], probs[:, :], m8[:, 1:2], None, op0=OP.is_ge)
            vec.tensor_mul(cwt[:, :], probs[:, :], cwt[:, :])
            s.wait(ten, "v")
            for half8 in range(2):
                for e8 in range(4):
                    ee = half8 * 4 + e8
                    last = ten.transpose(pb[4][0:1, e8 * 128:(e8 + 1) * 128],
                                         cwt[:, ee:ee + 1], ident[:, :])
                s.inc(last, "t")
                s.wait(vec, "t")
                for e8 in range(4):
                    ee = half8 * 4 + e8
                    vec.tensor_copy(cwT[0:1, ee, tsl],
                                    pb[4][0:1, e8 * 128:(e8 + 1) * 128])
                s.wait(ten, "v")

        wslot_mm = [0, 0]
        wd_slot_mm = [0, 0]
        xe_last_mm = 0
        for h in range(NH):
            s.wait(vec, "t")
            vec.memset(yac[:, :, :], 0.0)
            for e in range(E):
                if e > 0:
                    s.wait(vec, "t", xe_last_mm)
                    for c in range(TH // 512):
                        csl = slice(h * TH + c * 512, h * TH + (c + 1) * 512)
                        osl = slice(c * 512, (c + 1) * 512)
                        s.wait(ten, "v")
                        lastm = ten.matmul(pb[5][:, :], ones_bf1[:, :],
                                           cwT[0:1, e - 1, csl], start=True, stop=True)
                        s.inc(lastm, "t")
                        s.wait(vec, "t")
                        vec.tensor_copy(cwbc[:, :], pb[5][:, :])
                        for kt in range(KT):
                            vec.tensor_mul(xe[:, kt, osl], xn[:, kt, csl], cwbc[:, :])
                    s.wait(ten, "v")
                for ii in range(IT):
                    sl_i = ii % 2
                    ch = f"wl{sl_i}"
                    s.wait(gps, "t", wslot_mm[sl_i])
                    lw = gps.dma_start(out=wgsb[sl_i][:, :, :],
                                       in_=wg[e, ii].rearrange("k p c -> p k c"))
                    s.inc(lw, ch, 16)
                    lw = gps.dma_start(out=wusb[sl_i][:, :, :],
                                       in_=wu[e, ii].rearrange("k p c -> p k c"))
                    s.inc(lw, ch, 16)
                    s.wait(ten, ch)
                    for c in range(TH // 512):
                        osl = slice(c * 512, (c + 1) * 512)
                        csl = slice(h * TH + c * 512, h * TH + (c + 1) * 512)
                        s.wait(ten, "v")
                        s.wait(ten, "a")
                        for kt in range(KT):
                            ten.matmul(pb[0][:, :], wgsb[sl_i][:, kt, :],
                                       xn[:, kt, csl],
                                       start=(kt == 0), stop=(kt == KT - 1))
                        for kt in range(KT):
                            rv = xn[:, kt, csl] if e == 0 else xe[:, kt, osl]
                            lastm = ten.matmul(pb[2][:, :], wusb[sl_i][:, kt, :], rv,
                                               start=(kt == 0), stop=(kt == KT - 1))
                        s.inc(lastm, "t")
                        s.wait(act, "t")
                        s.wait(act, "v")
                        act.activation(sg[:, :], pb[0][:, :], AF.Silu)
                        s.wait(vec, "a")
                        vec.tensor_mul(gu[:, ii, osl], sg[:, :], pb[2][:, :])
                    wslot_mm[sl_i] = s.mark("t")
                xe_last_mm = s.mark("t")
                for oo in range(KT):
                    sl_o = oo % 2
                    ch = f"wd{sl_o}"
                    s.wait(gps, "t", wd_slot_mm[sl_o])
                    lw = gps.dma_start(out=wdsb[sl_o][:, :, :],
                                       in_=wdr[e, oo].rearrange("i p c -> p i c"))
                    s.inc(lw, ch, 16)
                    s.wait(ten, ch)
                    if oo == 0:
                        s.wait(ten, "v")
                    for c in range(TH // 512):
                        osl = slice(c * 512, (c + 1) * 512)
                        for ii in range(IT):
                            lastm = ten.matmul(
                                pb[4][:, :], wdsb[sl_o][:, ii, :], gu[:, ii, osl],
                                start=(ii == 0), stop=(ii == IT - 1))
                        s.inc(lastm, "t")
                        s.wait(vec, "t")
                        vec.tensor_add(yac[:, oo, osl], yac[:, oo, osl], pb[4][:, :])
                        s.wait(ten, "v")
                    wd_slot_mm[sl_o] = s.mark("t")
            for kt in range(KT):
                for c in range(TH // 512):
                    osl = slice(c * 512, (c + 1) * 512)
                    csl = slice(h * TH + c * 512, h * TH + (c + 1) * 512)
                    s.wait(sync, "v")
                    last = sync.dma_start(out=xr[:, :],
                                          in_=x[kt * 128:(kt + 1) * 128, csl])
                    s.inc(last, "xd", 16)
                    s.wait(vec, "xd")
                    s.wait(vec, "st")
                    vec.tensor_add(t1[:, :], xr[:, :], yac[:, kt, osl])
                    s.wait(sync, "v")
                    last = sync.dma_start(out=y[kt * 128:(kt + 1) * 128, csl],
                                          in_=t1[:, :])
                    s.inc(last, "st", 16)
        s.wait(sync, "st")
        s.wait(gps, "st")




def emit_a0(p, s, io, pref, S=4096, Q=2048):
    H = 16
    NKT2 = S // 128
    NQC = Q // 512
    nc = p.nc
    from contextlib import ExitStack as _ES0
    with _ES0() as _stage:
        xf = io["xf"]
        wqkv = io["wqkv"]
        wproj = io["wproj"]
        bproj = io["bproj"]
        exb = io["exb"]
        y = io["y"]

        xq = io["xq"]
        xn = p.sbx(_stage, pref + "xn", [128, KT, S], BF16)
        xnq = p.sbx(_stage, pref + "xnq", [128, KT, Q], BF16)
        xr = p.sbx(_stage, pref + "xr", [128, 512], F32)
        xbf = p.sbx(_stage, pref + "xbf", [128, 512], BF16)
        xsq = p.sbx(_stage, pref + "xsq", [128, 512], BF16)
        stats = p.sbx(_stage, pref + "stats", [1, 5, 512], F32)
        t1 = p.sbx(_stage, pref + "t1", [128, 512], F32)
        ones_bf = p.sbx(_stage, pref + "ones_bf", [128, 1], BF16)
        ones_f = p.sbx(_stage, pref + "ones_f", [1, 128], F32)
        wsb = [p.sbx(_stage, pref + f"wsb{i}", [128, KT, 128], BF16) for i in range(2)]
        kfm = p.sbx(_stage, pref + "kfm", [128, S], BF16)
        qfm = p.sbx(_stage, pref + "qfm", [128, Q], BF16)
        vtm = p.sbx(_stage, pref + "vtm", [128, NKT2, 130], BF16)
        ptile = [p.sbx(_stage, pref + f"ptile{i}", [128, 512], BF16) for i in range(2)]

        pb = [p.psx(_stage, pref + f"pb{i}", [128, 512]) for i in range(6)]

        sync, gps, ten = p.sync, p.gpsd, p.ten
        vec, act = p.vec, p.act

        vec.memset(ones_bf[:, :], 1.0)
        vec.memset(ones_f[:, :], 1.0)
        vec.memset(vtm[:, :, 64:65], 1.0)
        vec.memset(vtm[:, :, 129:130], 1.0)

        from contextlib import ExitStack as _ESL
        with _ESL() as _lnctx:
            xch = p.sbx(_lnctx, pref + "xch", [128, KT, 512], F32)
            emit_ln2(p, s, S, io["xfsrc"], xn, xch, xbf, xsq, stats, t1, ones_bf, ones_f,
                     pb[0][0:1, :], pb[3][0:1, :], pb[1], pb[2])
            emit_ln2(p, s, Q, FlatSrc(xq), xnq, xch, xbf, xsq, stats, t1, ones_bf, ones_f,
                     pb[0][0:1, :], pb[3][0:1, :], pb[1], pb[2])
        exbs = p.sbx(_stage, pref + "exbs", [128, NKT2, 512], BF16)
        ofm = p.sbx(_stage, pref + "ofm", [128, KT, Q], BF16)
        osb = p.sbx(_stage, pref + "osb", [128, 512], BF16)
        rd1 = p.sbx(_stage, pref + "rd1", [1, 512], F32)

        wslot_mm = [0, 0]

        def load_w(cols_off, slot, ch):
            s.wait(gps, "t", wslot_mm[slot])
            lw = gps.dma_start(
                out=wsb[slot][:, :, :],
                in_=wqkv[:, cols_off:cols_off + 128].rearrange(
                    "(k p) c -> p k c", p=128))
            s.inc(lw, ch, 16)

        s.wait(ten, "v")
        for hp in range(8):
            load_w(1024 + hp * 128, 0, "wl0")
            s.wait(ten, "wl0")
            s.wait(ten, "a")
            s.wait(ten, "v")
            for c in range(S // 512):
                sl = slice(c * 512, (c + 1) * 512)
                s.wait(ten, "v")
                for kt in range(KT):
                    lastm = ten.matmul(pb[4][:, :], wsb[0][:, kt, :], xn[:, kt, sl],
                                       start=(kt == 0), stop=(kt == KT - 1))
                s.inc(lastm, "t")
                s.wait(vec, "t")
                vec.tensor_copy(kfm[:, sl], pb[4][:, :])
            wslot_mm[0] = s.mark("t")
            load_w(hp * 128, 1, "wl1")
            s.wait(ten, "wl1")
            s.wait(ten, "v")
            for c in range(NQC):
                sl = slice(c * 512, (c + 1) * 512)
                s.wait(ten, "v")
                for kt in range(KT):
                    lastm = ten.matmul(pb[5][:, :], wsb[1][:, kt, :], xnq[:, kt, sl],
                                       start=(kt == 0), stop=(kt == KT - 1))
                s.inc(lastm, "t")
                s.wait(vec, "t")
                vec.tensor_copy(qfm[:, sl], pb[5][:, :])
            wslot_mm[1] = s.mark("t")
            load_w(2048 + hp * 128, 0, "wl0")
            s.wait(ten, "wl0")
            s.wait(ten, "v")
            for tt in range(NKT2):
                tsl = slice(tt * 128, (tt + 1) * 128)
                s.wait(ten, "v")
                for kt in range(KT):
                    lastm = ten.matmul(pb[4][:, 0:128], xn[:, kt, tsl],
                                       wsb[0][:, kt, :],
                                       start=(kt == 0), stop=(kt == KT - 1))
                s.inc(lastm, "t")
                s.wait(vec, "t")
                vec.tensor_copy(vtm[:, tt, 0:64], pb[4][:, 0:64])
                vec.tensor_copy(vtm[:, tt, 65:129], pb[4][:, 64:128])
            wslot_mm[0] = s.mark("t")
            s.wait(ten, "v")

            for qc in range(NQC):
                qsl = slice(qc * 512, (qc + 1) * 512)
                s.wait(sync, "v")
                last = sync.dma_start(
                    out=exbs[:, :, :],
                    in_=exb[:, qsl].rearrange("(kt p) q -> p kt q", p=128))
                s.inc(last, "eb", 16)
                s.wait(vec, "eb")
                for h2 in range(2):
                    hb = 64 * h2
                    opb = pb[2]
                    pv_done_v = [0, 0]
                    for kt2 in range(NKT2 + 1):
                        if kt2 < NKT2:
                            sb_ = kt2 % 2
                            s.wait(ten, "a", s.mark("a") if kt2 >= 2 else 0)
                            lastm = ten.matmul(
                                pb[sb_][:, :],
                                kfm[hb:hb + 64, kt2 * 128:(kt2 + 1) * 128],
                                qfm[hb:hb + 64, qsl],
                                start=True, stop=True, skip_group_check=True)
                            s.inc(lastm, "t")
                            s.wait(act, "t")
                            s.wait(act, "v", pv_done_v[sb_])
                            act.activation(ptile[sb_][:, :], pb[sb_][:, :],
                                           AF.Exp, scale=0.125)
                            s.wait(vec, "a")
                            vec.tensor_mul(ptile[sb_][:, :], ptile[sb_][:, :],
                                           exbs[:, kt2, :])
                            pv_done_v[sb_] = s.mark("v")
                        if kt2 > 0:
                            kk = kt2 - 1
                            sb_ = kk % 2
                            s.wait(ten, "v", pv_done_v[sb_])
                            lastm = ten.matmul(
                                opb[0:65, :],
                                vtm[:, kk, 65 * h2:65 * h2 + 65],
                                ptile[sb_][:, :],
                                start=(kk == 0), stop=(kk == NKT2 - 1),
                                skip_group_check=True)
                            s.inc(lastm, "t")
                    s.wait(vec, "t")
                    vec.reciprocal(rd1[0:1, :], opb[64:65, :])
                    s.wait(ten, "v")
                    lastm = ten.matmul(pb[3][0:64, :], ones_f[:, 0:64],
                                       rd1[0:1, :], start=True, stop=True)
                    s.inc(lastm, "t")
                    s.wait(vec, "t")
                    vec.tensor_copy(osb[0:64, :], opb[0:64, :])
                    h_global = hp * 2 + h2
                    otile = h_global // 2
                    orow = 64 * (h_global % 2)
                    vec.tensor_mul(ofm[orow:orow + 64, otile, qsl],
                                   osb[0:64, :], pb[3][0:64, :])
                    s.wait(ten, "v")

        bsb = p.sbx(_stage, pref + "bsb", [128, KT, 1], F32)
        last = sync.dma_start(out=bsb[:, :, :],
                              in_=bproj.rearrange("(k p) o -> p k o", p=128),
                              allow_slow_non_contiguous=True)
        s.inc(last, "xd", 16)
        wslot2 = list(wslot_mm)
        for ot in range(KT):
            slot = ot % 2
            ch = f"wl{slot}"
            s.wait(gps, "t", wslot2[slot])
            lw = gps.dma_start(
                out=wsb[slot][:, :, :],
                in_=wproj[:, ot * 128:(ot + 1) * 128].rearrange(
                    "(k p) c -> p k c", p=128))
            s.inc(lw, ch, 16)
            s.wait(ten, ch)
            for c in range(NQC):
                sl = slice(c * 512, (c + 1) * 512)
                s.wait(ten, "v")
                for kt in range(KT):
                    lastm = ten.matmul(pb[4][:, :], wsb[slot][:, kt, :],
                                       ofm[:, kt, sl], start=(kt == 0),
                                       stop=(kt == KT - 1))
                s.inc(lastm, "t")
                s.wait(sync, "v")
                last = sync.dma_start(out=xr[:, :],
                                      in_=xq[ot * 128:(ot + 1) * 128, sl])
                s.inc(last, "xd", 16)
                s.wait(vec, "xd")
                s.wait(vec, "t")
                s.wait(vec, "st")
                vec.tensor_scalar(t1[:, :], pb[4][:, :], bsb[:, ot, 0:1], None,
                                  op0=OP.add)
                vec.tensor_add(t1[:, :], t1[:, :], xr[:, :])
                s.wait(sync, "v")
                last = sync.dma_start(out=y[ot * 128:(ot + 1) * 128, sl],
                                      in_=t1[:, :])
                s.inc(last, "st", 16)
            wslot2[slot] = s.mark("t")
        s.wait(sync, "st")
        s.wait(gps, "st")



def emit_b0(p, s, io, pref, S=4096, Q=2048, A=3584, W=512, TOPK=64):
    NKT2 = S // 128
    NA = A // 128
    NQC = Q // 512
    nc = p.nc
    from contextlib import ExitStack as _ES0
    with _ES0() as _stage:
        xf = io["xf"]
        xq = io["xq"]
        wqw = io["wqw"]
        wkw = io["wkw"]
        wvw = io["wvw"]
        wow = io["wow"]
        wqb = io["wqb"]
        wkb = io["wkb"]
        wvb = io["wvb"]
        wob = io["wob"]
        wqiw = io["wqiw"]
        wkiw = io["wkiw"]
        wqib = io["wqib"]
        wkib = io["wkib"]
        identf = io["identf"]
        y = io["y"]
        odram = io["odram"]

        xn = p.sbx(_stage, pref + "xn", [128, KT, S], BF16)
        xnq = p.sbx(_stage, pref + "xnq", [128, KT, Q], BF16)
        xr = p.sbx(_stage, pref + "xr", [128, 512], F32)
        xbf = p.sbx(_stage, pref + "xbf", [128, 512], BF16)
        xsq = p.sbx(_stage, pref + "xsq", [128, 512], BF16)
        stats = p.sbx(_stage, pref + "stats", [1, 5, 512], F32)
        t1 = p.sbx(_stage, pref + "t1", [128, 512], F32)
        ones_bf = p.sbx(_stage, pref + "ones_bf", [128, 1], BF16)
        ones_f = p.sbx(_stage, pref + "ones_f", [1, 128], F32)
        ident = p.sbx(_stage, pref + "ident", [128, 128], F32)
        wsb = [p.sbx(_stage, pref + f"wsb{i}", [128, KT, 128], BF16) for i in range(2)]
        wisb = p.sbx(_stage, pref + "wisb", [128, KT, 128], BF16)
        ibia = p.sbx(_stage, pref + "ibia", [128, 2], F32)
        qidx = p.sbx(_stage, pref + "qidx", [64, Q], BF16)
        kidx = p.sbx(_stage, pref + "kidx", [64, A], BF16)
        thrr = p.sbx(_stage, pref + "thrr", [1, Q], F32)
        thrb = p.sbx(_stage, pref + "thrb", [128, 512], BF16)
        m8 = p.sbx(_stage, pref + "m8", [128, 8], F32)
        bcv = p.sbx(_stage, pref + "bcv", [128, 128], F32)
        kfm = p.sbx(_stage, pref + "kfm", [128, S], BF16)
        qfm = p.sbx(_stage, pref + "qfm", [128, Q], BF16)
        vtm = p.sbx(_stage, pref + "vtm", [128, NKT2, 130], BF16)
        ptile = p.sbx(_stage, pref + "ptile", [128, 512], BF16)
        osb = p.sbx(_stage, pref + "osb", [128, 512], BF16)
        rd1 = p.sbx(_stage, pref + "rd1", [1, 512], F32)
        bsb = p.sbx(_stage, pref + "bsb", [128, KT, 1], F32)
        bqk = p.sbx(_stage, pref + "bqk", [128, 2], F32)
        wvb_sb = p.sbx(_stage, pref + "wvb_sb", [1, D], F32)

        pb = [p.psx(_stage, pref + f"pb{i}", [128, 512]) for i in range(6)]

        sync, gps, ten = p.sync, p.gpsd, p.ten
        vec, act = p.vec, p.act

        vec.memset(ones_bf[:, :], 1.0)
        vec.memset(ones_f[:, :], 1.0)
        vec.memset(vtm[:, :, 64:65], 1.0)
        vec.memset(vtm[:, :, 129:130], 1.0)
        last = sync.dma_start(out=ident[:, :], in_=identf)
        s.inc(last, "xd", 16)
        last = sync.dma_start(out=bsb[:, :, :],
                              in_=wob.rearrange("(k p) o -> p k o", p=128),
                              allow_slow_non_contiguous=True)
        s.inc(last, "xd", 16)
        last = sync.dma_start(out=ibia[0:64, 0:1], in_=wqib)
        s.inc(last, "xd", 16)
        last = sync.dma_start(out=ibia[0:64, 1:2], in_=wkib)
        s.inc(last, "xd", 16)
        last = sync.dma_start(out=wvb_sb[:, :], in_=wvb)
        s.inc(last, "xd", 16)
        lw = gps.dma_start(out=wisb[:, :, 0:64],
                           in_=wqiw.rearrange("(k p) c -> p k c", p=128))
        s.inc(lw, "ld1", 16)
        lw = gps.dma_start(out=wisb[:, :, 64:128],
                           in_=wkiw.rearrange("(k p) c -> p k c", p=128))
        s.inc(lw, "ld1", 16)

        from contextlib import ExitStack as _ES
        with _ES() as sub:
            xch = sub.enter_context(nc.sbuf_tensor(pref + "xch", [128, KT, 512], F32))
            emit_ln2(p, s, S, io["xfsrc"], xn, xch, xbf, xsq, stats, t1, ones_bf, ones_f,
                    pb[0][0:1, :], pb[3][0:1, :], pb[1], pb[2])
            emit_ln2(p, s, Q, FlatSrc(xq), xnq, xch, xbf, xsq, stats, t1, ones_bf, ones_f,
                    pb[0][0:1, :], pb[3][0:1, :], pb[1], pb[2])

        s.wait(ten, "ld1")
        s.wait(ten, "xd")
        s.wait(ten, "v")
        for c in range(NQC):
            sl = slice(c * 512, (c + 1) * 512)
            s.wait(ten, "v")
            for kt in range(KT):
                lastm = ten.matmul(pb[4][0:64, :], wisb[:, kt, 0:64],
                                   xnq[:, kt, sl], start=(kt == 0),
                                   stop=(kt == KT - 1))
            s.inc(lastm, "t")
            s.wait(vec, "t")
            vec.tensor_scalar(qidx[:, sl], pb[4][0:64, :], ibia[0:64, 0:1], None,
                              op0=OP.add)
        for c in range(A // 512):
            sl = slice(c * 512, (c + 1) * 512)
            s.wait(ten, "v")
            for kt in range(KT):
                lastm = ten.matmul(pb[4][0:64, :], wisb[:, kt, 64:128],
                                   xn[:, kt, sl], start=(kt == 0),
                                   stop=(kt == KT - 1))
            s.inc(lastm, "t")
            s.wait(vec, "t")
            vec.tensor_scalar(kidx[:, sl], pb[4][0:64, :], ibia[0:64, 1:2], None,
                              op0=OP.add)

        with _ES() as sub:
            scr = sub.enter_context(nc.sbuf_tensor(pref + "scr", [128, A], F32))
            for qt in range(Q // 128):
                tsl = slice(qt * 128, (qt + 1) * 128)
                for kc in range(A // 512):
                    ksl = slice(kc * 512, (kc + 1) * 512)
                    s.wait(ten, "a")
                    s.wait(ten, "v")
                    lastm = ten.matmul(pb[4][:, :], qidx[:, tsl], kidx[:, ksl],
                                       start=True, stop=True,
                                       skip_group_check=True)
                    s.inc(lastm, "t")
                    s.wait(act, "t")
                    s.wait(act, "v")
                    act.activation(scr[:, ksl], pb[4][:, :], AF.Relu)
                for r in range(TOPK // 8):
                    s.wait(vec, "a")
                    vec.max(out=m8[:, :], in_=scr[:, :])
                    if r < TOPK // 8 - 1:
                        vec.match_replace(out=scr[:, :], in_to_replace=m8[:, :],
                                          in_values=scr[:, :], imm_value=-1e30)
                s.wait(ten, "v")
                lastm = ten.transpose(pb[5][0:1, 0:128], m8[:, 7:8], ident[:, :])
                s.inc(lastm, "t")
                s.wait(vec, "t")
                vec.tensor_copy(thrr[0:1, tsl], pb[5][0:1, 0:128])
                s.wait(ten, "v")
                s.wait(act, "v")

        msks = p.sbx(_stage, pref + "msks", [128, NA, 512], BF16)
        osb2 = p.sbx(_stage, pref + "osb2", [64, 512], BF16)
        ofm_sb = p.sbx(_stage, pref + "ofm_sb", [128, KT, 512], BF16)

        wslot_mm = [0, 0]

        def load_w(src, cols_off, slot, ch):
            s.wait(gps, "t", wslot_mm[slot])
            lw = gps.dma_start(
                out=wsb[slot][:, :, :],
                in_=src[:, cols_off:cols_off + 128].rearrange(
                    "(k p) c -> p k c", p=128))
            s.inc(lw, ch, 16)

        for hp in range(8):
            csl128 = slice(hp * 128, (hp + 1) * 128)
            s.wait(sync, "v")
            last = sync.dma_start(out=bqk[:, 0:1],
                                  in_=wqb[0:1, csl128].rearrange("o p -> p o"))
            s.inc(last, "xd", 16)
            last = sync.dma_start(out=bqk[:, 1:2],
                                  in_=wkb[0:1, csl128].rearrange("o p -> p o"))
            s.inc(last, "xd", 16)
            s.wait(ten, "xd")
            s.wait(ten, "v")
            lastm = ten.matmul(pb[5][:, 0:128], ones_f[:, :], wvb_sb[0:1, csl128],
                               start=True, stop=True, skip_group_check=True)
            s.inc(lastm, "t")
            s.wait(vec, "t")
            vec.tensor_copy(bcv[:, :], pb[5][:, 0:128])

            load_w(wkw, hp * 128, 0, "wl0")
            s.wait(ten, "wl0")
            s.wait(ten, "a")
            s.wait(ten, "v")
            for c in range(S // 512):
                sl = slice(c * 512, (c + 1) * 512)
                s.wait(ten, "v")
                for kt in range(KT):
                    lastm = ten.matmul(pb[4][:, :], wsb[0][:, kt, :], xn[:, kt, sl],
                                       start=(kt == 0), stop=(kt == KT - 1))
                s.inc(lastm, "t")
                s.wait(vec, "t")
                vec.tensor_scalar(kfm[:, sl], pb[4][:, :], bqk[:, 1:2], None,
                                  op0=OP.add)
            wslot_mm[0] = s.mark("t")
            load_w(wqw, hp * 128, 1, "wl1")
            s.wait(ten, "wl1")
            s.wait(ten, "v")
            for c in range(NQC):
                sl = slice(c * 512, (c + 1) * 512)
                s.wait(ten, "v")
                for kt in range(KT):
                    lastm = ten.matmul(pb[5][:, :], wsb[1][:, kt, :], xnq[:, kt, sl],
                                       start=(kt == 0), stop=(kt == KT - 1))
                s.inc(lastm, "t")
                s.wait(vec, "t")
                vec.tensor_scalar(qfm[:, sl], pb[5][:, :], bqk[:, 0:1], None,
                                  op0=OP.add)
            wslot_mm[1] = s.mark("t")
            load_w(wvw, hp * 128, 0, "wl0")
            s.wait(ten, "wl0")
            s.wait(ten, "v")
            for tt in range(NKT2):
                tsl = slice(tt * 128, (tt + 1) * 128)
                s.wait(ten, "v")
                for kt in range(KT):
                    lastm = ten.matmul(pb[4][:, 0:128], xn[:, kt, tsl],
                                       wsb[0][:, kt, :],
                                       start=(kt == 0), stop=(kt == KT - 1))
                s.inc(lastm, "t")
                s.wait(vec, "t")
                vec.tensor_add(t1[:, 0:128], pb[4][:, 0:128], bcv[:, :])
                vec.tensor_copy(vtm[:, tt, 0:64], t1[:, 0:64])
                vec.tensor_copy(vtm[:, tt, 65:129], t1[:, 64:128])
            wslot_mm[0] = s.mark("t")
            s.wait(ten, "v")

            for qc in range(NQC):
                qsl = slice(qc * 512, (qc + 1) * 512)
                s.wait(ten, "v")
                lastm = ten.matmul(pb[5][:, :], ones_f[:, :], thrr[0:1, qsl],
                                   start=True, stop=True, skip_group_check=True)
                s.inc(lastm, "t")
                s.wait(vec, "t")
                vec.tensor_copy(thrb[:, :], pb[5][:, :])
                for ka in range(NA):
                    s.wait(ten, "v")
                    lastm = ten.matmul(pb[5][:, :],
                                       kidx[:, ka * 128:(ka + 1) * 128],
                                       qidx[:, qsl], start=True, stop=True,
                                       skip_group_check=True)
                    s.inc(lastm, "t")
                    s.wait(vec, "t")
                    vec.tensor_tensor(msks[:, ka, :], pb[5][:, :], thrb[:, :],
                                      OP.is_ge)
                s.wait(ten, "v")
                for h2 in range(2):
                    hb = 64 * h2
                    opb = pb[2 + h2]
                    for kt2 in range(NKT2):
                        s.wait(ten, "a")
                        s.wait(ten, "v")
                        lastm = ten.matmul(
                            pb[0][:, :],
                            kfm[hb:hb + 64, kt2 * 128:(kt2 + 1) * 128],
                            qfm[hb:hb + 64, qsl],
                            start=True, stop=True, skip_group_check=True)
                        s.inc(lastm, "t")
                        s.wait(act, "t")
                        s.wait(act, "v")
                        act.activation(osb[:, :], pb[0][:, :], AF.Exp, scale=0.125)
                        s.wait(vec, "a")
                        if kt2 < NA:
                            vec.tensor_mul(ptile[:, :], osb[:, :], msks[:, kt2, :])
                        else:
                            vec.tensor_copy(ptile[:, :], osb[:, :])
                        s.wait(ten, "v")
                        lastm = ten.matmul(
                            opb[0:65, :],
                            vtm[:, kt2, 65 * h2:65 * h2 + 65],
                            ptile[:, :],
                            start=(kt2 == 0), stop=(kt2 == NKT2 - 1),
                            skip_group_check=True)
                        s.inc(lastm, "t")
                for h2 in range(2):
                    opb = pb[2 + h2]
                    s.wait(vec, "t")
                    vec.reciprocal(rd1[0:1, :], opb[64:65, :])
                    s.wait(ten, "v")
                    lastm = ten.matmul(pb[5][0:64, :], ones_f[:, 0:64],
                                       rd1[0:1, :], start=True, stop=True,
                                       skip_group_check=True)
                    s.inc(lastm, "t")
                    s.wait(vec, "t")
                    vec.tensor_copy(osb[0:64, :], opb[0:64, :])
                    h_global = hp * 2 + h2
                    orow = h_global * 64
                    s.wait(vec, "st")
                    vec.tensor_mul(osb2[0:64, :], osb[0:64, :], pb[5][0:64, :])
                    s.wait(sync, "v")
                    last = sync.dma_start(out=odram[orow:orow + 64, qsl],
                                          in_=osb2[0:64, :])
                    s.inc(last, "st", 16)
                    s.wait(ten, "v")

        s.wait(sync, "st")
        wslot2 = list(wslot_mm)
        for c in range(NQC):
            sl = slice(c * 512, (c + 1) * 512)
            s.wait(sync, "t", 0 if c == 0 else s.mark("t"))
            last = sync.dma_start(
                out=ofm_sb[:, :, :],
                in_=odram[:, sl].rearrange("(k p) q -> p k q", p=128))
            s.inc(last, "eb", 16)
            for ot in range(KT):
                slot = ot % 2
                ch = f"wl{slot}"
                s.wait(gps, "t", wslot2[slot])
                lw = gps.dma_start(
                    out=wsb[slot][:, :, :],
                    in_=wow[:, ot * 128:(ot + 1) * 128].rearrange(
                        "(k p) c -> p k c", p=128))
                s.inc(lw, ch, 16)
                s.wait(ten, ch)
                s.wait(ten, "eb")
                s.wait(ten, "v")
                for kt in range(KT):
                    lastm = ten.matmul(pb[4][:, :], wsb[slot][:, kt, :],
                                       ofm_sb[:, kt, :], start=(kt == 0),
                                       stop=(kt == KT - 1))
                s.inc(lastm, "t")
                wslot2[slot] = s.mark("t")
                s.wait(sync, "v")
                last = sync.dma_start(out=xr[:, :],
                                      in_=xq[ot * 128:(ot + 1) * 128, sl])
                s.inc(last, "xd", 16)
                s.wait(vec, "xd")
                s.wait(vec, "t")
                s.wait(vec, "st")
                vec.tensor_scalar(t1[:, :], pb[4][:, :], bsb[:, ot, 0:1], None,
                                  op0=OP.add)
                vec.tensor_add(t1[:, :], t1[:, :], xr[:, :])
                s.wait(sync, "v")
                last = sync.dma_start(out=y[ot * 128:(ot + 1) * 128, sl],
                                      in_=t1[:, :])
                s.inc(last, "st", 16)
        s.wait(sync, "st")
        s.wait(gps, "st")



def emit_a1(p, s, io, pref, S=4096, Q=2048):
    NKT2 = S // 128
    NQC = Q // 512
    nc = p.nc
    from contextlib import ExitStack as _ES0
    with _ES0() as _stage:
        xf = io["xf"]
        xq = io["xq"]
        wq = io["wq"]
        wk = io["wk"]
        wv = io["wv"]
        cm = io["cm"]
        y = io["y"]

        xn = p.sbx(_stage, pref + "xn", [128, KT, S], BF16)
        xnq = p.sbx(_stage, pref + "xnq", [128, KT, Q], BF16)
        xch = p.sbx(_stage, pref + "xch", [128, KT, 512], F32)
        xr = p.sbx(_stage, pref + "xr", [128, 512], F32)
        xbf = p.sbx(_stage, pref + "xbf", [128, 512], BF16)
        xsq = p.sbx(_stage, pref + "xsq", [128, 512], BF16)
        stats = p.sbx(_stage, pref + "stats", [1, 5, 512], F32)
        t1 = p.sbx(_stage, pref + "t1", [128, 512], F32)
        ones_bf = p.sbx(_stage, pref + "ones_bf", [128, 1], BF16)
        ones_f = p.sbx(_stage, pref + "ones_f", [1, 128], F32)
        wqk_sb = p.sbx(_stage, pref + "wqk_sb", [128, KT, 128], BF16)
        wv_sb = p.sbx(_stage, pref + "wv_sb", [128, KT, 256], BF16)
        kfm = p.sbx(_stage, pref + "kfm", [64, S], BF16)
        qfm = p.sbx(_stage, pref + "qfm", [64, Q], BF16)
        vtm = p.sbx(_stage, pref + "vtm", [128, NKT2, 256], BF16)
        cms = p.sbx(_stage, pref + "cms", [128, NKT2, 512], BF16)
        rdq = p.sbx(_stage, pref + "rdq", [1, NQC, 512], F32)
        osb = p.sbx(_stage, pref + "osb", [128, 512], BF16)

        pb = [p.psx(_stage, pref + f"pb{i}", [128, 512]) for i in range(6)]

        sync, gps, ten = p.sync, p.gpsd, p.ten
        vec, act = p.vec, p.act

        vec.memset(ones_bf[:, :], 1.0)
        vec.memset(ones_f[:, :], 1.0)

        emit_ln2(p, s, S, io["xfsrc"], xn, xch, xbf, xsq, stats, t1, ones_bf, ones_f,
                pb[0][0:1, :], pb[3][0:1, :], pb[1], pb[2])
        emit_ln2(p, s, Q, FlatSrc(xq), xnq, xch, xbf, xsq, stats, t1, ones_bf, ones_f,
                pb[0][0:1, :], pb[3][0:1, :], pb[1], pb[2])

        lw = gps.dma_start(out=wqk_sb[:, :, 0:64],
                           in_=wq.rearrange("(k p) c -> p k c", p=128))
        s.inc(lw, "wl0", 16)
        lw = gps.dma_start(out=wqk_sb[:, :, 64:128],
                           in_=wk.rearrange("(k p) c -> p k c", p=128))
        s.inc(lw, "wl0", 16)
        s.wait(ten, "wl0")
        s.wait(ten, "v")
        for c in range(NQC):
            sl = slice(c * 512, (c + 1) * 512)
            s.wait(ten, "v")
            for kt in range(KT):
                lastm = ten.matmul(pb[4][0:64, :], wqk_sb[:, kt, 0:64],
                                   xnq[:, kt, sl], start=(kt == 0),
                                   stop=(kt == KT - 1))
            s.inc(lastm, "t")
            s.wait(vec, "t")
            vec.tensor_copy(qfm[:, sl], pb[4][0:64, :])
        for c in range(S // 512):
            sl = slice(c * 512, (c + 1) * 512)
            s.wait(ten, "v")
            for kt in range(KT):
                lastm = ten.matmul(pb[4][0:64, :], wqk_sb[:, kt, 64:128],
                                   xn[:, kt, sl], start=(kt == 0),
                                   stop=(kt == KT - 1))
            s.inc(lastm, "t")
            s.wait(vec, "t")
            vec.tensor_copy(kfm[:, sl], pb[4][0:64, :])

        vload_mark = 0
        for vh in range(4):
            s.wait(gps, "t", vload_mark)
            lw = gps.dma_start(
                out=wv_sb[:, :, :],
                in_=wv[:, vh * 256:(vh + 1) * 256].rearrange(
                    "(k p) c -> p k c", p=128))
            s.inc(lw, "wl1", 16)
            s.wait(ten, "wl1")
            for tt in range(NKT2):
                tsl = slice(tt * 128, (tt + 1) * 128)
                s.wait(ten, "v")
                for kt in range(KT):
                    lastm = ten.matmul(pb[4][:, 0:256], xn[:, kt, tsl],
                                       wv_sb[:, kt, :], start=(kt == 0),
                                       stop=(kt == KT - 1))
                s.inc(lastm, "t")
                s.wait(vec, "t")
                vec.tensor_copy(vtm[:, tt, :], pb[4][:, 0:256])
            vload_mark = s.mark("t")
            for qc in range(NQC):
                qsl = slice(qc * 512, (qc + 1) * 512)
                s.wait(sync, "v")
                last = sync.dma_start(
                    out=cms[:, :, :],
                    in_=cm[:, qsl].rearrange("(kt p) q -> p kt q", p=128))
                s.inc(last, "eb", 16)
                s.wait(vec, "eb")
                for kt2 in range(NKT2):
                    s.wait(ten, "a")
                    lastm = ten.matmul(pb[0][:, :], kfm[:, kt2 * 128:(kt2 + 1) * 128],
                                       qfm[:, qsl], start=True, stop=True,
                                       skip_group_check=True)
                    s.inc(lastm, "t")
                    s.wait(act, "t")
                    s.wait(act, "v")
                    act.activation(osb[:, :], pb[0][:, :], AF.Exp, scale=0.125)
                    s.wait(vec, "a")
                    vec.tensor_mul(cms[:, kt2, :], osb[:, :], cms[:, kt2, :])
                    s.wait(ten, "v")
                    lastm = ten.matmul(pb[1][0:1, :], ones_bf[:, :],
                                       cms[:, kt2, :], start=(kt2 == 0),
                                       stop=(kt2 == NKT2 - 1),
                                       skip_group_check=True)
                    s.inc(lastm, "t")
                if vh == 0:
                    s.wait(vec, "t")
                    vec.reciprocal(rdq[0:1, qc, :], pb[1][0:1, :])
                for dvt in range(2):
                    s.wait(ten, "v")
                    for kt2 in range(NKT2):
                        lastm = ten.matmul(
                            pb[2][:, :], vtm[:, kt2, dvt * 128:(dvt + 1) * 128],
                            cms[:, kt2, :], start=(kt2 == 0),
                            stop=(kt2 == NKT2 - 1), skip_group_check=True)
                    s.inc(lastm, "t")
                    s.wait(ten, "v")
                    lastm = ten.matmul(pb[3][:, :], ones_f[:, :], rdq[0:1, qc, :],
                                       start=True, stop=True, skip_group_check=True)
                    s.inc(lastm, "t")
                    s.wait(vec, "t")
                    vec.tensor_copy(osb[:, :], pb[2][:, :])
                    ot = vh * 2 + dvt
                    s.wait(sync, "v")
                    last = sync.dma_start(out=xr[:, :],
                                          in_=xq[ot * 128:(ot + 1) * 128, qsl])
                    s.inc(last, "xd", 16)
                    s.wait(vec, "xd")
                    s.wait(vec, "st")
                    vec.tensor_mul(t1[:, :], osb[:, :], pb[3][:, :])
                    vec.tensor_add(t1[:, :], t1[:, :], xr[:, :])
                    s.wait(sync, "v")
                    last = sync.dma_start(out=y[ot * 128:(ot + 1) * 128, qsl],
                                          in_=t1[:, :])
                    s.inc(last, "st", 16)
                    s.wait(ten, "v")
        s.wait(sync, "st")
        s.wait(gps, "st")



def emit_b1(p, s, io, pref, S=4096, Q=4096):
    NKT2 = S // 128
    NQC = Q // 512
    nc = p.nc
    from contextlib import ExitStack as _ES0
    with _ES0() as _stage:
        xf = io["xf"]
        xq = io["xq"]
        wkT = io["wkT"]
        wqT = io["wqT"]
        wvT = io["wvT"]
        woT = io["woT"]
        bo = io["bo"]
        klng2 = io["klng2"]
        klnb2 = io["klnb2"]
        qlng2 = io["qlng2"]
        qlnb2 = io["qlnb2"]
        cm = io["cm"]
        blk2Tc = io["blk2Tc"]
        y = io["y"]

        xn = p.sbx(_stage, pref + "xn", [128, KT, S], BF16)
        qn_all = p.sbx(_stage, pref + "qn_all", [128, KT, Q], BF16)
        xr = p.sbx(_stage, pref + "xr", [128, 512], F32)
        xbf = p.sbx(_stage, pref + "xbf", [128, 512], BF16)
        xsq = p.sbx(_stage, pref + "xsq", [128, 512], BF16)
        stats = p.sbx(_stage, pref + "stats", [1, 5, 512], F32)
        hstats = p.sbx(_stage, pref + "hstats", [2, 5, 512], F32)
        t1 = p.sbx(_stage, pref + "t1", [128, 512], F32)
        t2 = p.sbx(_stage, pref + "t2", [128, 512], F32)
        ones_bf = p.sbx(_stage, pref + "ones_bf", [128, 1], BF16)
        ones_f = p.sbx(_stage, pref + "ones_f", [1, 128], F32)
        blk2 = p.sbx(_stage, pref + "blk2", [128, 2], BF16)
        blk2T = p.sbx(_stage, pref + "blk2T", [2, 128], BF16)
        gln = p.sbx(_stage, pref + "gln", [128, 4], F32)
        wsb = [p.sbx(_stage, pref + f"wsb{i}", [128, KT, 128], BF16) for i in range(2)]
        kfm = p.sbx(_stage, pref + "kfm", [128, S], BF16)
        vtm = p.sbx(_stage, pref + "vtm", [128, NKT2, 130], BF16)
        ptile = p.sbx(_stage, pref + "ptile", [128, 512], BF16)
        osb = p.sbx(_stage, pref + "osb", [128, 512], BF16)
        rd1 = p.sbx(_stage, pref + "rd1", [1, 512], F32)
        bsb = p.sbx(_stage, pref + "bsb", [128, KT, 1], F32)

        pb = [p.psx(_stage, pref + f"pb{i}", [128, 512]) for i in range(6)]

        sync, gps, ten = p.sync, p.gpsd, p.ten
        vec, act = p.vec, p.act

        vec.memset(ones_bf[:, :], 1.0)
        vec.memset(ones_f[:, :], 1.0)
        vec.memset(blk2[:, :], 0.0)
        vec.memset(blk2[0:64, 0:1], 1.0)
        vec.memset(blk2[64:128, 1:2], 1.0)
        lw = gps.dma_start(out=blk2T[:, :], in_=blk2Tc)
        s.inc(lw, "ld1", 16)
        s.wait(ten, "ld1")
        vec.memset(vtm[:, :, 64:65], 1.0)
        vec.memset(vtm[:, :, 129:130], 1.0)
        for i, t in enumerate([klng2, klnb2, qlng2, qlnb2]):
            last = sync.dma_start(out=gln[:, i:i + 1], in_=t)
            s.inc(last, "xd", 16)
        last = sync.dma_start(out=bsb[:, :, :],
                              in_=bo.rearrange("(k p) o -> p k o", p=128),
                              allow_slow_non_contiguous=True)
        s.inc(last, "xd", 16)

        wslot_mm = [0, 0]

        def load_w(src, cols_off, slot, ch):
            s.wait(gps, "t", wslot_mm[slot])
            lw = gps.dma_start(
                out=wsb[slot][:, :, :],
                in_=src[:, cols_off:cols_off + 128].rearrange(
                    "(k p) c -> p k c", p=128))
            s.inc(lw, ch, 16)

        def headln(dst, src_fm, ncols, gcol, bcol):
            for c in range(ncols // 512):
                sl = slice(c * 512, (c + 1) * 512)
                s.wait(vec, "t")
                vec.tensor_copy(xbf[:, :], src_fm[:, sl])
                vec.tensor_mul(xsq[:, :], xbf[:, :], xbf[:, :])
                s.wait(ten, "v")
                ten.matmul(pb[4][0:2, :], blk2[:, :], xbf[:, :],
                           start=True, stop=True, skip_group_check=True)
                lastm = ten.matmul(pb[5][0:2, :], blk2[:, :], xsq[:, :],
                                   start=True, stop=True, skip_group_check=True)
                s.inc(lastm, "t")
                s.wait(vec, "t")
                vec.tensor_scalar(hstats[0:2, 2, :], pb[4][0:2, :], 1.0 / 64, None,
                                  op0=OP.mult)
                vec.tensor_scalar(hstats[0:2, 1, :], pb[5][0:2, :], 1.0 / 64, None,
                                  op0=OP.mult)
                vec.tensor_mul(hstats[0:2, 3, :], hstats[0:2, 2, :], hstats[0:2, 2, :])
                vec.tensor_sub(hstats[0:2, 3, :], hstats[0:2, 1, :], hstats[0:2, 3, :])
                vec.tensor_scalar(hstats[0:2, 3, :], hstats[0:2, 3, :], EPS, None,
                                  op0=OP.add)
                s.wait(act, "v")
                act.activation(hstats[0:2, 4, :], hstats[0:2, 3, :], AF.Ln)
                act.activation(hstats[0:2, 4, :], hstats[0:2, 4, :], AF.Exp,
                               scale=-0.5)
                s.wait(vec, "a")
                vec.tensor_copy(xbf[0:2, 0:512], hstats[0:2, 2, :])
                vec.tensor_copy(xsq[0:2, 0:512], hstats[0:2, 4, :])
                s.wait(ten, "v")
                ten.matmul(pb[4][:, :], blk2T[:, :], xbf[0:2, 0:512],
                           start=True, stop=True, skip_group_check=True)
                lastm = ten.matmul(pb[5][:, :], blk2T[:, :], xsq[0:2, 0:512],
                                   start=True, stop=True, skip_group_check=True)
                s.inc(lastm, "t")
                s.wait(vec, "t")
                vec.tensor_sub(t2[:, :], src_fm[:, sl], pb[4][:, :])
                vec.tensor_mul(t2[:, :], t2[:, :], pb[5][:, :])
                vec.tensor_scalar(dst[:, sl], t2[:, :], gln[:, gcol:gcol + 1],
                                  gln[:, bcol:bcol + 1], op0=OP.mult, op1=OP.add)
                s.wait(ten, "v")

        from contextlib import ExitStack as _ES
        with _ES() as sub:
            xch = sub.enter_context(nc.sbuf_tensor(pref + "xch", [128, KT, 512], F32))
            xnq = xn
            emit_ln2(p, s, S, io["xfsrc"], xn, xch, xbf, xsq, stats, t1, ones_bf, ones_f,
                    pb[0][0:1, :], pb[3][0:1, :], pb[1], pb[2])
            for hp in range(8):
                load_w(wqT, hp * 128, 1, "wl1")
                s.wait(ten, "wl1")
                for c in range(NQC):
                    sl = slice(c * 512, (c + 1) * 512)
                    s.wait(ten, "v")
                    for kt in range(KT):
                        lastm = ten.matmul(pb[5][:, :], wsb[1][:, kt, :],
                                           xnq[:, kt, sl], start=(kt == 0),
                                           stop=(kt == KT - 1))
                    s.inc(lastm, "t")
                    s.wait(vec, "t")
                    vec.tensor_copy(qn_all[:, hp, sl], pb[5][:, :])
                wslot_mm[1] = s.mark("t")
                headln(qn_all[:, hp, :], qn_all[:, hp, :], Q, 2, 3)
        cms = p.sbx(_stage, pref + "cms", [128, 16, 512], BF16)
        odram = io["odram"]
        osb2 = p.sbx(_stage, pref + "osb2", [64, 512], BF16)
        ofm_sb = p.sbx(_stage, pref + "ofm_sb", [128, KT, 512], BF16)

        for hp in range(8):
            load_w(wkT, hp * 128, 0, "wl0")
            s.wait(ten, "wl0")
            s.wait(ten, "a")
            s.wait(ten, "v")
            for c in range(S // 512):
                sl = slice(c * 512, (c + 1) * 512)
                s.wait(ten, "v")
                for kt in range(KT):
                    lastm = ten.matmul(pb[4][:, :], wsb[0][:, kt, :], xn[:, kt, sl],
                                       start=(kt == 0), stop=(kt == KT - 1))
                s.inc(lastm, "t")
                s.wait(vec, "t")
                vec.tensor_copy(kfm[:, sl], pb[4][:, :])
            wslot_mm[0] = s.mark("t")
            headln(kfm, kfm, S, 0, 1)
            load_w(wvT, hp * 128, 0, "wl0")
            s.wait(ten, "wl0")
            s.wait(ten, "v")
            for tt in range(NKT2):
                tsl = slice(tt * 128, (tt + 1) * 128)
                s.wait(ten, "v")
                for kt in range(KT):
                    lastm = ten.matmul(pb[4][:, 0:128], xn[:, kt, tsl],
                                       wsb[0][:, kt, :],
                                       start=(kt == 0), stop=(kt == KT - 1))
                s.inc(lastm, "t")
                s.wait(vec, "t")
                vec.tensor_copy(vtm[:, tt, 0:64], pb[4][:, 0:64])
                vec.tensor_copy(vtm[:, tt, 65:129], pb[4][:, 64:128])
            wslot_mm[0] = s.mark("t")
            s.wait(ten, "v")

            for qc in range(NQC):
                qsl = slice(qc * 512, (qc + 1) * 512)
                for kh in range(2):
                    s.wait(sync, "v")
                    last = sync.dma_start(
                        out=cms[:, :, :],
                        in_=cm[kh * 2048:(kh + 1) * 2048, qsl].rearrange(
                            "(kt p) q -> p kt q", p=128))
                    s.inc(last, "eb", 16)
                    s.wait(vec, "eb")
                    for h2 in range(2):
                        hb = 64 * h2
                        opb = pb[2 + h2]
                        for k16 in range(16):
                            kt2 = kh * 16 + k16
                            s.wait(ten, "a")
                            s.wait(ten, "v")
                            lastm = ten.matmul(
                                pb[0][:, :],
                                kfm[hb:hb + 64, kt2 * 128:(kt2 + 1) * 128],
                                qn_all[hb:hb + 64, hp, qsl],
                                start=True, stop=True, skip_group_check=True)
                            s.inc(lastm, "t")
                            s.wait(act, "t")
                            s.wait(act, "v")
                            act.activation(osb[:, :], pb[0][:, :],
                                           AF.Exp, scale=1.0 / 32.0)
                            s.wait(vec, "a")
                            vec.tensor_mul(ptile[:, :], osb[:, :], cms[:, k16, :])
                            s.wait(ten, "v")
                            lastm = ten.matmul(
                                opb[0:65, :],
                                vtm[:, kt2, 65 * h2:65 * h2 + 65],
                                ptile[:, :],
                                start=(kt2 == 0), stop=(kt2 == NKT2 - 1),
                                skip_group_check=True)
                            s.inc(lastm, "t")
                for h2 in range(2):
                    opb = pb[2 + h2]
                    s.wait(vec, "t")
                    vec.reciprocal(rd1[0:1, :], opb[64:65, :])
                    s.wait(ten, "v")
                    lastm = ten.matmul(pb[5][0:64, :], ones_f[:, 0:64],
                                       rd1[0:1, :], start=True, stop=True,
                                       skip_group_check=True)
                    s.inc(lastm, "t")
                    s.wait(vec, "t")
                    vec.tensor_copy(osb[0:64, :], opb[0:64, :])
                    h_global = hp * 2 + h2
                    orow = h_global * 64
                    s.wait(vec, "st")
                    vec.tensor_mul(osb2[0:64, :], osb[0:64, :], pb[5][0:64, :])
                    s.wait(sync, "v")
                    last = sync.dma_start(out=odram[orow:orow + 64, qsl],
                                          in_=osb2[0:64, :])
                    s.inc(last, "st", 16)
                    s.wait(ten, "v")

        s.wait(sync, "st")
        wslot2 = list(wslot_mm)
        for c in range(NQC):
            sl = slice(c * 512, (c + 1) * 512)
            s.wait(sync, "t", 0 if c == 0 else s.mark("t"))
            last = sync.dma_start(
                out=ofm_sb[:, :, :],
                in_=odram[:, sl].rearrange("(k p) q -> p k q", p=128))
            s.inc(last, "eb", 16)
            for ot in range(KT):
                slot = ot % 2
                ch = f"wl{slot}"
                s.wait(gps, "t", wslot2[slot])
                lw = gps.dma_start(
                    out=wsb[slot][:, :, :],
                    in_=woT[:, ot * 128:(ot + 1) * 128].rearrange(
                        "(k p) c -> p k c", p=128))
                s.inc(lw, ch, 16)
                s.wait(ten, ch)
                s.wait(ten, "eb")
                s.wait(ten, "v")
                for kt in range(KT):
                    lastm = ten.matmul(pb[4][:, :], wsb[slot][:, kt, :],
                                       ofm_sb[:, kt, :], start=(kt == 0),
                                       stop=(kt == KT - 1))
                s.inc(lastm, "t")
                wslot2[slot] = s.mark("t")
                s.wait(sync, "v")
                last = sync.dma_start(out=xr[:, :],
                                      in_=xq[ot * 128:(ot + 1) * 128, sl])
                s.inc(last, "xd", 16)
                s.wait(vec, "xd")
                s.wait(vec, "t")
                s.wait(vec, "st")
                vec.tensor_scalar(t1[:, :], pb[4][:, :], bsb[:, ot, 0:1], None,
                                  op0=OP.add)
                vec.tensor_add(t1[:, :], t1[:, :], xr[:, :])
                s.wait(sync, "v")
                last = sync.dma_start(out=y[ot * 128:(ot + 1) * 128, sl],
                                      in_=t1[:, :])
                s.inc(last, "st", 16)
        s.wait(sync, "st")
        s.wait(gps, "st")



def emit_gla(p, s, io, pref, S=4096):
    NCH = S // 128
    nc = p.nc
    H, DK = 8, 64
    from contextlib import ExitStack as _ES0
    with _ES0() as _stage:
        xf = io["xf"]
        wq = io["wq"]
        wk = io["wk"]
        wv = io["wv"]
        wqa = io["wqa"]
        wka = io["wka"]
        wga = io["wga"]
        wqb = io["wqb"]
        wkb = io["wkb"]
        wgb = io["wgb"]
        wg1a = io["wg1a"]
        wg2a = io["wg2a"]
        wg1b = io["wg1b"]
        wg2b = io["wg2b"]
        bg1r = io["bg1r"]
        bg2r = io["bg2r"]
        we = io["we"]
        normwb = io["normwb"]
        wo = io["wo"]
        trilm = io["trilm"]
        identf = io["identf"]
        blk2Tc = io["blk2Tc"]
        y = io["y"]

        xn = p.sbx(_stage, pref + "xn", [128, KT, S], BF16)
        xch = p.sbx(_stage, pref + "xch", [128, KT, 512], F32)
        xr = p.sbx(_stage, pref + "xr", [128, 128], F32)
        xbf = p.sbx(_stage, pref + "xbf", [128, 512], BF16)
        xsq = p.sbx(_stage, pref + "xsq", [128, 512], BF16)
        stats = p.sbx(_stage, pref + "stats", [1, 5, 512], F32)
        t1 = p.sbx(_stage, pref + "t1", [128, 512], F32)
        t2 = p.sbx(_stage, pref + "t2", [128, 512], F32)
        tb = p.sbx(_stage, pref + "tb", [128, 512], BF16)
        ones_bf = p.sbx(_stage, pref + "ones_bf", [128, 1], BF16)
        ones_f = p.sbx(_stage, pref + "ones_f", [1, 128], F32)
        ident = p.sbx(_stage, pref + "ident", [128, 128], F32)
        identb = p.sbx(_stage, pref + "identb", [128, 128], BF16)
        tril = p.sbx(_stage, pref + "tril", [128, 128], BF16)
        nwb0 = p.sbx(_stage, pref + "nwb0", [128, 512], F32)
        dec = p.sbx(_stage, pref + "dec", [128, 4], F32)
        zer = p.sbx(_stage, pref + "zer", [128, 128], F32)
        blk2 = p.sbx(_stage, pref + "blk2", [128, 2], BF16)
        blk2T = p.sbx(_stage, pref + "blk2T", [2, 128], BF16)
        nwb = p.sbx(_stage, pref + "nwb", [1, 512], F32)
        wq_s = p.sbx(_stage, pref + "wq_s", [128, KT, 512], BF16)
        wk_s = p.sbx(_stage, pref + "wk_s", [128, KT, 512], BF16)
        wv_s = p.sbx(_stage, pref + "wv_s", [128, KT, 512], BF16)
        wqa_s = p.sbx(_stage, pref + "wqa_s", [128, KT, 64], BF16)
        wka_s = p.sbx(_stage, pref + "wka_s", [128, KT, 64], BF16)
        wga_s = p.sbx(_stage, pref + "wga_s", [128, KT, 64], BF16)
        wqb_s = p.sbx(_stage, pref + "wqb_s", [64, 512], BF16)
        wkb_s = p.sbx(_stage, pref + "wkb_s", [64, 512], BF16)
        wgb_s = p.sbx(_stage, pref + "wgb_s", [64, 512], BF16)
        wg1a_s = p.sbx(_stage, pref + "wg1a_s", [128, KT, 16], BF16)
        wg2a_s = p.sbx(_stage, pref + "wg2a_s", [128, KT, 16], BF16)
        wg1b_s = p.sbx(_stage, pref + "wg1b_s", [16, 512], BF16)
        wg2b_s = p.sbx(_stage, pref + "wg2b_s", [16, 512], BF16)
        bg1_s = p.sbx(_stage, pref + "bg1_s", [128, 4], F32)
        bg2_s = p.sbx(_stage, pref + "bg2_s", [128, 4], F32)
        we_s = p.sbx(_stage, pref + "we_s", [128, KT, 4], BF16)
        wo_s = p.sbx(_stage, pref + "wo_s", [128, 4, D], BF16)
        q1c = p.sbx(_stage, pref + "q1c", [128, 4, 128], BF16)
        k1c = p.sbx(_stage, pref + "k1c", [128, 4, 128], BF16)
        q2c = p.sbx(_stage, pref + "q2c", [128, 4, 128], BF16)
        k2c = p.sbx(_stage, pref + "k2c", [128, 4, 128], BF16)
        vc = p.sbx(_stage, pref + "vc", [128, 512], BF16)
        gk1c = p.sbx(_stage, pref + "gk1c", [128, 4, 128], F32)
        gk2c = p.sbx(_stage, pref + "gk2c", [128, 4, 128], F32)
        gke = p.sbx(_stage, pref + "gke", [128, 4, 128], F32)
        egv = p.sbx(_stage, pref + "egv", [128, 4, 128], F32)
        env = p.sbx(_stage, pref + "env", [128, 4, 128], F32)
        elv = p.sbx(_stage, pref + "elv", [128, 4, 128], F32)
        qt_ = p.sbx(_stage, pref + "qt_", [128, 4, 128], BF16)
        kg_ = p.sbx(_stage, pref + "kg_", [128, 4, 128], BF16)
        ko_ = p.sbx(_stage, pref + "ko_", [128, 4, 128], BF16)
        kot = p.sbx(_stage, pref + "kot", [128, 512], BF16)
        pa_ = p.sbx(_stage, pref + "pa_", [128, 8, 128], BF16)
        facb = p.sbx(_stage, pref + "facb", [128, 128], BF16)
        mkb = p.sbx(_stage, pref + "mkb", [128, 128], BF16)
        xl_ = p.sbx(_stage, pref + "xl_", [64, 128], BF16)
        glc = p.sbx(_stage, pref + "glc", [128, 512], BF16)
        rwt = p.sbx(_stage, pref + "rwt", [128, 8], F32)
        ofm = p.sbx(_stage, pref + "ofm", [128, 4, 128], BF16)
        S_ = [p.sbx(_stage, pref + f"S{i}", [128, 4, 64], F32) for i in range(5)]
        Sbf = p.sbx(_stage, pref + "Sbf", [128, 4, 64], BF16)

        pb = [p.psx(_stage, pref + f"pb{i}", [128, 512]) for i in range(6)]
        pbt = p.psx(_stage, pref + "pbt", [128, 512], BF16)

        sync, gps, ten = p.sync, p.gpsd, p.ten
        vec, act = p.vec, p.act

        vec.memset(ones_bf[:, :], 1.0)
        vec.memset(ones_f[:, :], 1.0)
        vec.memset(blk2[:, :], 0.0)
        vec.memset(blk2[0:64, 0:1], 1.0)
        vec.memset(blk2[64:128, 1:2], 1.0)
        for i in range(5):
            vec.memset(S_[i][:, :, :], 0.0)
        vec.memset(Sbf[:, :, :], 0.0)

        def ld(dst, src, rearr=None, ch="ld1"):
            inp = src if rearr is None else src.rearrange(rearr, p=128)
            lw = gps.dma_start(out=dst, in_=inp)
            s.inc(lw, ch, 16)

        ld(wq_s[:, :, :], wq, "(k p) c -> p k c")
        ld(wk_s[:, :, :], wk, "(k p) c -> p k c")
        ld(wv_s[:, :, :], wv, "(k p) c -> p k c")
        ld(wqa_s[:, :, :], wqa, "(k p) c -> p k c")
        ld(wka_s[:, :, :], wka, "(k p) c -> p k c")
        ld(wga_s[:, :, :], wga, "(k p) c -> p k c")
        ld(wqb_s[:, :], wqb)
        ld(wkb_s[:, :], wkb)
        ld(wgb_s[:, :], wgb)
        ld(wg1a_s[:, :, :], wg1a, "(k p) c -> p k c")
        ld(wg2a_s[:, :, :], wg2a, "(k p) c -> p k c")
        ld(wg1b_s[:, :], wg1b)
        ld(wg2b_s[:, :], wg2b)
        ld(we_s[:, :, :], we, "(k p) c -> p k c")
        ld(wo_s[:, :, :], wo, "(k p) c -> p k c")
        ld(tril[:, :], trilm)
        ld(blk2T[:, :], blk2Tc)
        lw = sync.dma_start(out=bg1_s[:, :], in_=bg1r.rearrange("(k p) o -> p k o", p=128),
                            allow_slow_non_contiguous=True)
        s.inc(lw, "xd", 16)
        lw = sync.dma_start(out=bg2_s[:, :], in_=bg2r.rearrange("(k p) o -> p k o", p=128),
                            allow_slow_non_contiguous=True)
        s.inc(lw, "xd", 16)
        lw = sync.dma_start(out=ident[:, :], in_=identf)
        s.inc(lw, "xd", 16)
        lw = sync.dma_start(out=nwb[:, :], in_=normwb)
        s.inc(lw, "xd", 16)

        emit_ln2(p, s, S, io["xfsrc"], xn, xch, xbf, xsq, stats, t1, ones_bf, ones_f,
                pb[0][0:1, :], pb[3][0:1, :], pb[1], pb[2])

        s.wait(ten, "ld1")
        s.wait(ten, "xd")
        s.wait(ten, "v")
        s.wait(vec, "xd")
        vec.tensor_copy(identb[:, :], ident[:, :])
        vec.memset(zer[:, :], 0.0)
        lastm = ten.matmul(pb[5][:, :], ones_f[:, :], nwb[0:1, :],
                           start=True, stop=True)
        s.inc(lastm, "t")
        s.wait(vec, "t")
        vec.tensor_copy(nwb0[:, :], pb[5][:, :])
        s.wait(ten, "v")

        def proj_fm(dst, w_s, csl, ncol=512, scale=None, bias=None, post=None):
            for rt in range(ncol // 128):
                s.wait(ten, "v")
                for kt in range(KT):
                    lastm = ten.matmul(pb[0][:, 0:128], w_s[:, kt, rt * 128:(rt + 1) * 128],
                                       xn[:, kt, csl], start=(kt == 0),
                                       stop=(kt == KT - 1))
                s.inc(lastm, "t")
                s.wait(vec, "t")
                if scale is not None:
                    vec.tensor_scalar(dst[:, rt, :], pb[0][:, 0:128], scale, None,
                                      op0=OP.mult)
                else:
                    vec.tensor_copy(dst[:, rt, :], pb[0][:, 0:128])

        def logsig16(dst, raw_psum4, bias_s):
            for rt in range(4):
                s.wait(vec, "t")
                s.wait(vec, "st")
                vec.tensor_scalar(t1[:, 0:128], raw_psum4[rt], bias_s[:, rt:rt + 1],
                                  None, op0=OP.add)
                s.wait(act, "v")
                act.activation(t2[:, 0:128], t1[:, 0:128], AF.Abs)
                act.activation(t2[:, 0:128], t2[:, 0:128], AF.Exp, scale=-1.0)
                s.wait(vec, "a")
                vec.tensor_scalar(t2[:, 0:128], t2[:, 0:128], 1.0, None, op0=OP.add)
                s.wait(act, "v")
                act.activation(t2[:, 0:128], t2[:, 0:128], AF.Ln)
                s.wait(vec, "a")
                vec.tensor_scalar_min(t1[:, 0:128], t1[:, 0:128], 0.0)
                vec.tensor_sub(t1[:, 0:128], t1[:, 0:128], t2[:, 0:128])
                vec.tensor_scalar(dst[:, rt, :], t1[:, 0:128], 1.0 / 16.0, None,
                                  op0=OP.mult)

        for cc in range(NCH):
            csl = slice(cc * 128, (cc + 1) * 128)
            proj_fm(q1c, wq_s, csl, scale=0.125)
            proj_fm(k1c, wk_s, csl)
            s.wait(ten, "v")
            for kt in range(KT):
                lastm = ten.matmul(pb[0][:, :], xn[:, kt, csl], wv_s[:, kt, :],
                                   start=(kt == 0), stop=(kt == KT - 1))
            s.inc(lastm, "t")
            s.wait(vec, "t")
            vec.tensor_copy(vc[:, :], pb[0][:, :])
            s.wait(ten, "v")
            for kt in range(KT):
                lastm = ten.matmul(pb[0][0:64, 0:128], wqa_s[:, kt, :], xn[:, kt, csl],
                                   start=(kt == 0), stop=(kt == KT - 1))
            s.inc(lastm, "t")
            s.wait(vec, "t")
            vec.tensor_copy(xl_[:, :], pb[0][0:64, 0:128])
            s.wait(ten, "v")
            for rt in range(4):
                lastm = ten.matmul(pb[0][:, 0:128] if rt == 0 else pb[0][:, 0:128],
                                   wqb_s[:, rt * 128:(rt + 1) * 128], xl_[:, :],
                                   start=True, stop=True, skip_group_check=True)
                s.inc(lastm, "t")
                s.wait(vec, "t")
                vec.tensor_scalar(q2c[:, rt, :], pb[0][:, 0:128], 0.125, None,
                                  op0=OP.mult)
                s.wait(ten, "v")
            s.wait(ten, "v")
            for kt in range(KT):
                lastm = ten.matmul(pb[0][0:64, 0:128], wka_s[:, kt, :], xn[:, kt, csl],
                                   start=(kt == 0), stop=(kt == KT - 1))
            s.inc(lastm, "t")
            s.wait(vec, "t")
            vec.tensor_copy(xl_[:, :], pb[0][0:64, 0:128])
            s.wait(ten, "v")
            for rt in range(4):
                lastm = ten.matmul(pb[0][:, 0:128], wkb_s[:, rt * 128:(rt + 1) * 128],
                                   xl_[:, :], start=True, stop=True,
                                   skip_group_check=True)
                s.inc(lastm, "t")
                s.wait(act, "t")
                s.wait(act, "v")
                s.wait(act, "st")
                act.activation(t1[:, 0:128], pb[0][:, 0:128], AF.Exp)
                s.wait(vec, "a")
                vec.tensor_copy(tb[:, 0:128], t1[:, 0:128])
                s.wait(ten, "v")
                lastm = ten.matmul(pb[5][0:2, 0:128], blk2[:, :], tb[:, 0:128],
                                   start=True, stop=True, skip_group_check=True)
                s.inc(lastm, "t")
                s.wait(vec, "t")
                vec.reciprocal(t2[0:2, 0:128], pb[5][0:2, 0:128])
                vec.tensor_copy(xsq[0:2, 0:128], t2[0:2, 0:128])
                s.wait(ten, "v")
                lastm = ten.matmul(pb[5][:, 0:128], blk2T[:, :], xsq[0:2, 0:128],
                                   start=True, stop=True, skip_group_check=True)
                s.inc(lastm, "t")
                s.wait(vec, "t")
                vec.tensor_mul(k2c[:, rt, :], t1[:, 0:128], pb[5][:, 0:128])
                s.wait(ten, "v")
                s.wait(act, "v")
            for (ga_s, gb_s, bgs, dst) in ((wg1a_s, wg1b_s, bg1_s, gk1c),
                                           (wg2a_s, wg2b_s, bg2_s, gk2c)):
                s.wait(ten, "v")
                s.wait(ten, "a")
                for kt in range(KT):
                    lastm = ten.matmul(pb[0][0:16, 0:128], ga_s[:, kt, :],
                                       xn[:, kt, csl], start=(kt == 0),
                                       stop=(kt == KT - 1))
                s.inc(lastm, "t")
                s.wait(vec, "t")
                vec.tensor_copy(xl_[0:16, :], pb[0][0:16, 0:128])
                s.wait(ten, "v")
                for rt in range(4):
                    lastm = ten.matmul(pb[1][:, rt * 128:(rt + 1) * 128],
                                       gb_s[:, rt * 128:(rt + 1) * 128],
                                       xl_[0:16, :], start=(rt == 0), stop=(rt == 3),
                                       skip_group_check=True)
                    s.inc(lastm, "t")
                    s.wait(ten, "t")
                s.wait(vec, "t")
                logsig16(dst, [pb[1][:, rt * 128:(rt + 1) * 128] for rt in range(4)],
                         bgs)
            s.wait(ten, "v")
            s.wait(ten, "a")
            for kt in range(KT):
                lastm = ten.matmul(pb[0][0:64, 0:128], wga_s[:, kt, :], xn[:, kt, csl],
                                   start=(kt == 0), stop=(kt == KT - 1))
            s.inc(lastm, "t")
            s.wait(vec, "t")
            vec.tensor_copy(xl_[:, :], pb[0][0:64, 0:128])
            s.wait(ten, "v")
            lastm = ten.matmul(pb[0][:, :], xl_[:, :], wgb_s[:, :],
                               start=True, stop=True, skip_group_check=True)
            s.inc(lastm, "t")
            s.wait(act, "t")
            s.wait(act, "v")
            act.activation(glc[:, :], pb[0][:, :], AF.Silu)
            s.wait(ten, "a")
            s.wait(ten, "v")
            for kt in range(KT):
                lastm = ten.matmul(pb[0][:, 0:4], xn[:, kt, csl], we_s[:, kt, :],
                                   start=(kt == 0), stop=(kt == KT - 1))
            s.inc(lastm, "t")
            s.wait(act, "t")
            act.activation(t1[:, 0:4], pb[0][:, 0:4], AF.Exp)
            s.wait(vec, "a")
            vec.reduce_sum(t1[:, 4:5], t1[:, 0:4], axis=AX.X)
            vec.reciprocal(t1[:, 5:6], t1[:, 4:5])
            vec.tensor_scalar(t1[:, 0:4], t1[:, 0:4], t1[:, 5:6], None, op0=OP.mult)
            vec.reduce_max(t1[:, 6:7], t1[:, 0:4], axis=AX.X)
            vec.tensor_scalar(rwt[:, 4:8], t1[:, 0:4], t1[:, 6:7], None, op0=OP.is_ge)
            vec.tensor_mul(rwt[:, 0:4], t1[:, 0:4], rwt[:, 4:8])

            for inst in range(5):
                if inst == 0:
                    qsrc, ksrc, gsrc = q1c, k1c, gk1c
                else:
                    e = inst - 1
                    s.wait(ten, "v")
                    lastm = ten.transpose(pb[5][0:1, 0:128], rwt[:, e:e + 1],
                                          ident[:, :])
                    s.inc(lastm, "t")
                    s.wait(vec, "t")
                    vec.tensor_copy(t2[0:1, 0:128], pb[5][0:1, 0:128])
                    s.wait(ten, "v")
                    lastm = ten.transpose(pb[5][0:1, 128:256], rwt[:, 4 + e:5 + e],
                                          ident[:, :])
                    s.inc(lastm, "t")
                    s.wait(vec, "t")
                    vec.tensor_copy(t2[0:1, 128:256], pb[5][0:1, 128:256])
                    s.wait(ten, "v")
                    lastm = ten.matmul(pb[5][:, 0:128], ones_f[:, :],
                                       t2[0:1, 0:128], start=True, stop=False,
                                       skip_group_check=True)
                    s.inc(lastm, "t")
                    s.wait(ten, "t")
                    lastm = ten.matmul(pb[5][:, 128:256], ones_f[:, :],
                                       t2[0:1, 128:256], start=False, stop=True,
                                       skip_group_check=True)
                    s.inc(lastm, "t")
                    s.wait(ten, "t")
                    s.wait(vec, "t")
                    vec.tensor_copy(facb[:, :], pb[5][:, 0:128])
                    vec.tensor_copy(mkb[:, :], pb[5][:, 128:256])
                    s.wait(ten, "v")
                    qsrc, ksrc, gsrc = q2c, k2c, None
                if inst == 0:
                    gin = gk1c
                else:
                    vec.tensor_mul(gke[:, :, :], gk2c[:, :, :],
                                   mkb[:, :].unsqueeze(1).to_broadcast([128, 4, 128]))
                    gin = gke
                for rt in range(4):
                    vec.tensor_tensor_scan(gke[:, rt, :] if inst == 0 else egv[:, rt, :],
                                           zer[:, :], gin[:, rt, :], 0.0,
                                           op0=OP.add, op1=OP.add)
                gcs = gke if inst == 0 else egv
                for rt in range(4):
                    s.wait(act, "v")
                    act.activation(env[:, rt, :], gcs[:, rt, :], AF.Exp, scale=-1.0)
                    act.activation(dec[:, rt:rt + 1], gcs[:, rt, 127:128], AF.Exp)
                    s.wait(vec, "a")
                    vec.tensor_scalar(elv[:, rt, :], gcs[:, rt, :],
                                      gcs[:, rt, 127:128], None, op0=OP.subtract)
                    s.wait(act, "v")
                    act.activation(elv[:, rt, :], elv[:, rt, :], AF.Exp, scale=-1.0)
                    act.activation(egv[:, rt, :] if inst == 0 else t1[:, 0:128],
                                   gcs[:, rt, :], AF.Exp)
                    if inst != 0:
                        s.wait(vec, "a")
                        vec.tensor_copy(gke[:, rt, :], t1[:, 0:128])
                ege = egv if inst == 0 else gke
                s.wait(vec, "t")
                s.wait(vec, "a")
                vec.tensor_copy(Sbf[:, :, :], S_[inst][:, :, :])
                s.wait(vec, "a")
                if inst == 0:
                    vec.tensor_mul(qt_[:, :, :], qsrc[:, :, :], ege[:, :, :])
                    vec.tensor_mul(kg_[:, :, :], ksrc[:, :, :], env[:, :, :])
                    vec.tensor_mul(ko_[:, :, :], ksrc[:, :, :], elv[:, :, :])
                else:
                    fb = facb[:, :].unsqueeze(1).to_broadcast([128, 4, 128])
                    vec.tensor_mul(qt_[:, :, :], qsrc[:, :, :], ege[:, :, :])
                    vec.tensor_mul(qt_[:, :, :], qt_[:, :, :], fb)
                    vec.tensor_mul(kg_[:, :, :], ksrc[:, :, :], env[:, :, :])
                    vec.tensor_mul(kg_[:, :, :], kg_[:, :, :], fb)
                    vec.tensor_mul(ko_[:, :, :], ksrc[:, :, :], elv[:, :, :])
                    vec.tensor_mul(ko_[:, :, :], ko_[:, :, :], fb)
                s.wait(ten, "v")
                for h in range(H):
                    rt, rr = h // 2, 64 * (h % 2)
                    bank = pb[1] if h < 4 else pb[2]
                    lastm = ten.matmul(bank[:, (h % 4) * 128:(h % 4 + 1) * 128],
                                       kg_[rr:rr + 64, rt, :], qt_[rr:rr + 64, rt, :],
                                       start=(h % 4 == 0), stop=(h % 4 == 3),
                                       skip_group_check=True)
                    s.inc(lastm, "t")
                    s.wait(ten, "t")
                s.wait(vec, "t")
                vec.tensor_mul(pa_[:, 0:4, :], pb[1][:, :].rearrange("p (h t) -> p h t", h=4),
                               tril[:, :].unsqueeze(1).to_broadcast([128, 4, 128]))
                vec.tensor_mul(pa_[:, 4:8, :], pb[2][:, :].rearrange("p (h t) -> p h t", h=4),
                               tril[:, :].unsqueeze(1).to_broadcast([128, 4, 128]))
                for rt in range(4):
                    s.wait(ten, "v")
                    lastm = ten.transpose(pbt[:, rt * 128:(rt + 1) * 128],
                                          ko_[:, rt, :], identb[:, :])
                    s.inc(lastm, "t")
                    s.wait(vec, "t")
                    vec.tensor_copy(kot[:, rt * 128:(rt + 1) * 128],
                                    pbt[:, rt * 128:(rt + 1) * 128])
                s.wait(ten, "v")
                for h in range(H):
                    lastm = ten.matmul(pb[3][:, h * 64:(h + 1) * 64],
                                       pa_[:, h, :], vc[:, h * 64:(h + 1) * 64],
                                       start=(inst == 0 and h == 0), stop=False,
                                       skip_group_check=True)
                    s.inc(lastm, "t")
                    s.wait(ten, "t")
                for h in range(H):
                    rt, rr = h // 2, 64 * (h % 2)
                    lastm = ten.matmul(pb[3][:, h * 64:(h + 1) * 64],
                                       qt_[rr:rr + 64, rt, :], Sbf[rr:rr + 64, rt, :],
                                       start=False,
                                       stop=(inst == 4 and h == H - 1),
                                       skip_group_check=True)
                    s.inc(lastm, "t")
                    s.wait(ten, "t")
                s.wait(ten, "v")
                for h in range(H):
                    rt, rr = h // 2, 64 * (h % 2)
                    lastm = ten.matmul(pb[4][rr:rr + 64, rt * 64:(rt + 1) * 64],
                                       kot[:, h * 64:(h + 1) * 64],
                                       vc[:, h * 64:(h + 1) * 64],
                                       start=(h < 2), stop=(h >= H - 2),
                                       skip_group_check=True)
                    s.inc(lastm, "t")
                    s.wait(ten, "t")
                s.wait(vec, "t")
                for rt in range(4):
                    vec.scalar_tensor_tensor(
                        S_[inst][:, rt, :], S_[inst][:, rt, :],
                        dec[:, rt:rt + 1], pb[4][:, rt * 64:(rt + 1) * 64],
                        op0=OP.mult, op1=OP.add)
                s.wait(ten, "v")

            s.wait(vec, "t")
            vec.tensor_copy(t1[:, :], pb[3][:, :])
            vec.tensor_mul(t2[:, :], t1[:, :], t1[:, :])
            vec.reduce_sum(t2[:, 0:8],
                           t2[:, :].rearrange("p (h d) -> p h d", h=8), axis=AX.X)
            vec.tensor_scalar(t2[:, 0:8], t2[:, 0:8], 1.0 / 64, None, op0=OP.mult)
            vec.tensor_scalar(t2[:, 0:8], t2[:, 0:8], EPS, None, op0=OP.add)
            s.wait(act, "v")
            act.activation(t2[:, 8:16], t2[:, 0:8], AF.Ln)
            act.activation(t2[:, 8:16], t2[:, 8:16], AF.Exp, scale=-0.5)
            s.wait(vec, "a")
            vec.tensor_mul(t1[:, :].rearrange("p (h d) -> p h d", h=8),
                           t1[:, :].rearrange("p (h d) -> p h d", h=8),
                           t2[:, 8:16].rearrange("p (h o) -> p h o", h=8)
                           .to_broadcast([128, 8, 64]))
            vec.tensor_mul(t1[:, :], t1[:, :], nwb0[:, :])
            vec.tensor_mul(tb[:, :], t1[:, :], glc[:, :])
            for rt in range(4):
                s.wait(ten, "v")
                lastm = ten.transpose(pbt[:, rt * 128:(rt + 1) * 128],
                                      tb[:, rt * 128:(rt + 1) * 128], identb[:, :])
                s.inc(lastm, "t")
                s.wait(vec, "t")
                vec.tensor_copy(ofm[:, rt, :], pbt[:, rt * 128:(rt + 1) * 128])
            for ot in range(KT):
                s.wait(ten, "v")
                for rt in range(4):
                    lastm = ten.matmul(pb[0][:, 0:128],
                                       wo_s[:, rt, ot * 128:(ot + 1) * 128],
                                       ofm[:, rt, :], start=(rt == 0),
                                       stop=(rt == 3), skip_group_check=True)
                s.inc(lastm, "t")
                s.wait(sync, "v")
                xeng, xch_ = (gps, "xg") if io["xfsrc"].bf16 else (sync, "xd")
                s.wait(xeng, "v")
                last = xeng.dma_start(out=xr[:, :],
                                      in_=io["xfsrc"].ap(slice(ot * 128, (ot + 1) * 128), csl))
                s.inc(last, xch_, 16)
                s.wait(vec, xch_)
                s.wait(vec, "t")
                s.wait(vec, "st")
                vec.tensor_add(t1[:, 0:128], pb[0][:, 0:128], xr[:, :])
                s.wait(sync, "v")
                last = sync.dma_start(out=y[ot * 128:(ot + 1) * 128, csl],
                                      in_=t1[:, 0:128])
                s.inc(last, "st", 16)
        s.wait(sync, "st")
        s.wait(gps, "st")


def tile_wT(wT, IT_out=None):
    ind, outd = wT.shape
    t = wT.reshape(ind // 128, 128, outd // 128, 128)
    return np.ascontiguousarray(t.transpose(2, 0, 1, 3))



def host_moe_inputs(inp, prefix="L0_ffn"):
    Wg_s = inp[f"{prefix}_Wg_s"]; Wu_s = inp[f"{prefix}_Wu_s"]; Wd_s = inp[f"{prefix}_Wd_s"]
    Wg = inp[f"{prefix}_Wg"]; Wu = inp[f"{prefix}_Wu"]; Wd = inp[f"{prefix}_Wd"]
    Wr = inp[f"{prefix}_Wr"]; rb = inp[f"{prefix}_rb"]
    wg = np.stack([tile_wT(np.ascontiguousarray(w.T))
                   for w in [Wg_s] + list(Wg)]).astype(np.float32)
    wu = np.stack([tile_wT(np.ascontiguousarray(w.T))
                   for w in [Wu_s] + list(Wu)]).astype(np.float32)
    wdr = np.stack([tile_wT(np.ascontiguousarray(w.T))
                    for w in [Wd_s] + list(Wd)]).astype(np.float32)
    return {
        "wg": wg, "wu": wu, "wdr": wdr,
        "wr": np.ascontiguousarray(Wr.T).astype(np.float32),
        "rbv": np.asarray(rb).reshape(1, 8).astype(np.float32),
        "identf": np.eye(128, dtype=np.float32),
    }



def host_attn_a_inputs(inp):
    PHI = (1.0 + np.sqrt(5.0)) / 2.0
    S = 4096
    pos = (np.arange(S, dtype=np.float64) * PHI) % 1.0
    dist = np.abs(pos[:, None] - pos[None, :])
    exb_full = (np.exp(-dist) + 1e-8).astype(np.float32)
    import ml_dtypes
    return {
        "wqkv": np.ascontiguousarray(inp["L0_attn_a_w_qkv"]).astype(np.float32),
        "wproj": np.ascontiguousarray(inp["L0_attn_a_w_proj"]).astype(np.float32),
        "bproj": np.asarray(inp["L0_attn_a_b_proj"]).reshape(D, 1).astype(np.float32),
    }, exb_full




def build_fused(nstages=8):
    pass_lite = (nstages == 80)
    p = P("fused")
    nc = p.nc
    with p.ctx:
        E, IT = 9, 16
        dt = nc.dram_tensor
        x_own = dt("x_own", [D, 2048], F32, kind="ExternalInput").ap()
        a0_wqkv = dt("a0_wqkv", [D, 3072], F32, kind="ExternalInput").ap()
        a0_wproj = dt("a0_wproj", [D, D], F32, kind="ExternalInput").ap()
        a0_bproj = dt("a0_bproj", [D, 1], F32, kind="ExternalInput").ap()
        a0_exb = dt("a0_exb", [4096, 2048], BF16, kind="ExternalInput").ap()
        m_wg = dt("m_wg", [E, IT, KT, 128, 128], F32, kind="ExternalInput").ap()
        m_wu = dt("m_wu", [E, IT, KT, 128, 128], F32, kind="ExternalInput").ap()
        m_wdr = dt("m_wdr", [E, KT, IT, 128, 128], F32, kind="ExternalInput").ap()
        m_wr = dt("m_wr", [D, 8], F32, kind="ExternalInput").ap()
        m_rbv = dt("m_rbv", [1, 8], F32, kind="ExternalInput").ap()
        identf = dt("identf", [128, 128], F32, kind="ExternalInput").ap()
        blk2Tc = dt("blk2Tc", [2, 128], F32, kind="ExternalInput").ap()
        trilm = dt("trilm", [128, 128], BF16, kind="ExternalInput").ap()
        b0_names = ["wqw", "wkw", "wvw", "wow"]
        b0w = {n: dt("b0_" + n, [D, D], F32, kind="ExternalInput").ap() for n in b0_names}
        b0_wqb = dt("b0_wqb", [1, D], F32, kind="ExternalInput").ap()
        b0_wkb = dt("b0_wkb", [1, D], F32, kind="ExternalInput").ap()
        b0_wvb = dt("b0_wvb", [1, D], F32, kind="ExternalInput").ap()
        b0_wob = dt("b0_wob", [D, 1], F32, kind="ExternalInput").ap()
        b0_wqiw = dt("b0_wqiw", [D, 64], F32, kind="ExternalInput").ap()
        b0_wkiw = dt("b0_wkiw", [D, 64], F32, kind="ExternalInput").ap()
        b0_wqib = dt("b0_wqib", [64, 1], F32, kind="ExternalInput").ap()
        b0_wkib = dt("b0_wkib", [64, 1], F32, kind="ExternalInput").ap()
        a1_wq = dt("a1_wq", [D, 64], F32, kind="ExternalInput").ap()
        a1_wk = dt("a1_wk", [D, 64], F32, kind="ExternalInput").ap()
        a1_wv = dt("a1_wv", [D, D], F32, kind="ExternalInput").ap()
        a1_cm = dt("a1_cm", [4096, 2048], BF16, kind="ExternalInput").ap()
        b1_wkT = dt("b1_wkT", [D, D], F32, kind="ExternalInput").ap()
        b1_wqT = dt("b1_wqT", [D, D], F32, kind="ExternalInput").ap()
        b1_wvT = dt("b1_wvT", [D, D], F32, kind="ExternalInput").ap()
        b1_woT = dt("b1_woT", [D, D], F32, kind="ExternalInput").ap()
        b1_bo = dt("b1_bo", [D, 1], F32, kind="ExternalInput").ap()
        b1_klng2 = dt("b1_klng2", [128, 1], F32, kind="ExternalInput").ap()
        b1_klnb2 = dt("b1_klnb2", [128, 1], F32, kind="ExternalInput").ap()
        b1_qlng2 = dt("b1_qlng2", [128, 1], F32, kind="ExternalInput").ap()
        b1_qlnb2 = dt("b1_qlnb2", [128, 1], F32, kind="ExternalInput").ap()
        b1_cm = dt("b1_cm", [4096, 4096], BF16, kind="ExternalInput").ap()
        g_wq = dt("g_wq", [D, 512], F32, kind="ExternalInput").ap()
        g_wk = dt("g_wk", [D, 512], F32, kind="ExternalInput").ap()
        g_wv = dt("g_wv", [D, 512], F32, kind="ExternalInput").ap()
        g_wqa = dt("g_wqa", [D, 64], F32, kind="ExternalInput").ap()
        g_wka = dt("g_wka", [D, 64], F32, kind="ExternalInput").ap()
        g_wga = dt("g_wga", [D, 64], F32, kind="ExternalInput").ap()
        g_wqb = dt("g_wqb", [64, 512], F32, kind="ExternalInput").ap()
        g_wkb = dt("g_wkb", [64, 512], F32, kind="ExternalInput").ap()
        g_wgb = dt("g_wgb", [64, 512], F32, kind="ExternalInput").ap()
        g_wg1a = dt("g_wg1a", [D, 16], F32, kind="ExternalInput").ap()
        g_wg2a = dt("g_wg2a", [D, 16], F32, kind="ExternalInput").ap()
        g_wg1b = dt("g_wg1b", [16, 512], F32, kind="ExternalInput").ap()
        g_wg2b = dt("g_wg2b", [16, 512], F32, kind="ExternalInput").ap()
        g_bg1r = dt("g_bg1r", [512, 1], F32, kind="ExternalInput").ap()
        g_bg2r = dt("g_bg2r", [512, 1], F32, kind="ExternalInput").ap()
        g_we = dt("g_we", [D, 4], F32, kind="ExternalInput").ap()
        g_normwb = dt("g_normwb", [1, 512], F32, kind="ExternalInput").ap()
        g_wo = dt("g_wo", [512, D], F32, kind="ExternalInput").ap()
        yout = dt("yout", [D, 4096], F32, kind="ExternalOutput").ap()
        xs = [dt(f"xs{i}", [D, 2048], F32).ap() for i in range(6)]
        y6 = dt("y6", [D, 4096], F32).ap()
        y7 = dt("y7", [D, 4096], F32).ap()
        b0_odram = dt("b0_odram", [D, 2048], BF16).ap()
        b1_odram = dt("b1_odram", [D, 4096], BF16).ap()
        agin = [dt(f"agin{i}", [D, 2048], BF16).ap() for i in range(4)]
        agout = [dt(f"agout{i}", [2 * D, 2048], BF16).ap() for i in range(4)]

        s = p.engines(["xd", "xg", "wl0", "wl1", "wd0", "wd1", "wr0", "eb",
                       "st", "ld1", "cc"])

        emit_ag(p, s, x_own, agin[0], agout[0])
        stage_barrier(p, s)
        io = {"xf": agout[0], "xq": x_own, "xfsrc": AgSrc(agout[0], 2048),
              "wqkv": a0_wqkv, "wproj": a0_wproj, "bproj": a0_bproj,
              "exb": a0_exb, "y": xs[1]}
        emit_a0(p, s, io, "a0_")
        stage_barrier(p, s)
        if nstages < 2:
            s.wait(p.sync, "st"); s.wait(p.gpsd, "st"); return nc
        io = {"x": xs[1], "wg": m_wg, "wu": m_wu, "wdr": m_wdr, "wr": m_wr,
              "rbv": m_rbv, "identf": identf, "y": xs[2]}
        emit_moe(p, s, io, "m1_")
        stage_barrier(p, s)
        if nstages < 3:
            s.wait(p.sync, "st"); s.wait(p.gpsd, "st"); return nc
        emit_ag(p, s, xs[2], agin[1], agout[1])
        stage_barrier(p, s)
        io = {"xf": agout[1], "xq": xs[2], "xfsrc": AgSrc(agout[1], 2048),
              "odram": b0_odram, "identf": identf, "y": xs[3],
              "wqb": b0_wqb, "wkb": b0_wkb, "wvb": b0_wvb, "wob": b0_wob,
              "wqiw": b0_wqiw, "wkiw": b0_wkiw, "wqib": b0_wqib,
              "wkib": b0_wkib}
    
        io.update(b0w)
        emit_b0(p, s, io, "b0_")
        stage_barrier(p, s)
        if nstages < 4:
            s.wait(p.sync, "st"); s.wait(p.gpsd, "st"); return nc
        io = {"x": xs[3], "wg": m_wg, "wu": m_wu, "wdr": m_wdr, "wr": m_wr,
              "rbv": m_rbv, "identf": identf, "y": xs[4]}
        emit_moe(p, s, io, "m2_")
        stage_barrier(p, s)
        if nstages < 5:
            s.wait(p.sync, "st"); s.wait(p.gpsd, "st"); return nc
        emit_ag(p, s, xs[4], agin[2], agout[2])
        stage_barrier(p, s)
        io = {"xf": agout[2], "xq": xs[4], "xfsrc": AgSrc(agout[2], 2048),
              "wq": a1_wq, "wk": a1_wk, "wv": a1_wv, "cm": a1_cm, "y": xs[5]}
        emit_a1(p, s, io, "a1_")
        stage_barrier(p, s)
        if nstages < 6:
            s.wait(p.sync, "st"); s.wait(p.gpsd, "st"); return nc
        emit_ag(p, s, xs[5], agin[3], agout[3])
        stage_barrier(p, s)
        io = {"xf": agout[3], "xfsrc": AgSrc(agout[3], 2048),
              "wq": g_wq, "wk": g_wk, "wv": g_wv, "wqa": g_wqa, "wka": g_wka,
              "wga": g_wga, "wqb": g_wqb, "wkb": g_wkb, "wgb": g_wgb,
              "wg1a": g_wg1a, "wg2a": g_wg2a, "wg1b": g_wg1b, "wg2b": g_wg2b,
              "bg1r": g_bg1r, "bg2r": g_bg2r, "we": g_we, "normwb": g_normwb,
              "wo": g_wo, "trilm": trilm, "identf": identf, "blk2Tc": blk2Tc,
              "y": y6}
    
        emit_gla(p, s, io, "g1_")
        stage_barrier(p, s)
        if nstages < 7:
            s.wait(p.sync, "st"); s.wait(p.gpsd, "st"); return nc
        io = {"xf": y6, "xq": y6, "xfsrc": FlatSrc(y6),
              "wkT": b1_wkT, "wqT": b1_wqT, "wvT": b1_wvT, "woT": b1_woT,
              "bo": b1_bo, "klng2": b1_klng2, "klnb2": b1_klnb2,
              "qlng2": b1_qlng2, "qlnb2": b1_qlnb2, "cm": b1_cm,
              "blk2Tc": blk2Tc, "odram": b1_odram, "y": y7}
        emit_b1(p, s, io, "b1_")
        stage_barrier(p, s)
        if nstages < 8 and nstages != 81:
            s.wait(p.sync, "st"); s.wait(p.gpsd, "st"); return nc
        if nstages == 80:
            lw = p.gpsd.dma_start(out=yout, in_=y7)
            s.inc(lw, "st", 16)
            s.wait(p.sync, "st"); s.wait(p.gpsd, "st"); return nc
        y8 = dt("y8", [D, 4096], F32).ap()
        io = {"xf": y7, "xfsrc": FlatSrc(y7),
              "wq": g_wq, "wk": g_wk, "wv": g_wv, "wqa": g_wqa, "wka": g_wka,
              "wga": g_wga, "wqb": g_wqb, "wkb": g_wkb, "wgb": g_wgb,
              "wg1a": g_wg1a, "wg2a": g_wg2a, "wg1b": g_wg1b, "wg2b": g_wg2b,
              "bg1r": g_bg1r, "bg2r": g_bg2r, "we": g_we, "normwb": g_normwb,
              "wo": g_wo, "trilm": trilm, "identf": identf, "blk2Tc": blk2Tc,
              "y": y8}
        emit_gla(p, s, io, "g2_")
        stage_barrier(p, s)
        lw = p.gpsd.dma_start(out=yout, in_=y8)
        s.inc(lw, "st", 16)
        s.wait(p.sync, "st")
        s.wait(p.gpsd, "st")
    return nc


import ml_dtypes

_FCACHE = {}


def _fprog():
    if "nc" not in _FCACHE:
        _FCACHE["nc"] = build_fused(nstages=8)
    return _FCACHE["nc"]


def kernel(**inp):
    inp = {k: np.asarray(v) for k, v in inp.items()}
    x0 = inp["L0_attn_a_x"].astype(np.float32)
    B, Sn, Dm = x0.shape
    Q = 2048
    wa0 = {
        "a0_wqkv": np.ascontiguousarray(inp["L0_attn_a_w_qkv"]).astype(np.float32),
        "a0_wproj": np.ascontiguousarray(inp["L0_attn_a_w_proj"]).astype(np.float32),
        "a0_bproj": inp["L0_attn_a_b_proj"].reshape(D, 1).astype(np.float32),
    }
    PHI = (1.0 + np.sqrt(5.0)) / 2.0
    pos = (np.arange(Sn, dtype=np.float64) * PHI) % 1.0
    dist = np.abs(pos[:, None] - pos[None, :])
    exbf = (np.exp(-dist) + 1e-8).astype(np.float32)
    wmoe = host_moe_inputs(inp, "L0_ffn")
    ident = np.eye(128, dtype=np.float32)
    blk2Tc = np.kron(np.eye(2), np.ones((1, 64))).astype(np.float32)
    trilc = (np.arange(128)[None, :] >= np.arange(128)[:, None]).astype(ml_dtypes.bfloat16)
    cm_full = (np.arange(Sn)[:, None] <= np.arange(Sn)[None, :])
    wshared = {
        "m_wg": wmoe["wg"], "m_wu": wmoe["wu"], "m_wdr": wmoe["wdr"],
        "m_wr": wmoe["wr"], "m_rbv": wmoe["rbv"],
        "identf": ident, "blk2Tc": blk2Tc, "trilm": trilc,
        "b0_wqw": np.ascontiguousarray(inp["L0_attn_b_wq_w"]).astype(np.float32),
        "b0_wkw": np.ascontiguousarray(inp["L0_attn_b_wk_w"]).astype(np.float32),
        "b0_wvw": np.ascontiguousarray(inp["L0_attn_b_wv_w"]).astype(np.float32),
        "b0_wow": np.ascontiguousarray(inp["L0_attn_b_wo_w"]).astype(np.float32),
        "b0_wqb": inp["L0_attn_b_wq_b"].reshape(1, D).astype(np.float32),
        "b0_wkb": inp["L0_attn_b_wk_b"].reshape(1, D).astype(np.float32),
        "b0_wvb": inp["L0_attn_b_wv_b"].reshape(1, D).astype(np.float32),
        "b0_wob": inp["L0_attn_b_wo_b"].reshape(D, 1).astype(np.float32),
        "b0_wqiw": np.ascontiguousarray(inp["L0_attn_b_wqi_w"]).astype(np.float32),
        "b0_wkiw": np.ascontiguousarray(inp["L0_attn_b_wki_w"]).astype(np.float32),
        "b0_wqib": inp["L0_attn_b_wqi_b"].reshape(64, 1).astype(np.float32),
        "b0_wkib": inp["L0_attn_b_wki_b"].reshape(64, 1).astype(np.float32),
        "a1_wq": np.ascontiguousarray(inp["L1_attn_a_Wq"]).astype(np.float32),
        "a1_wk": np.ascontiguousarray(inp["L1_attn_a_Wk"]).astype(np.float32),
        "a1_wv": np.ascontiguousarray(inp["L1_attn_a_Wv"]).astype(np.float32),
        "b1_wkT": np.ascontiguousarray(inp["L1_attn_b_Wk"].T).astype(np.float32),
        "b1_wqT": np.ascontiguousarray(inp["L1_attn_b_Wq"].T).astype(np.float32),
        "b1_wvT": np.ascontiguousarray(inp["L1_attn_b_Wv"].T).astype(np.float32),
        "b1_woT": np.ascontiguousarray(inp["L1_attn_b_Wo"].T).astype(np.float32),
        "b1_bo": inp["L1_attn_b_bo"].reshape(D, 1).astype(np.float32),
        "b1_klng2": np.tile(inp["L1_attn_b_kln_g"].astype(np.float32), 2).reshape(128, 1),
        "b1_klnb2": np.tile(inp["L1_attn_b_kln_b"].astype(np.float32), 2).reshape(128, 1),
        "b1_qlng2": np.tile(inp["L1_attn_b_qln_g"].astype(np.float32), 2).reshape(128, 1),
        "b1_qlnb2": np.tile(inp["L1_attn_b_qln_b"].astype(np.float32), 2).reshape(128, 1),
        "b1_cm": np.ascontiguousarray(cm_full).astype(ml_dtypes.bfloat16),
        "g_wq": np.ascontiguousarray(inp["L1_ffn_Wq"]).astype(np.float32),
        "g_wk": np.ascontiguousarray(inp["L1_ffn_Wk"]).astype(np.float32),
        "g_wv": np.ascontiguousarray(inp["L1_ffn_Wv"]).astype(np.float32),
        "g_wqa": np.ascontiguousarray(inp["L1_ffn_Wqa"]).astype(np.float32),
        "g_wka": np.ascontiguousarray(inp["L1_ffn_Wka"]).astype(np.float32),
        "g_wga": np.ascontiguousarray(inp["L1_ffn_Wga"]).astype(np.float32),
        "g_wqb": np.ascontiguousarray(inp["L1_ffn_Wqb"]).astype(np.float32),
        "g_wkb": np.ascontiguousarray(inp["L1_ffn_Wkb"]).astype(np.float32),
        "g_wgb": np.ascontiguousarray(inp["L1_ffn_Wgb"]).astype(np.float32),
        "g_wg1a": np.ascontiguousarray(inp["L1_ffn_Wg1a"]).astype(np.float32),
        "g_wg2a": np.ascontiguousarray(inp["L1_ffn_Wg2a"]).astype(np.float32),
        "g_wg1b": np.ascontiguousarray(inp["L1_ffn_Wg1b"]).astype(np.float32),
        "g_wg2b": np.ascontiguousarray(inp["L1_ffn_Wg2b"]).astype(np.float32),
        "g_bg1r": inp["L1_ffn_bg1"].reshape(512, 1).astype(np.float32),
        "g_bg2r": inp["L1_ffn_bg2"].reshape(512, 1).astype(np.float32),
        "g_we": np.ascontiguousarray(inp["L1_ffn_We"]).astype(np.float32),
        "g_normwb": np.tile(inp["L1_ffn_norm_w"].astype(np.float32), 8).reshape(1, 512),
        "g_wo": np.ascontiguousarray(inp["L1_ffn_Wo"]).astype(np.float32),
    }
    wshared.update(wa0)
    nc = _fprog()
    im = []
    for c in range(8):
        b, half = c // 2, c % 2
        own = slice(half * Q, (half + 1) * Q)
        m = {
            "x_own": np.ascontiguousarray(x0[b, own].T),
            "a0_exb": np.ascontiguousarray(exbf[:, own]).astype(ml_dtypes.bfloat16),
            "a1_cm": np.ascontiguousarray(cm_full[:, own]).astype(ml_dtypes.bfloat16),
        }
        m.update(wshared)
        im.append(m)
    res = run_bass_kernel_spmd(nc, im, core_ids=list(range(8)))
    out = np.empty((B, Sn, Dm), np.float32)
    for b in range(B):
        out[b] = res.results[2 * b]["yout"].T
    return out


# version 05
# speedup vs baseline: 1.2970x; 1.2970x over previous
import numpy as np
from contextlib import ExitStack

import concourse.bass as bass
import concourse.mybir as mybir
from concourse.bass_utils import run_bass_kernel_spmd

F32 = mybir.dt.float32
BF16 = mybir.dt.bfloat16
AF = mybir.ActivationFunctionType
OP = mybir.AluOpType
AX = mybir.AxisListType

D = 1024
KT = 8
EPS = 1e-5

_WRAP = {
    "tensor_copy", "tensor_tensor", "tensor_add", "tensor_sub", "tensor_mul",
    "tensor_max", "tensor_scalar", "tensor_single_scalar", "tensor_scalar_add",
    "tensor_scalar_sub", "tensor_scalar_mul", "tensor_scalar_max",
    "tensor_scalar_min", "tensor_relu", "tensor_reduce", "reduce_max",
    "reduce_sum", "scalar_tensor_tensor", "tensor_tensor_scan", "memset",
    "memzero", "activation", "sign", "square", "sqrt", "mul", "copy", "add",
    "max", "max_index", "match_replace", "select", "copy_predicated",
    "reciprocal", "bn_stats", "bn_aggr", "transpose", "pool", "pool_max",
    "pool_avg", "stream_shuffle", "iota", "affine_select",
}


class Sems:
    def __init__(self, nc, ctx, names):
        self.h = {n: ctx.enter_context(nc.semaphore(n)) for n in names}
        self.c = {n: 0 for n in names}

    def inc(self, inst, name, k=1):
        self.c[name] += k
        inst.then_inc(self.h[name], k)
        return self.c[name]

    def wait(self, eng, name, val=None):
        v = self.c[name] if val is None else val
        if v > 0:
            e = eng.e if isinstance(eng, Eng) else eng
            e.wait_ge(self.h[name], v)

    def mark(self, name):
        return self.c[name]


class Eng:

    def __init__(self, eng, s, ch):
        self.e = eng
        self.s = s
        self.ch = ch

    def __getattr__(self, k):
        f = getattr(self.e, k)
        if k in _WRAP:
            s, ch, e = self.s, self.ch, self.e

            def g(*a, **kw):
                n = s.c[ch]
                if n:
                    e.wait_ge(s.h[ch], n)
                inst = f(*a, **kw)
                s.c[ch] += 1
                inst.then_inc(s.h[ch], 1)
                return inst
            return g
        return f


class P:
    def __init__(self, name):
        self.nc = bass.Bass("TRN2", target_bir_lowering=False)
        self.ctx = ExitStack()
        self.name = name

    def sb(self, name, shape, dtype):
        return self.ctx.enter_context(self.nc.sbuf_tensor(name, shape, dtype))

    def ps(self, name, shape, dtype=F32):
        return self.ctx.enter_context(self.nc.psum_tensor(name, shape, dtype))

    def engines(self, sem_names):
        nc = self.nc
        s = Sems(nc, self.ctx, ["v", "a", "g", "t"] + sem_names)
        self.s = s
        self.blk = self.ctx.enter_context(nc.Block())
        self.sync = nc.sync
        self.gpsd = nc.gpsimd
        self.ten = nc.tensor
        self.vec = Eng(nc.vector, s, "v")
        self.act = Eng(nc.scalar, s, "a")
        self.gpc = Eng(nc.gpsimd, s, "g")
        return s


def _p_sbx(self, st, name, shape, dtype):
    return st.enter_context(self.nc.sbuf_tensor(name, shape, dtype))


def _p_psx(self, st, name, shape, dtype=F32):
    return st.enter_context(self.nc.psum_tensor(name, shape, dtype))


P.sbx = _p_sbx
P.psx = _p_psx


def emit_ln(p, s, T, x_hbm, xn, xch, xbf, xsq, stats, t1, ones_bf, ones_f,
            pb_s, pb_ss, pb_mb, pb_rb, xd="xd"):
    sync, ten, vec, act = p.sync, p.ten, p.vec, p.act
    for c in range(T // 512):
        sl = slice(c * 512, (c + 1) * 512)
        s.wait(sync, "v")
        last = sync.dma_start(out=xch[:, :, :],
                              in_=x_hbm[:, sl].rearrange("(k p) t -> p k t", p=128))
        s.inc(last, xd, 16)
        s.wait(vec, xd)
        s.wait(vec, "t")
        for kt in range(KT):
            vec.tensor_copy(xbf[:, :], xch[:, kt, :])
            vec.tensor_mul(xsq[:, :], xbf[:, :], xbf[:, :])
            s.wait(ten, "v")
            ten.matmul(pb_s[0:1, :], ones_bf[:, :], xbf[:, :],
                       start=(kt == 0), stop=(kt == KT - 1))
            last = ten.matmul(pb_ss[0:1, :], ones_bf[:, :], xsq[:, :],
                              start=(kt == 0), stop=(kt == KT - 1))
            s.inc(last, "t")
            s.wait(vec, "t")
        vec.tensor_scalar(stats[0:1, 2, :], pb_s[0:1, :], 1.0 / D, None, op0=OP.mult)
        vec.tensor_scalar(stats[0:1, 1, :], pb_ss[0:1, :], 1.0 / D, None, op0=OP.mult)
        vec.tensor_mul(stats[0:1, 3, :], stats[0:1, 2, :], stats[0:1, 2, :])
        vec.tensor_sub(stats[0:1, 3, :], stats[0:1, 1, :], stats[0:1, 3, :])
        vec.tensor_scalar(stats[0:1, 3, :], stats[0:1, 3, :], EPS, None, op0=OP.add)
        s.wait(act, "v")
        act.activation(stats[0:1, 4, :], stats[0:1, 3, :], AF.Ln)
        act.activation(stats[0:1, 4, :], stats[0:1, 4, :], AF.Exp, scale=-0.5)
        s.wait(ten, "a")
        s.wait(ten, "v")
        ten.matmul(pb_mb[:, :], ones_f[:, :], stats[0:1, 2, :], start=True, stop=True)
        last = ten.matmul(pb_rb[:, :], ones_f[:, :], stats[0:1, 4, :], start=True, stop=True)
        s.inc(last, "t")
        s.wait(vec, "t")
        for kt in range(KT):
            vec.tensor_sub(t1[:, :], xch[:, kt, :], pb_mb[:, :])
            vec.tensor_mul(xn[:, kt, sl], t1[:, :], pb_rb[:, :])


class FlatSrc:
    bf16 = False

    def __init__(self, t):
        self.t = t

    def ap(self, rows, cols):
        return self.t[rows, cols]


class AgSrc:
    bf16 = True

    def __init__(self, t, half):
        self.t = t
        self.half = half

    def ap(self, rows, cols):
        blk = cols.start // self.half
        off = blk * D
        return self.t[off + rows.start:off + rows.stop,
                      cols.start - blk * self.half:cols.stop - blk * self.half]


def emit_ln2(p, s, T, xsrc, xn, xch, xbf, xsq, stats, t1, ones_bf, ones_f,
             pb_s, pb_ss, pb_mb, pb_rb):
    nc = p.nc
    sync, gps, ten, vec, act = p.sync, p.gpsd, p.ten, p.vec, p.act
    ldeng, ldch = (gps, "xg") if xsrc.bf16 else (sync, "xd")
    for c in range(T // 512):
        sl = slice(c * 512, (c + 1) * 512)
        s.wait(ldeng, "v")
        last = ldeng.dma_start(
            out=xch[:, :, :],
            in_=xsrc.ap(slice(0, D), sl).rearrange("(k p) t -> p k t", p=128))
        s.inc(last, ldch, 16)
        s.wait(vec, ldch)
        s.wait(vec, "t")
        for kt in range(KT):
            vec.tensor_copy(xbf[:, :], xch[:, kt, :])
            vec.tensor_mul(xsq[:, :], xbf[:, :], xbf[:, :])
            s.wait(ten, "v")
            ten.matmul(pb_s[0:1, :], ones_bf[:, :], xbf[:, :],
                       start=(kt == 0), stop=(kt == KT - 1))
            last = ten.matmul(pb_ss[0:1, :], ones_bf[:, :], xsq[:, :],
                              start=(kt == 0), stop=(kt == KT - 1))
            s.inc(last, "t")
            s.wait(vec, "t")
        vec.tensor_scalar(stats[0:1, 2, :], pb_s[0:1, :], 1.0 / D, None, op0=OP.mult)
        vec.tensor_scalar(stats[0:1, 1, :], pb_ss[0:1, :], 1.0 / D, None, op0=OP.mult)
        vec.tensor_mul(stats[0:1, 3, :], stats[0:1, 2, :], stats[0:1, 2, :])
        vec.tensor_sub(stats[0:1, 3, :], stats[0:1, 1, :], stats[0:1, 3, :])
        vec.tensor_scalar(stats[0:1, 3, :], stats[0:1, 3, :], EPS, None, op0=OP.add)
        s.wait(act, "v")
        act.activation(stats[0:1, 4, :], stats[0:1, 3, :], AF.Ln)
        act.activation(stats[0:1, 4, :], stats[0:1, 4, :], AF.Exp, scale=-0.5)
        s.wait(ten, "a")
        s.wait(ten, "v")
        ten.matmul(pb_mb[:, :], ones_f[:, :], stats[0:1, 2, :], start=True, stop=True)
        last = ten.matmul(pb_rb[:, :], ones_f[:, :], stats[0:1, 4, :], start=True, stop=True)
        s.inc(last, "t")
        s.wait(vec, "t")
        for kt in range(KT):
            vec.tensor_sub(t1[:, :], xch[:, kt, :], pb_mb[:, :])
            vec.tensor_mul(xn[:, kt, sl], t1[:, :], pb_rb[:, :])


def stage_barrier(p, s):
    engs = [p.sync, p.gpsd, p.ten, p.vec, p.act]
    for e in engs:
        for ch in s.c:
            s.wait(e, ch)


def emit_ag(p, s, src_f32, agin, agout):
    gps = p.gpsd
    lw = gps.dma_start(out=agin, in_=src_f32)
    s.inc(lw, "xg", 16)
    s.wait(gps, "xg")
    cc = gps.collective_compute(
        "AllGather", OP.bypass,
        replica_groups=[[0, 1], [2, 3], [4, 5], [6, 7]],
        ins=[agin.opt()], outs=[agout.opt()])
    s.inc(cc, "cc", 1)
    s.wait(gps, "cc")


def emit_moe(p, s, io, pref, T=2048, TH=1024, E=9, I=2048):
    IT = I // 128
    NH = T // TH
    nc = p.nc
    from contextlib import ExitStack as _ES0
    with _ES0() as _stage:
        x = io["x"]
        wg = io["wg"]
        wu = io["wu"]
        wdr = io["wdr"]
        wr = io["wr"]
        rbv = io["rbv"]
        identf = io["identf"]
        y = io["y"]

        xn = p.sbx(_stage, pref + "xn", [128, KT, T], BF16)
        xch = p.sbx(_stage, pref + "xch", [128, KT, 512], F32)
        xr = p.sbx(_stage, pref + "xr", [128, 512], F32)
        xbf = p.sbx(_stage, pref + "xbf", [128, 512], BF16)
        xsq = p.sbx(_stage, pref + "xsq", [128, 512], BF16)
        stats = p.sbx(_stage, pref + "stats", [1, 5, 512], F32)
        t1 = p.sbx(_stage, pref + "t1", [128, 512], F32)
        ones_bf = p.sbx(_stage, pref + "ones_bf", [128, 1], BF16)
        ones_f = p.sbx(_stage, pref + "ones_f", [1, 128], F32)
        ones_bf1 = p.sbx(_stage, pref + "ones_bf1", [1, 128], BF16)
        ident = p.sbx(_stage, pref + "ident", [128, 128], F32)
        rbt = p.sbx(_stage, pref + "rbt", [128, 8], F32)
        etmp = p.sbx(_stage, pref + "etmp", [128, 8], F32)
        probs = p.sbx(_stage, pref + "probs", [128, 8], F32)
        m8 = p.sbx(_stage, pref + "m8", [128, 8], F32)
        cwt = p.sbx(_stage, pref + "cwt", [128, 8], F32)
        cwT = p.sbx(_stage, pref + "cwT", [1, 8, T], BF16)
        cwbc = p.sbx(_stage, pref + "cwbc", [128, 512], BF16)
        wrsb = p.sbx(_stage, pref + "wrsb", [128, KT, 8], BF16)
        xe = p.sbx(_stage, pref + "xe", [128, KT, TH], BF16)
        wgsb = [p.sbx(_stage, pref + f"wgsb{i}", [128, KT, 128], BF16) for i in range(2)]
        wusb = [p.sbx(_stage, pref + f"wusb{i}", [128, KT, 128], BF16) for i in range(2)]
        wdsb = [p.sbx(_stage, pref + f"wdsb{i}", [128, IT, 128], BF16) for i in range(2)]
        sg = p.sbx(_stage, pref + "sg", [128, 2, 512], BF16)
        gu = p.sbx(_stage, pref + "gu", [128, IT, TH], BF16)
        yac = p.sbx(_stage, pref + "yac", [128, KT, TH], F32)

        pb = [p.psx(_stage, pref + f"pb{i}", [128, 512]) for i in range(6)]

        sync, gps, ten = p.sync, p.gpsd, p.ten
        vec, act = p.vec, p.act

        vec.memset(ones_bf[:, :], 1.0)
        vec.memset(ones_f[:, :], 1.0)
        vec.memset(ones_bf1[:, :], 1.0)
        last = sync.dma_start(out=ident[:, :], in_=identf)
        s.inc(last, "xd", 16)
        last = sync.dma_start(out=rbt[:, :], in_=rbv.partition_broadcast(128))
        s.inc(last, "xd", 16)
        last = gps.dma_start(out=wrsb[:, :, :],
                             in_=wr.rearrange("(k p) e -> p k e", p=128))
        s.inc(last, "wr0", 16)

        emit_ln2(p, s, T, FlatSrc(x), xn, xch, xbf, xsq, stats, t1, ones_bf, ones_f,
                pb[0][0:1, :], pb[3][0:1, :], pb[1], pb[2])

        s.wait(ten, "wr0")
        s.wait(ten, "xd")
        s.wait(ten, "v")
        for tt in range(T // 128):
            tsl = slice(tt * 128, (tt + 1) * 128)
            for kt in range(KT):
                lastm = ten.matmul(pb[3][:, 0:8], xn[:, kt, tsl], wrsb[:, kt, :],
                                   start=(kt == 0), stop=(kt == KT - 1))
            s.inc(lastm, "t")
            s.wait(vec, "t")
            vec.tensor_add(etmp[:, :], pb[3][:, 0:8], rbt[:, :])
            s.wait(act, "v")
            act.activation(etmp[:, :], etmp[:, :], AF.Exp, scale=-1.0)
            s.wait(vec, "a")
            vec.tensor_scalar(etmp[:, :], etmp[:, :], 1.0, None, op0=OP.add)
            vec.reciprocal(probs[:, :], etmp[:, :])
            vec.max(out=m8[:, :], in_=probs[:, :])
            vec.tensor_scalar(cwt[:, :], probs[:, :], m8[:, 1:2], None, op0=OP.is_ge)
            vec.tensor_mul(cwt[:, :], probs[:, :], cwt[:, :])
            s.wait(ten, "v")
            for half8 in range(2):
                for e8 in range(4):
                    ee = half8 * 4 + e8
                    last = ten.transpose(pb[4][0:1, e8 * 128:(e8 + 1) * 128],
                                         cwt[:, ee:ee + 1], ident[:, :])
                s.inc(last, "t")
                s.wait(vec, "t")
                for e8 in range(4):
                    ee = half8 * 4 + e8
                    vec.tensor_copy(cwT[0:1, ee, tsl],
                                    pb[4][0:1, e8 * 128:(e8 + 1) * 128])
                s.wait(ten, "v")

        wslot_mm = [0, 0]
        wd_slot_mm = [0, 0]
        xe_last_mm = 0
        for h in range(NH):
            s.wait(vec, "t")
            vec.memset(yac[:, :, :], 0.0)
            for e in range(E):
                if e > 0:
                    s.wait(vec, "t", xe_last_mm)
                    for c in range(TH // 512):
                        csl = slice(h * TH + c * 512, h * TH + (c + 1) * 512)
                        osl = slice(c * 512, (c + 1) * 512)
                        s.wait(ten, "v")
                        lastm = ten.matmul(pb[5][:, :], ones_bf1[:, :],
                                           cwT[0:1, e - 1, csl], start=True, stop=True)
                        s.inc(lastm, "t")
                        s.wait(vec, "t")
                        vec.tensor_copy(cwbc[:, :], pb[5][:, :])
                        for kt in range(KT):
                            vec.tensor_mul(xe[:, kt, osl], xn[:, kt, csl], cwbc[:, :])
                    s.wait(ten, "v")
                silu_mark = [0, 0]
                gu_mark = [0, 0]
                jj = 0
                for ii in range(IT):
                    sl_i = ii % 2
                    ch = f"wl{sl_i}"
                    s.wait(gps, "t", wslot_mm[sl_i])
                    lw = gps.dma_start(out=wgsb[sl_i][:, :, :],
                                       in_=wg[e, ii].rearrange("k p c -> p k c"))
                    s.inc(lw, ch, 16)
                    lw = gps.dma_start(out=wusb[sl_i][:, :, :],
                                       in_=wu[e, ii].rearrange("k p c -> p k c"))
                    s.inc(lw, ch, 16)
                    s.wait(ten, ch)
                    for c in range(TH // 512):
                        osl = slice(c * 512, (c + 1) * 512)
                        csl = slice(h * TH + c * 512, h * TH + (c + 1) * 512)
                        par = jj % 2
                        hgb = pb[0] if par == 0 else pb[1]
                        hub = pb[2] if par == 0 else pb[3]
                        s.wait(ten, "v", gu_mark[par])
                        s.wait(ten, "a", silu_mark[par])
                        for kt in range(KT):
                            ten.matmul(hgb[:, :], wgsb[sl_i][:, kt, :],
                                       xn[:, kt, csl],
                                       start=(kt == 0), stop=(kt == KT - 1))
                        for kt in range(KT):
                            rv = xn[:, kt, csl] if e == 0 else xe[:, kt, osl]
                            lastm = ten.matmul(hub[:, :], wusb[sl_i][:, kt, :], rv,
                                               start=(kt == 0), stop=(kt == KT - 1))
                        s.inc(lastm, "t")
                        s.wait(act, "t")
                        s.wait(act, "v", gu_mark[par])
                        act.activation(sg[:, par, :], hgb[:, :], AF.Silu)
                        silu_mark[par] = s.mark("a")
                        s.wait(vec, "a", silu_mark[par])
                        vec.tensor_mul(gu[:, ii, osl], sg[:, par, :], hub[:, :])
                        gu_mark[par] = s.mark("v")
                        jj += 1
                    wslot_mm[sl_i] = s.mark("t")
                xe_last_mm = s.mark("t")
                yev_mark = [0, 0]
                jjb = 0
                for oo in range(KT):
                    sl_o = oo % 2
                    ch = f"wd{sl_o}"
                    s.wait(gps, "t", wd_slot_mm[sl_o])
                    lw = gps.dma_start(out=wdsb[sl_o][:, :, :],
                                       in_=wdr[e, oo].rearrange("i p c -> p i c"))
                    s.inc(lw, ch, 16)
                    s.wait(ten, ch)
                    if oo == 0:
                        s.wait(ten, "v")
                    for c in range(TH // 512):
                        osl = slice(c * 512, (c + 1) * 512)
                        par = jjb % 2
                        yb = pb[4] if par == 0 else pb[5]
                        s.wait(ten, "v", yev_mark[par])
                        for ii in range(IT):
                            lastm = ten.matmul(
                                yb[:, :], wdsb[sl_o][:, ii, :], gu[:, ii, osl],
                                start=(ii == 0), stop=(ii == IT - 1))
                        s.inc(lastm, "t")
                        s.wait(vec, "t")
                        vec.tensor_add(yac[:, oo, osl], yac[:, oo, osl], yb[:, :])
                        yev_mark[par] = s.mark("v")
                        jjb += 1
                    wd_slot_mm[sl_o] = s.mark("t")
            for kt in range(KT):
                for c in range(TH // 512):
                    osl = slice(c * 512, (c + 1) * 512)
                    csl = slice(h * TH + c * 512, h * TH + (c + 1) * 512)
                    s.wait(sync, "v")
                    last = sync.dma_start(out=xr[:, :],
                                          in_=x[kt * 128:(kt + 1) * 128, csl])
                    s.inc(last, "xd", 16)
                    s.wait(vec, "xd")
                    s.wait(vec, "st")
                    vec.tensor_add(t1[:, :], xr[:, :], yac[:, kt, osl])
                    s.wait(sync, "v")
                    last = sync.dma_start(out=y[kt * 128:(kt + 1) * 128, csl],
                                          in_=t1[:, :])
                    s.inc(last, "st", 16)
        s.wait(sync, "st")
        s.wait(gps, "st")




def emit_a0(p, s, io, pref, S=4096, Q=2048):
    H = 16
    NKT2 = S // 128
    NQC = Q // 512
    nc = p.nc
    from contextlib import ExitStack as _ES0
    with _ES0() as _stage:
        xf = io["xf"]
        wqkv = io["wqkv"]
        wproj = io["wproj"]
        bproj = io["bproj"]
        exb = io["exb"]
        y = io["y"]

        xq = io["xq"]
        xn = p.sbx(_stage, pref + "xn", [128, KT, S], BF16)
        xnq = p.sbx(_stage, pref + "xnq", [128, KT, Q], BF16)
        xr = p.sbx(_stage, pref + "xr", [128, 512], F32)
        xbf = p.sbx(_stage, pref + "xbf", [128, 512], BF16)
        xsq = p.sbx(_stage, pref + "xsq", [128, 512], BF16)
        stats = p.sbx(_stage, pref + "stats", [1, 5, 512], F32)
        t1 = p.sbx(_stage, pref + "t1", [128, 512], F32)
        ones_bf = p.sbx(_stage, pref + "ones_bf", [128, 1], BF16)
        ones_f = p.sbx(_stage, pref + "ones_f", [1, 128], F32)
        wsb = [p.sbx(_stage, pref + f"wsb{i}", [128, KT, 128], BF16) for i in range(2)]
        kfm = p.sbx(_stage, pref + "kfm", [128, S], BF16)
        qfm = p.sbx(_stage, pref + "qfm", [128, Q], BF16)
        vtm = p.sbx(_stage, pref + "vtm", [128, NKT2, 130], BF16)
        ptile = [p.sbx(_stage, pref + f"ptile{i}", [128, 512], BF16) for i in range(2)]

        pb = [p.psx(_stage, pref + f"pb{i}", [128, 512]) for i in range(6)]

        sync, gps, ten = p.sync, p.gpsd, p.ten
        vec, act = p.vec, p.act

        vec.memset(ones_bf[:, :], 1.0)
        vec.memset(ones_f[:, :], 1.0)
        vec.memset(vtm[:, :, 64:65], 1.0)
        vec.memset(vtm[:, :, 129:130], 1.0)

        from contextlib import ExitStack as _ESL
        with _ESL() as _lnctx:
            xch = p.sbx(_lnctx, pref + "xch", [128, KT, 512], F32)
            emit_ln2(p, s, S, io["xfsrc"], xn, xch, xbf, xsq, stats, t1, ones_bf, ones_f,
                     pb[0][0:1, :], pb[3][0:1, :], pb[1], pb[2])
            emit_ln2(p, s, Q, FlatSrc(xq), xnq, xch, xbf, xsq, stats, t1, ones_bf, ones_f,
                     pb[0][0:1, :], pb[3][0:1, :], pb[1], pb[2])
        exbs = p.sbx(_stage, pref + "exbs", [128, NKT2, 512], BF16)
        ofm = p.sbx(_stage, pref + "ofm", [128, KT, Q], BF16)
        osb = p.sbx(_stage, pref + "osb", [128, 512], BF16)
        rd1 = p.sbx(_stage, pref + "rd1", [1, 512], F32)

        wslot_mm = [0, 0]

        def load_w(cols_off, slot, ch):
            s.wait(gps, "t", wslot_mm[slot])
            lw = gps.dma_start(
                out=wsb[slot][:, :, :],
                in_=wqkv[:, cols_off:cols_off + 128].rearrange(
                    "(k p) c -> p k c", p=128))
            s.inc(lw, ch, 16)

        s.wait(ten, "v")
        for hp in range(8):
            load_w(1024 + hp * 128, 0, "wl0")
            s.wait(ten, "wl0")
            s.wait(ten, "a")
            s.wait(ten, "v")
            for c in range(S // 512):
                sl = slice(c * 512, (c + 1) * 512)
                s.wait(ten, "v")
                for kt in range(KT):
                    lastm = ten.matmul(pb[4][:, :], wsb[0][:, kt, :], xn[:, kt, sl],
                                       start=(kt == 0), stop=(kt == KT - 1))
                s.inc(lastm, "t")
                s.wait(vec, "t")
                vec.tensor_copy(kfm[:, sl], pb[4][:, :])
            wslot_mm[0] = s.mark("t")
            load_w(hp * 128, 1, "wl1")
            s.wait(ten, "wl1")
            s.wait(ten, "v")
            for c in range(NQC):
                sl = slice(c * 512, (c + 1) * 512)
                s.wait(ten, "v")
                for kt in range(KT):
                    lastm = ten.matmul(pb[5][:, :], wsb[1][:, kt, :], xnq[:, kt, sl],
                                       start=(kt == 0), stop=(kt == KT - 1))
                s.inc(lastm, "t")
                s.wait(vec, "t")
                vec.tensor_copy(qfm[:, sl], pb[5][:, :])
            wslot_mm[1] = s.mark("t")
            load_w(2048 + hp * 128, 0, "wl0")
            s.wait(ten, "wl0")
            s.wait(ten, "v")
            for tt in range(NKT2):
                tsl = slice(tt * 128, (tt + 1) * 128)
                s.wait(ten, "v")
                for kt in range(KT):
                    lastm = ten.matmul(pb[4][:, 0:128], xn[:, kt, tsl],
                                       wsb[0][:, kt, :],
                                       start=(kt == 0), stop=(kt == KT - 1))
                s.inc(lastm, "t")
                s.wait(vec, "t")
                vec.tensor_copy(vtm[:, tt, 0:64], pb[4][:, 0:64])
                vec.tensor_copy(vtm[:, tt, 65:129], pb[4][:, 64:128])
            wslot_mm[0] = s.mark("t")
            s.wait(ten, "v")

            for qc in range(NQC):
                qsl = slice(qc * 512, (qc + 1) * 512)
                s.wait(sync, "v")
                last = sync.dma_start(
                    out=exbs[:, :, :],
                    in_=exb[:, qsl].rearrange("(kt p) q -> p kt q", p=128))
                s.inc(last, "eb", 16)
                s.wait(vec, "eb")
                for h2 in range(2):
                    hb = 64 * h2
                    opb = pb[2]
                    pv_done_v = [0, 0]
                    for kt2 in range(NKT2 + 1):
                        if kt2 < NKT2:
                            sb_ = kt2 % 2
                            s.wait(ten, "a", s.mark("a") if kt2 >= 2 else 0)
                            lastm = ten.matmul(
                                pb[sb_][:, :],
                                kfm[hb:hb + 64, kt2 * 128:(kt2 + 1) * 128],
                                qfm[hb:hb + 64, qsl],
                                start=True, stop=True, skip_group_check=True)
                            s.inc(lastm, "t")
                            s.wait(act, "t")
                            s.wait(act, "v", pv_done_v[sb_])
                            act.activation(ptile[sb_][:, :], pb[sb_][:, :],
                                           AF.Exp, scale=0.125)
                            s.wait(vec, "a")
                            vec.tensor_mul(ptile[sb_][:, :], ptile[sb_][:, :],
                                           exbs[:, kt2, :])
                            pv_done_v[sb_] = s.mark("v")
                        if kt2 > 0:
                            kk = kt2 - 1
                            sb_ = kk % 2
                            s.wait(ten, "v", pv_done_v[sb_])
                            lastm = ten.matmul(
                                opb[0:65, :],
                                vtm[:, kk, 65 * h2:65 * h2 + 65],
                                ptile[sb_][:, :],
                                start=(kk == 0), stop=(kk == NKT2 - 1),
                                skip_group_check=True)
                            s.inc(lastm, "t")
                    s.wait(vec, "t")
                    vec.reciprocal(rd1[0:1, :], opb[64:65, :])
                    s.wait(ten, "v")
                    lastm = ten.matmul(pb[3][0:64, :], ones_f[:, 0:64],
                                       rd1[0:1, :], start=True, stop=True)
                    s.inc(lastm, "t")
                    s.wait(vec, "t")
                    vec.tensor_copy(osb[0:64, :], opb[0:64, :])
                    h_global = hp * 2 + h2
                    otile = h_global // 2
                    orow = 64 * (h_global % 2)
                    vec.tensor_mul(ofm[orow:orow + 64, otile, qsl],
                                   osb[0:64, :], pb[3][0:64, :])
                    s.wait(ten, "v")

        bsb = p.sbx(_stage, pref + "bsb", [128, KT, 1], F32)
        last = sync.dma_start(out=bsb[:, :, :],
                              in_=bproj.rearrange("(k p) o -> p k o", p=128),
                              allow_slow_non_contiguous=True)
        s.inc(last, "xd", 16)
        wslot2 = list(wslot_mm)
        for ot in range(KT):
            slot = ot % 2
            ch = f"wl{slot}"
            s.wait(gps, "t", wslot2[slot])
            lw = gps.dma_start(
                out=wsb[slot][:, :, :],
                in_=wproj[:, ot * 128:(ot + 1) * 128].rearrange(
                    "(k p) c -> p k c", p=128))
            s.inc(lw, ch, 16)
            s.wait(ten, ch)
            for c in range(NQC):
                sl = slice(c * 512, (c + 1) * 512)
                s.wait(ten, "v")
                for kt in range(KT):
                    lastm = ten.matmul(pb[4][:, :], wsb[slot][:, kt, :],
                                       ofm[:, kt, sl], start=(kt == 0),
                                       stop=(kt == KT - 1))
                s.inc(lastm, "t")
                s.wait(sync, "v")
                last = sync.dma_start(out=xr[:, :],
                                      in_=xq[ot * 128:(ot + 1) * 128, sl])
                s.inc(last, "xd", 16)
                s.wait(vec, "xd")
                s.wait(vec, "t")
                s.wait(vec, "st")
                vec.tensor_scalar(t1[:, :], pb[4][:, :], bsb[:, ot, 0:1], None,
                                  op0=OP.add)
                vec.tensor_add(t1[:, :], t1[:, :], xr[:, :])
                s.wait(sync, "v")
                last = sync.dma_start(out=y[ot * 128:(ot + 1) * 128, sl],
                                      in_=t1[:, :])
                s.inc(last, "st", 16)
            wslot2[slot] = s.mark("t")
        s.wait(sync, "st")
        s.wait(gps, "st")



def emit_b0(p, s, io, pref, S=4096, Q=2048, A=3584, W=512, TOPK=64):
    NKT2 = S // 128
    NA = A // 128
    NQC = Q // 512
    nc = p.nc
    from contextlib import ExitStack as _ES0
    with _ES0() as _stage:
        xf = io["xf"]
        xq = io["xq"]
        wqw = io["wqw"]
        wkw = io["wkw"]
        wvw = io["wvw"]
        wow = io["wow"]
        wqb = io["wqb"]
        wkb = io["wkb"]
        wvb = io["wvb"]
        wob = io["wob"]
        wqiw = io["wqiw"]
        wkiw = io["wkiw"]
        wqib = io["wqib"]
        wkib = io["wkib"]
        identf = io["identf"]
        y = io["y"]
        odram = io["odram"]

        xn = p.sbx(_stage, pref + "xn", [128, KT, S], BF16)
        xnq = p.sbx(_stage, pref + "xnq", [128, KT, Q], BF16)
        xr = p.sbx(_stage, pref + "xr", [128, 512], F32)
        xbf = p.sbx(_stage, pref + "xbf", [128, 512], BF16)
        xsq = p.sbx(_stage, pref + "xsq", [128, 512], BF16)
        stats = p.sbx(_stage, pref + "stats", [1, 5, 512], F32)
        t1 = p.sbx(_stage, pref + "t1", [128, 512], F32)
        ones_bf = p.sbx(_stage, pref + "ones_bf", [128, 1], BF16)
        ones_f = p.sbx(_stage, pref + "ones_f", [1, 128], F32)
        ident = p.sbx(_stage, pref + "ident", [128, 128], F32)
        wsb = [p.sbx(_stage, pref + f"wsb{i}", [128, KT, 128], BF16) for i in range(2)]
        wisb = p.sbx(_stage, pref + "wisb", [128, KT, 128], BF16)
        ibia = p.sbx(_stage, pref + "ibia", [128, 2], F32)
        qidx = p.sbx(_stage, pref + "qidx", [64, Q], BF16)
        kidx = p.sbx(_stage, pref + "kidx", [64, A], BF16)
        thrr = p.sbx(_stage, pref + "thrr", [1, Q], F32)
        thrb = p.sbx(_stage, pref + "thrb", [128, 512], BF16)
        m8 = p.sbx(_stage, pref + "m8", [128, 8], F32)
        bcv = p.sbx(_stage, pref + "bcv", [128, 128], F32)
        kfm = p.sbx(_stage, pref + "kfm", [128, S], BF16)
        qfm = p.sbx(_stage, pref + "qfm", [128, Q], BF16)
        vtm = p.sbx(_stage, pref + "vtm", [128, NKT2, 130], BF16)
        ptile = p.sbx(_stage, pref + "ptile", [128, 2, 512], BF16)
        osb = p.sbx(_stage, pref + "osb", [128, 512], BF16)
        rd1 = p.sbx(_stage, pref + "rd1", [1, 512], F32)
        bsb = p.sbx(_stage, pref + "bsb", [128, KT, 1], F32)
        bqk = p.sbx(_stage, pref + "bqk", [128, 2], F32)
        wvb_sb = p.sbx(_stage, pref + "wvb_sb", [1, D], F32)

        pb = [p.psx(_stage, pref + f"pb{i}", [128, 512]) for i in range(6)]

        sync, gps, ten = p.sync, p.gpsd, p.ten
        vec, act = p.vec, p.act

        vec.memset(ones_bf[:, :], 1.0)
        vec.memset(ones_f[:, :], 1.0)
        vec.memset(vtm[:, :, 64:65], 1.0)
        vec.memset(vtm[:, :, 129:130], 1.0)
        last = sync.dma_start(out=ident[:, :], in_=identf)
        s.inc(last, "xd", 16)
        last = sync.dma_start(out=bsb[:, :, :],
                              in_=wob.rearrange("(k p) o -> p k o", p=128),
                              allow_slow_non_contiguous=True)
        s.inc(last, "xd", 16)
        last = sync.dma_start(out=ibia[0:64, 0:1], in_=wqib)
        s.inc(last, "xd", 16)
        last = sync.dma_start(out=ibia[0:64, 1:2], in_=wkib)
        s.inc(last, "xd", 16)
        last = sync.dma_start(out=wvb_sb[:, :], in_=wvb)
        s.inc(last, "xd", 16)
        lw = gps.dma_start(out=wisb[:, :, 0:64],
                           in_=wqiw.rearrange("(k p) c -> p k c", p=128))
        s.inc(lw, "ld1", 16)
        lw = gps.dma_start(out=wisb[:, :, 64:128],
                           in_=wkiw.rearrange("(k p) c -> p k c", p=128))
        s.inc(lw, "ld1", 16)

        from contextlib import ExitStack as _ES
        with _ES() as sub:
            xch = sub.enter_context(nc.sbuf_tensor(pref + "xch", [128, KT, 512], F32))
            emit_ln2(p, s, S, io["xfsrc"], xn, xch, xbf, xsq, stats, t1, ones_bf, ones_f,
                    pb[0][0:1, :], pb[3][0:1, :], pb[1], pb[2])
            emit_ln2(p, s, Q, FlatSrc(xq), xnq, xch, xbf, xsq, stats, t1, ones_bf, ones_f,
                    pb[0][0:1, :], pb[3][0:1, :], pb[1], pb[2])

        s.wait(ten, "ld1")
        s.wait(ten, "xd")
        s.wait(ten, "v")
        for c in range(NQC):
            sl = slice(c * 512, (c + 1) * 512)
            s.wait(ten, "v")
            for kt in range(KT):
                lastm = ten.matmul(pb[4][0:64, :], wisb[:, kt, 0:64],
                                   xnq[:, kt, sl], start=(kt == 0),
                                   stop=(kt == KT - 1))
            s.inc(lastm, "t")
            s.wait(vec, "t")
            vec.tensor_scalar(qidx[:, sl], pb[4][0:64, :], ibia[0:64, 0:1], None,
                              op0=OP.add)
        for c in range(A // 512):
            sl = slice(c * 512, (c + 1) * 512)
            s.wait(ten, "v")
            for kt in range(KT):
                lastm = ten.matmul(pb[4][0:64, :], wisb[:, kt, 64:128],
                                   xn[:, kt, sl], start=(kt == 0),
                                   stop=(kt == KT - 1))
            s.inc(lastm, "t")
            s.wait(vec, "t")
            vec.tensor_scalar(kidx[:, sl], pb[4][0:64, :], ibia[0:64, 1:2], None,
                              op0=OP.add)

        with _ES() as sub:
            scr = sub.enter_context(nc.sbuf_tensor(pref + "scr", [128, A], F32))
            for qt in range(Q // 128):
                tsl = slice(qt * 128, (qt + 1) * 128)
                for kc in range(A // 512):
                    ksl = slice(kc * 512, (kc + 1) * 512)
                    s.wait(ten, "a")
                    s.wait(ten, "v")
                    lastm = ten.matmul(pb[4][:, :], qidx[:, tsl], kidx[:, ksl],
                                       start=True, stop=True,
                                       skip_group_check=True)
                    s.inc(lastm, "t")
                    s.wait(act, "t")
                    s.wait(act, "v")
                    act.activation(scr[:, ksl], pb[4][:, :], AF.Relu)
                for r in range(TOPK // 8):
                    s.wait(vec, "a")
                    vec.max(out=m8[:, :], in_=scr[:, :])
                    if r < TOPK // 8 - 1:
                        vec.match_replace(out=scr[:, :], in_to_replace=m8[:, :],
                                          in_values=scr[:, :], imm_value=-1e30)
                s.wait(ten, "v")
                lastm = ten.transpose(pb[5][0:1, 0:128], m8[:, 7:8], ident[:, :])
                s.inc(lastm, "t")
                s.wait(vec, "t")
                vec.tensor_copy(thrr[0:1, tsl], pb[5][0:1, 0:128])
                s.wait(ten, "v")
                s.wait(act, "v")

        msks = p.sbx(_stage, pref + "msks", [128, NA, 512], BF16)
        osb2 = p.sbx(_stage, pref + "osb2", [64, 512], BF16)
        ofm_sb = p.sbx(_stage, pref + "ofm_sb", [128, KT, 512], BF16)

        wslot_mm = [0, 0]

        def load_w(src, cols_off, slot, ch):
            s.wait(gps, "t", wslot_mm[slot])
            lw = gps.dma_start(
                out=wsb[slot][:, :, :],
                in_=src[:, cols_off:cols_off + 128].rearrange(
                    "(k p) c -> p k c", p=128))
            s.inc(lw, ch, 16)

        for hp in range(8):
            csl128 = slice(hp * 128, (hp + 1) * 128)
            s.wait(sync, "v")
            last = sync.dma_start(out=bqk[:, 0:1],
                                  in_=wqb[0:1, csl128].rearrange("o p -> p o"))
            s.inc(last, "xd", 16)
            last = sync.dma_start(out=bqk[:, 1:2],
                                  in_=wkb[0:1, csl128].rearrange("o p -> p o"))
            s.inc(last, "xd", 16)
            s.wait(ten, "xd")
            s.wait(ten, "v")
            lastm = ten.matmul(pb[5][:, 0:128], ones_f[:, :], wvb_sb[0:1, csl128],
                               start=True, stop=True, skip_group_check=True)
            s.inc(lastm, "t")
            s.wait(vec, "t")
            vec.tensor_copy(bcv[:, :], pb[5][:, 0:128])

            load_w(wkw, hp * 128, 0, "wl0")
            s.wait(ten, "wl0")
            s.wait(ten, "a")
            s.wait(ten, "v")
            for c in range(S // 512):
                sl = slice(c * 512, (c + 1) * 512)
                s.wait(ten, "v")
                for kt in range(KT):
                    lastm = ten.matmul(pb[4][:, :], wsb[0][:, kt, :], xn[:, kt, sl],
                                       start=(kt == 0), stop=(kt == KT - 1))
                s.inc(lastm, "t")
                s.wait(vec, "t")
                vec.tensor_scalar(kfm[:, sl], pb[4][:, :], bqk[:, 1:2], None,
                                  op0=OP.add)
            wslot_mm[0] = s.mark("t")
            load_w(wqw, hp * 128, 1, "wl1")
            s.wait(ten, "wl1")
            s.wait(ten, "v")
            for c in range(NQC):
                sl = slice(c * 512, (c + 1) * 512)
                s.wait(ten, "v")
                for kt in range(KT):
                    lastm = ten.matmul(pb[5][:, :], wsb[1][:, kt, :], xnq[:, kt, sl],
                                       start=(kt == 0), stop=(kt == KT - 1))
                s.inc(lastm, "t")
                s.wait(vec, "t")
                vec.tensor_scalar(qfm[:, sl], pb[5][:, :], bqk[:, 0:1], None,
                                  op0=OP.add)
            wslot_mm[1] = s.mark("t")
            load_w(wvw, hp * 128, 0, "wl0")
            s.wait(ten, "wl0")
            s.wait(ten, "v")
            for tt in range(NKT2):
                tsl = slice(tt * 128, (tt + 1) * 128)
                s.wait(ten, "v")
                for kt in range(KT):
                    lastm = ten.matmul(pb[4][:, 0:128], xn[:, kt, tsl],
                                       wsb[0][:, kt, :],
                                       start=(kt == 0), stop=(kt == KT - 1))
                s.inc(lastm, "t")
                s.wait(vec, "t")
                vec.tensor_add(t1[:, 0:128], pb[4][:, 0:128], bcv[:, :])
                vec.tensor_copy(vtm[:, tt, 0:64], t1[:, 0:64])
                vec.tensor_copy(vtm[:, tt, 65:129], t1[:, 64:128])
            wslot_mm[0] = s.mark("t")
            s.wait(ten, "v")

            for qc in range(NQC):
                qsl = slice(qc * 512, (qc + 1) * 512)
                s.wait(ten, "v")
                lastm = ten.matmul(pb[5][:, :], ones_f[:, :], thrr[0:1, qsl],
                                   start=True, stop=True, skip_group_check=True)
                s.inc(lastm, "t")
                s.wait(vec, "t")
                vec.tensor_copy(thrb[:, :], pb[5][:, :])
                for ka in range(NA):
                    s.wait(ten, "v")
                    lastm = ten.matmul(pb[5][:, :],
                                       kidx[:, ka * 128:(ka + 1) * 128],
                                       qidx[:, qsl], start=True, stop=True,
                                       skip_group_check=True)
                    s.inc(lastm, "t")
                    s.wait(vec, "t")
                    vec.tensor_tensor(msks[:, ka, :], pb[5][:, :], thrb[:, :],
                                      OP.is_ge)
                s.wait(ten, "v")
                for h2 in range(2):
                    hb = 64 * h2
                    opb = pb[2 + h2]
                    exp_mark = [0, 0]
                    p_mark = [("a", 0), ("a", 0)]
                    for j in range(NKT2 + 1):
                        if j < NKT2:
                            kt2 = j
                            sb_ = j % 2
                            s.wait(ten, "a", exp_mark[sb_])
                            lastm = ten.matmul(
                                pb[sb_][:, :],
                                kfm[hb:hb + 64, kt2 * 128:(kt2 + 1) * 128],
                                qfm[hb:hb + 64, qsl],
                                start=True, stop=True, skip_group_check=True)
                            s.inc(lastm, "t")
                            s.wait(act, "t")
                            if kt2 < NA:
                                s.wait(act, "v")
                                act.activation(osb[:, :], pb[sb_][:, :],
                                               AF.Exp, scale=0.125)
                                exp_mark[sb_] = s.mark("a")
                                s.wait(vec, "a")
                                vec.tensor_mul(ptile[:, sb_, :], osb[:, :],
                                               msks[:, kt2, :])
                                p_mark[sb_] = ("v", s.mark("v"))
                            else:
                                act.activation(ptile[:, sb_, :], pb[sb_][:, :],
                                               AF.Exp, scale=0.125)
                                exp_mark[sb_] = s.mark("a")
                                p_mark[sb_] = ("a", s.mark("a"))
                        if j > 0:
                            kk = j - 1
                            sb_ = kk % 2
                            ch_, mk_ = p_mark[sb_]
                            s.wait(ten, ch_, mk_)
                            lastm = ten.matmul(
                                opb[0:65, :],
                                vtm[:, kk, 65 * h2:65 * h2 + 65],
                                ptile[:, sb_, :],
                                start=(kk == 0), stop=(kk == NKT2 - 1),
                                skip_group_check=True)
                            s.inc(lastm, "t")
                for h2 in range(2):
                    opb = pb[2 + h2]
                    s.wait(vec, "t")
                    vec.reciprocal(rd1[0:1, :], opb[64:65, :])
                    s.wait(ten, "v")
                    lastm = ten.matmul(pb[5][0:64, :], ones_f[:, 0:64],
                                       rd1[0:1, :], start=True, stop=True,
                                       skip_group_check=True)
                    s.inc(lastm, "t")
                    s.wait(vec, "t")
                    vec.tensor_copy(osb[0:64, :], opb[0:64, :])
                    h_global = hp * 2 + h2
                    orow = h_global * 64
                    s.wait(vec, "st")
                    vec.tensor_mul(osb2[0:64, :], osb[0:64, :], pb[5][0:64, :])
                    s.wait(sync, "v")
                    last = sync.dma_start(out=odram[orow:orow + 64, qsl],
                                          in_=osb2[0:64, :])
                    s.inc(last, "st", 16)
                    s.wait(ten, "v")

        s.wait(sync, "st")
        wslot2 = list(wslot_mm)
        for c in range(NQC):
            sl = slice(c * 512, (c + 1) * 512)
            s.wait(sync, "t", 0 if c == 0 else s.mark("t"))
            last = sync.dma_start(
                out=ofm_sb[:, :, :],
                in_=odram[:, sl].rearrange("(k p) q -> p k q", p=128))
            s.inc(last, "eb", 16)
            for ot in range(KT):
                slot = ot % 2
                ch = f"wl{slot}"
                s.wait(gps, "t", wslot2[slot])
                lw = gps.dma_start(
                    out=wsb[slot][:, :, :],
                    in_=wow[:, ot * 128:(ot + 1) * 128].rearrange(
                        "(k p) c -> p k c", p=128))
                s.inc(lw, ch, 16)
                s.wait(ten, ch)
                s.wait(ten, "eb")
                s.wait(ten, "v")
                for kt in range(KT):
                    lastm = ten.matmul(pb[4][:, :], wsb[slot][:, kt, :],
                                       ofm_sb[:, kt, :], start=(kt == 0),
                                       stop=(kt == KT - 1))
                s.inc(lastm, "t")
                wslot2[slot] = s.mark("t")
                s.wait(sync, "v")
                last = sync.dma_start(out=xr[:, :],
                                      in_=xq[ot * 128:(ot + 1) * 128, sl])
                s.inc(last, "xd", 16)
                s.wait(vec, "xd")
                s.wait(vec, "t")
                s.wait(vec, "st")
                vec.tensor_scalar(t1[:, :], pb[4][:, :], bsb[:, ot, 0:1], None,
                                  op0=OP.add)
                vec.tensor_add(t1[:, :], t1[:, :], xr[:, :])
                s.wait(sync, "v")
                last = sync.dma_start(out=y[ot * 128:(ot + 1) * 128, sl],
                                      in_=t1[:, :])
                s.inc(last, "st", 16)
        s.wait(sync, "st")
        s.wait(gps, "st")



def emit_a1(p, s, io, pref, S=4096, Q=2048):
    NKT2 = S // 128
    NQC = Q // 512
    nc = p.nc
    from contextlib import ExitStack as _ES0
    with _ES0() as _stage:
        xf = io["xf"]
        xq = io["xq"]
        wq = io["wq"]
        wk = io["wk"]
        wv = io["wv"]
        cm = io["cm"]
        y = io["y"]

        xn = p.sbx(_stage, pref + "xn", [128, KT, S], BF16)
        xnq = p.sbx(_stage, pref + "xnq", [128, KT, Q], BF16)
        xch = p.sbx(_stage, pref + "xch", [128, KT, 512], F32)
        xr = p.sbx(_stage, pref + "xr", [128, 512], F32)
        xbf = p.sbx(_stage, pref + "xbf", [128, 512], BF16)
        xsq = p.sbx(_stage, pref + "xsq", [128, 512], BF16)
        stats = p.sbx(_stage, pref + "stats", [1, 5, 512], F32)
        t1 = p.sbx(_stage, pref + "t1", [128, 512], F32)
        ones_bf = p.sbx(_stage, pref + "ones_bf", [128, 1], BF16)
        ones_f = p.sbx(_stage, pref + "ones_f", [1, 128], F32)
        wqk_sb = p.sbx(_stage, pref + "wqk_sb", [128, KT, 128], BF16)
        wv_sb = p.sbx(_stage, pref + "wv_sb", [128, KT, 256], BF16)
        kfm = p.sbx(_stage, pref + "kfm", [64, S], BF16)
        qfm = p.sbx(_stage, pref + "qfm", [64, Q], BF16)
        vtm = p.sbx(_stage, pref + "vtm", [128, NKT2, 256], BF16)
        cms = p.sbx(_stage, pref + "cms", [128, NKT2, 512], BF16)
        rdq = p.sbx(_stage, pref + "rdq", [1, NQC, 512], F32)
        osb = p.sbx(_stage, pref + "osb", [128, 512], BF16)

        pb = [p.psx(_stage, pref + f"pb{i}", [128, 512]) for i in range(6)]

        sync, gps, ten = p.sync, p.gpsd, p.ten
        vec, act = p.vec, p.act

        vec.memset(ones_bf[:, :], 1.0)
        vec.memset(ones_f[:, :], 1.0)

        emit_ln2(p, s, S, io["xfsrc"], xn, xch, xbf, xsq, stats, t1, ones_bf, ones_f,
                pb[0][0:1, :], pb[3][0:1, :], pb[1], pb[2])
        emit_ln2(p, s, Q, FlatSrc(xq), xnq, xch, xbf, xsq, stats, t1, ones_bf, ones_f,
                pb[0][0:1, :], pb[3][0:1, :], pb[1], pb[2])

        lw = gps.dma_start(out=wqk_sb[:, :, 0:64],
                           in_=wq.rearrange("(k p) c -> p k c", p=128))
        s.inc(lw, "wl0", 16)
        lw = gps.dma_start(out=wqk_sb[:, :, 64:128],
                           in_=wk.rearrange("(k p) c -> p k c", p=128))
        s.inc(lw, "wl0", 16)
        s.wait(ten, "wl0")
        s.wait(ten, "v")
        for c in range(NQC):
            sl = slice(c * 512, (c + 1) * 512)
            s.wait(ten, "v")
            for kt in range(KT):
                lastm = ten.matmul(pb[4][0:64, :], wqk_sb[:, kt, 0:64],
                                   xnq[:, kt, sl], start=(kt == 0),
                                   stop=(kt == KT - 1))
            s.inc(lastm, "t")
            s.wait(vec, "t")
            vec.tensor_copy(qfm[:, sl], pb[4][0:64, :])
        for c in range(S // 512):
            sl = slice(c * 512, (c + 1) * 512)
            s.wait(ten, "v")
            for kt in range(KT):
                lastm = ten.matmul(pb[4][0:64, :], wqk_sb[:, kt, 64:128],
                                   xn[:, kt, sl], start=(kt == 0),
                                   stop=(kt == KT - 1))
            s.inc(lastm, "t")
            s.wait(vec, "t")
            vec.tensor_copy(kfm[:, sl], pb[4][0:64, :])

        vload_mark = 0
        for vh in range(4):
            s.wait(gps, "t", vload_mark)
            lw = gps.dma_start(
                out=wv_sb[:, :, :],
                in_=wv[:, vh * 256:(vh + 1) * 256].rearrange(
                    "(k p) c -> p k c", p=128))
            s.inc(lw, "wl1", 16)
            s.wait(ten, "wl1")
            for tt in range(NKT2):
                tsl = slice(tt * 128, (tt + 1) * 128)
                s.wait(ten, "v")
                for kt in range(KT):
                    lastm = ten.matmul(pb[4][:, 0:256], xn[:, kt, tsl],
                                       wv_sb[:, kt, :], start=(kt == 0),
                                       stop=(kt == KT - 1))
                s.inc(lastm, "t")
                s.wait(vec, "t")
                vec.tensor_copy(vtm[:, tt, :], pb[4][:, 0:256])
            vload_mark = s.mark("t")
            for qc in range(NQC):
                qsl = slice(qc * 512, (qc + 1) * 512)
                s.wait(sync, "v")
                last = sync.dma_start(
                    out=cms[:, :, :],
                    in_=cm[:, qsl].rearrange("(kt p) q -> p kt q", p=128))
                s.inc(last, "eb", 16)
                s.wait(vec, "eb")
                for kt2 in range(NKT2):
                    s.wait(ten, "a")
                    lastm = ten.matmul(pb[0][:, :], kfm[:, kt2 * 128:(kt2 + 1) * 128],
                                       qfm[:, qsl], start=True, stop=True,
                                       skip_group_check=True)
                    s.inc(lastm, "t")
                    s.wait(act, "t")
                    s.wait(act, "v")
                    act.activation(osb[:, :], pb[0][:, :], AF.Exp, scale=0.125)
                    s.wait(vec, "a")
                    vec.tensor_mul(cms[:, kt2, :], osb[:, :], cms[:, kt2, :])
                    s.wait(ten, "v")
                    lastm = ten.matmul(pb[1][0:1, :], ones_bf[:, :],
                                       cms[:, kt2, :], start=(kt2 == 0),
                                       stop=(kt2 == NKT2 - 1),
                                       skip_group_check=True)
                    s.inc(lastm, "t")
                if vh == 0:
                    s.wait(vec, "t")
                    vec.reciprocal(rdq[0:1, qc, :], pb[1][0:1, :])
                for dvt in range(2):
                    s.wait(ten, "v")
                    for kt2 in range(NKT2):
                        lastm = ten.matmul(
                            pb[2][:, :], vtm[:, kt2, dvt * 128:(dvt + 1) * 128],
                            cms[:, kt2, :], start=(kt2 == 0),
                            stop=(kt2 == NKT2 - 1), skip_group_check=True)
                    s.inc(lastm, "t")
                    s.wait(ten, "v")
                    lastm = ten.matmul(pb[3][:, :], ones_f[:, :], rdq[0:1, qc, :],
                                       start=True, stop=True, skip_group_check=True)
                    s.inc(lastm, "t")
                    s.wait(vec, "t")
                    vec.tensor_copy(osb[:, :], pb[2][:, :])
                    ot = vh * 2 + dvt
                    s.wait(sync, "v")
                    last = sync.dma_start(out=xr[:, :],
                                          in_=xq[ot * 128:(ot + 1) * 128, qsl])
                    s.inc(last, "xd", 16)
                    s.wait(vec, "xd")
                    s.wait(vec, "st")
                    vec.tensor_mul(t1[:, :], osb[:, :], pb[3][:, :])
                    vec.tensor_add(t1[:, :], t1[:, :], xr[:, :])
                    s.wait(sync, "v")
                    last = sync.dma_start(out=y[ot * 128:(ot + 1) * 128, qsl],
                                          in_=t1[:, :])
                    s.inc(last, "st", 16)
                    s.wait(ten, "v")
        s.wait(sync, "st")
        s.wait(gps, "st")



def emit_b1(p, s, io, pref, S=4096, Q=4096):
    NKT2 = S // 128
    NQC = Q // 512
    nc = p.nc
    from contextlib import ExitStack as _ES0
    with _ES0() as _stage:
        xf = io["xf"]
        xq = io["xq"]
        wkT = io["wkT"]
        wqT = io["wqT"]
        wvT = io["wvT"]
        woT = io["woT"]
        bo = io["bo"]
        klng2 = io["klng2"]
        klnb2 = io["klnb2"]
        qlng2 = io["qlng2"]
        qlnb2 = io["qlnb2"]
        cm = io["cm"]
        blk2Tc = io["blk2Tc"]
        y = io["y"]

        xn = p.sbx(_stage, pref + "xn", [128, KT, S], BF16)
        qn_all = p.sbx(_stage, pref + "qn_all", [128, KT, Q], BF16)
        xr = p.sbx(_stage, pref + "xr", [128, 512], F32)
        xbf = p.sbx(_stage, pref + "xbf", [128, 512], BF16)
        xsq = p.sbx(_stage, pref + "xsq", [128, 512], BF16)
        stats = p.sbx(_stage, pref + "stats", [1, 5, 512], F32)
        hstats = p.sbx(_stage, pref + "hstats", [2, 5, 512], F32)
        t1 = p.sbx(_stage, pref + "t1", [128, 512], F32)
        t2 = p.sbx(_stage, pref + "t2", [128, 512], F32)
        ones_bf = p.sbx(_stage, pref + "ones_bf", [128, 1], BF16)
        ones_f = p.sbx(_stage, pref + "ones_f", [1, 128], F32)
        blk2 = p.sbx(_stage, pref + "blk2", [128, 2], BF16)
        blk2T = p.sbx(_stage, pref + "blk2T", [2, 128], BF16)
        gln = p.sbx(_stage, pref + "gln", [128, 4], F32)
        wsb = [p.sbx(_stage, pref + f"wsb{i}", [128, KT, 128], BF16) for i in range(2)]
        kfm = p.sbx(_stage, pref + "kfm", [128, S], BF16)
        vtm = p.sbx(_stage, pref + "vtm", [128, NKT2, 130], BF16)
        ptile = p.sbx(_stage, pref + "ptile", [128, 2, 512], BF16)
        osb = p.sbx(_stage, pref + "osb", [128, 512], BF16)
        rd1 = p.sbx(_stage, pref + "rd1", [1, 512], F32)
        bsb = p.sbx(_stage, pref + "bsb", [128, KT, 1], F32)

        pb = [p.psx(_stage, pref + f"pb{i}", [128, 512]) for i in range(6)]

        sync, gps, ten = p.sync, p.gpsd, p.ten
        vec, act = p.vec, p.act

        vec.memset(ones_bf[:, :], 1.0)
        vec.memset(ones_f[:, :], 1.0)
        vec.memset(blk2[:, :], 0.0)
        vec.memset(blk2[0:64, 0:1], 1.0)
        vec.memset(blk2[64:128, 1:2], 1.0)
        lw = gps.dma_start(out=blk2T[:, :], in_=blk2Tc)
        s.inc(lw, "ld1", 16)
        s.wait(ten, "ld1")
        vec.memset(vtm[:, :, 64:65], 1.0)
        vec.memset(vtm[:, :, 129:130], 1.0)
        for i, t in enumerate([klng2, klnb2, qlng2, qlnb2]):
            last = sync.dma_start(out=gln[:, i:i + 1], in_=t)
            s.inc(last, "xd", 16)
        last = sync.dma_start(out=bsb[:, :, :],
                              in_=bo.rearrange("(k p) o -> p k o", p=128),
                              allow_slow_non_contiguous=True)
        s.inc(last, "xd", 16)

        wslot_mm = [0, 0]

        def load_w(src, cols_off, slot, ch):
            s.wait(gps, "t", wslot_mm[slot])
            lw = gps.dma_start(
                out=wsb[slot][:, :, :],
                in_=src[:, cols_off:cols_off + 128].rearrange(
                    "(k p) c -> p k c", p=128))
            s.inc(lw, ch, 16)

        def headln(dst, src_fm, ncols, gcol, bcol):
            for c in range(ncols // 512):
                sl = slice(c * 512, (c + 1) * 512)
                s.wait(vec, "t")
                vec.tensor_copy(xbf[:, :], src_fm[:, sl])
                vec.tensor_mul(xsq[:, :], xbf[:, :], xbf[:, :])
                s.wait(ten, "v")
                ten.matmul(pb[4][0:2, :], blk2[:, :], xbf[:, :],
                           start=True, stop=True, skip_group_check=True)
                lastm = ten.matmul(pb[5][0:2, :], blk2[:, :], xsq[:, :],
                                   start=True, stop=True, skip_group_check=True)
                s.inc(lastm, "t")
                s.wait(vec, "t")
                vec.tensor_scalar(hstats[0:2, 2, :], pb[4][0:2, :], 1.0 / 64, None,
                                  op0=OP.mult)
                vec.tensor_scalar(hstats[0:2, 1, :], pb[5][0:2, :], 1.0 / 64, None,
                                  op0=OP.mult)
                vec.tensor_mul(hstats[0:2, 3, :], hstats[0:2, 2, :], hstats[0:2, 2, :])
                vec.tensor_sub(hstats[0:2, 3, :], hstats[0:2, 1, :], hstats[0:2, 3, :])
                vec.tensor_scalar(hstats[0:2, 3, :], hstats[0:2, 3, :], EPS, None,
                                  op0=OP.add)
                s.wait(act, "v")
                act.activation(hstats[0:2, 4, :], hstats[0:2, 3, :], AF.Ln)
                act.activation(hstats[0:2, 4, :], hstats[0:2, 4, :], AF.Exp,
                               scale=-0.5)
                s.wait(vec, "a")
                vec.tensor_copy(xbf[0:2, 0:512], hstats[0:2, 2, :])
                vec.tensor_copy(xsq[0:2, 0:512], hstats[0:2, 4, :])
                s.wait(ten, "v")
                ten.matmul(pb[4][:, :], blk2T[:, :], xbf[0:2, 0:512],
                           start=True, stop=True, skip_group_check=True)
                lastm = ten.matmul(pb[5][:, :], blk2T[:, :], xsq[0:2, 0:512],
                                   start=True, stop=True, skip_group_check=True)
                s.inc(lastm, "t")
                s.wait(vec, "t")
                vec.tensor_sub(t2[:, :], src_fm[:, sl], pb[4][:, :])
                vec.tensor_mul(t2[:, :], t2[:, :], pb[5][:, :])
                vec.tensor_scalar(dst[:, sl], t2[:, :], gln[:, gcol:gcol + 1],
                                  gln[:, bcol:bcol + 1], op0=OP.mult, op1=OP.add)
                s.wait(ten, "v")

        from contextlib import ExitStack as _ES
        with _ES() as sub:
            xch = sub.enter_context(nc.sbuf_tensor(pref + "xch", [128, KT, 512], F32))
            xnq = xn
            emit_ln2(p, s, S, io["xfsrc"], xn, xch, xbf, xsq, stats, t1, ones_bf, ones_f,
                    pb[0][0:1, :], pb[3][0:1, :], pb[1], pb[2])
            for hp in range(8):
                load_w(wqT, hp * 128, 1, "wl1")
                s.wait(ten, "wl1")
                for c in range(NQC):
                    sl = slice(c * 512, (c + 1) * 512)
                    s.wait(ten, "v")
                    for kt in range(KT):
                        lastm = ten.matmul(pb[5][:, :], wsb[1][:, kt, :],
                                           xnq[:, kt, sl], start=(kt == 0),
                                           stop=(kt == KT - 1))
                    s.inc(lastm, "t")
                    s.wait(vec, "t")
                    vec.tensor_copy(qn_all[:, hp, sl], pb[5][:, :])
                wslot_mm[1] = s.mark("t")
                headln(qn_all[:, hp, :], qn_all[:, hp, :], Q, 2, 3)
        cms = p.sbx(_stage, pref + "cms", [128, 16, 512], BF16)
        odram = io["odram"]
        osb2 = p.sbx(_stage, pref + "osb2", [64, 512], BF16)
        ofm_sb = p.sbx(_stage, pref + "ofm_sb", [128, KT, 512], BF16)

        for hp in range(8):
            load_w(wkT, hp * 128, 0, "wl0")
            s.wait(ten, "wl0")
            s.wait(ten, "a")
            s.wait(ten, "v")
            for c in range(S // 512):
                sl = slice(c * 512, (c + 1) * 512)
                s.wait(ten, "v")
                for kt in range(KT):
                    lastm = ten.matmul(pb[4][:, :], wsb[0][:, kt, :], xn[:, kt, sl],
                                       start=(kt == 0), stop=(kt == KT - 1))
                s.inc(lastm, "t")
                s.wait(vec, "t")
                vec.tensor_copy(kfm[:, sl], pb[4][:, :])
            wslot_mm[0] = s.mark("t")
            headln(kfm, kfm, S, 0, 1)
            load_w(wvT, hp * 128, 0, "wl0")
            s.wait(ten, "wl0")
            s.wait(ten, "v")
            for tt in range(NKT2):
                tsl = slice(tt * 128, (tt + 1) * 128)
                s.wait(ten, "v")
                for kt in range(KT):
                    lastm = ten.matmul(pb[4][:, 0:128], xn[:, kt, tsl],
                                       wsb[0][:, kt, :],
                                       start=(kt == 0), stop=(kt == KT - 1))
                s.inc(lastm, "t")
                s.wait(vec, "t")
                vec.tensor_copy(vtm[:, tt, 0:64], pb[4][:, 0:64])
                vec.tensor_copy(vtm[:, tt, 65:129], pb[4][:, 64:128])
            wslot_mm[0] = s.mark("t")
            s.wait(ten, "v")

            for qc in range(NQC):
                qsl = slice(qc * 512, (qc + 1) * 512)
                kts = [k for k in range(NKT2) if 128 * k <= 512 * qc + 511]
                dset = set(range(4 * qc, 4 * qc + 4))
                s.wait(sync, "v")
                last = sync.dma_start(
                    out=cms[:, 0:4, :],
                    in_=cm[qc * 512:(qc + 1) * 512, qsl].rearrange(
                        "(kt p) q -> p kt q", p=128))
                s.inc(last, "eb", 16)
                s.wait(vec, "eb")
                s.wait(act, "eb")
                for h2 in range(2):
                    hb = 64 * h2
                    opb = pb[2 + h2]
                    exp_mark = [0, 0]
                    p_mark = [0, 0]
                    n = len(kts)
                    for j in range(n + 1):
                        if j < n:
                            kt2 = kts[j]
                            sb_ = j % 2
                            s.wait(ten, "a", exp_mark[sb_])
                            lastm = ten.matmul(
                                pb[sb_][:, :],
                                kfm[hb:hb + 64, kt2 * 128:(kt2 + 1) * 128],
                                qn_all[hb:hb + 64, hp, qsl],
                                start=True, stop=True, skip_group_check=True)
                            s.inc(lastm, "t")
                            s.wait(act, "t")
                            if kt2 in dset:
                                s.wait(act, "v")
                                act.activation(osb[:, :], pb[sb_][:, :],
                                               AF.Exp, scale=1.0 / 32.0)
                                exp_mark[sb_] = s.mark("a")
                                s.wait(vec, "a")
                                vec.tensor_mul(ptile[:, sb_, :], osb[:, :],
                                               cms[:, kt2 - 4 * qc, :])
                                p_mark[sb_] = ("v", s.mark("v"))
                            else:
                                act.activation(ptile[:, sb_, :], pb[sb_][:, :],
                                               AF.Exp, scale=1.0 / 32.0)
                                exp_mark[sb_] = s.mark("a")
                                p_mark[sb_] = ("a", s.mark("a"))
                        if j > 0:
                            kk = kts[j - 1]
                            sb_ = (j - 1) % 2
                            ch_, mk_ = p_mark[sb_]
                            s.wait(ten, ch_, mk_)
                            lastm = ten.matmul(
                                opb[0:65, :],
                                vtm[:, kk, 65 * h2:65 * h2 + 65],
                                ptile[:, sb_, :],
                                start=(j - 1 == 0), stop=(j - 1 == n - 1),
                                skip_group_check=True)
                            s.inc(lastm, "t")
                for h2 in range(2):
                    opb = pb[2 + h2]
                    s.wait(vec, "t")
                    vec.reciprocal(rd1[0:1, :], opb[64:65, :])
                    s.wait(ten, "v")
                    lastm = ten.matmul(pb[5][0:64, :], ones_f[:, 0:64],
                                       rd1[0:1, :], start=True, stop=True,
                                       skip_group_check=True)
                    s.inc(lastm, "t")
                    s.wait(vec, "t")
                    vec.tensor_copy(osb[0:64, :], opb[0:64, :])
                    h_global = hp * 2 + h2
                    orow = h_global * 64
                    s.wait(vec, "st")
                    vec.tensor_mul(osb2[0:64, :], osb[0:64, :], pb[5][0:64, :])
                    s.wait(sync, "v")
                    last = sync.dma_start(out=odram[orow:orow + 64, qsl],
                                          in_=osb2[0:64, :])
                    s.inc(last, "st", 16)
                    s.wait(ten, "v")

        s.wait(sync, "st")
        wslot2 = list(wslot_mm)
        for c in range(NQC):
            sl = slice(c * 512, (c + 1) * 512)
            s.wait(sync, "t", 0 if c == 0 else s.mark("t"))
            last = sync.dma_start(
                out=ofm_sb[:, :, :],
                in_=odram[:, sl].rearrange("(k p) q -> p k q", p=128))
            s.inc(last, "eb", 16)
            for ot in range(KT):
                slot = ot % 2
                ch = f"wl{slot}"
                s.wait(gps, "t", wslot2[slot])
                lw = gps.dma_start(
                    out=wsb[slot][:, :, :],
                    in_=woT[:, ot * 128:(ot + 1) * 128].rearrange(
                        "(k p) c -> p k c", p=128))
                s.inc(lw, ch, 16)
                s.wait(ten, ch)
                s.wait(ten, "eb")
                s.wait(ten, "v")
                for kt in range(KT):
                    lastm = ten.matmul(pb[4][:, :], wsb[slot][:, kt, :],
                                       ofm_sb[:, kt, :], start=(kt == 0),
                                       stop=(kt == KT - 1))
                s.inc(lastm, "t")
                wslot2[slot] = s.mark("t")
                s.wait(sync, "v")
                last = sync.dma_start(out=xr[:, :],
                                      in_=xq[ot * 128:(ot + 1) * 128, sl])
                s.inc(last, "xd", 16)
                s.wait(vec, "xd")
                s.wait(vec, "t")
                s.wait(vec, "st")
                vec.tensor_scalar(t1[:, :], pb[4][:, :], bsb[:, ot, 0:1], None,
                                  op0=OP.add)
                vec.tensor_add(t1[:, :], t1[:, :], xr[:, :])
                s.wait(sync, "v")
                last = sync.dma_start(out=y[ot * 128:(ot + 1) * 128, sl],
                                      in_=t1[:, :])
                s.inc(last, "st", 16)
        s.wait(sync, "st")
        s.wait(gps, "st")



def emit_gla(p, s, io, pref, S=4096):
    NCH = S // 128
    nc = p.nc
    H, DK = 8, 64
    from contextlib import ExitStack as _ES0
    with _ES0() as _stage:
        xf = io["xf"]
        wq = io["wq"]
        wk = io["wk"]
        wv = io["wv"]
        wqa = io["wqa"]
        wka = io["wka"]
        wga = io["wga"]
        wqb = io["wqb"]
        wkb = io["wkb"]
        wgb = io["wgb"]
        wg1a = io["wg1a"]
        wg2a = io["wg2a"]
        wg1b = io["wg1b"]
        wg2b = io["wg2b"]
        bg1r = io["bg1r"]
        bg2r = io["bg2r"]
        we = io["we"]
        normwb = io["normwb"]
        wo = io["wo"]
        trilm = io["trilm"]
        identf = io["identf"]
        blk2Tc = io["blk2Tc"]
        y = io["y"]

        xn = p.sbx(_stage, pref + "xn", [128, KT, S], BF16)
        xch = p.sbx(_stage, pref + "xch", [128, KT, 512], F32)
        xr = p.sbx(_stage, pref + "xr", [128, 128], F32)
        xbf = p.sbx(_stage, pref + "xbf", [128, 512], BF16)
        xsq = p.sbx(_stage, pref + "xsq", [128, 512], BF16)
        stats = p.sbx(_stage, pref + "stats", [1, 5, 512], F32)
        t1 = p.sbx(_stage, pref + "t1", [128, 512], F32)
        t2 = p.sbx(_stage, pref + "t2", [128, 512], F32)
        tb = p.sbx(_stage, pref + "tb", [128, 512], BF16)
        ones_bf = p.sbx(_stage, pref + "ones_bf", [128, 1], BF16)
        ones_f = p.sbx(_stage, pref + "ones_f", [1, 128], F32)
        ident = p.sbx(_stage, pref + "ident", [128, 128], F32)
        identb = p.sbx(_stage, pref + "identb", [128, 128], BF16)
        tril = p.sbx(_stage, pref + "tril", [128, 128], BF16)
        nwb0 = p.sbx(_stage, pref + "nwb0", [128, 512], F32)
        dec = p.sbx(_stage, pref + "dec", [128, 4], F32)
        zer = p.sbx(_stage, pref + "zer", [128, 128], F32)
        blk2 = p.sbx(_stage, pref + "blk2", [128, 2], BF16)
        blk2T = p.sbx(_stage, pref + "blk2T", [2, 128], BF16)
        nwb = p.sbx(_stage, pref + "nwb", [1, 512], F32)
        wq_s = p.sbx(_stage, pref + "wq_s", [128, KT, 512], BF16)
        wk_s = p.sbx(_stage, pref + "wk_s", [128, KT, 512], BF16)
        wv_s = p.sbx(_stage, pref + "wv_s", [128, KT, 512], BF16)
        wqa_s = p.sbx(_stage, pref + "wqa_s", [128, KT, 64], BF16)
        wka_s = p.sbx(_stage, pref + "wka_s", [128, KT, 64], BF16)
        wga_s = p.sbx(_stage, pref + "wga_s", [128, KT, 64], BF16)
        wqb_s = p.sbx(_stage, pref + "wqb_s", [64, 512], BF16)
        wkb_s = p.sbx(_stage, pref + "wkb_s", [64, 512], BF16)
        wgb_s = p.sbx(_stage, pref + "wgb_s", [64, 512], BF16)
        wg1a_s = p.sbx(_stage, pref + "wg1a_s", [128, KT, 16], BF16)
        wg2a_s = p.sbx(_stage, pref + "wg2a_s", [128, KT, 16], BF16)
        wg1b_s = p.sbx(_stage, pref + "wg1b_s", [16, 512], BF16)
        wg2b_s = p.sbx(_stage, pref + "wg2b_s", [16, 512], BF16)
        bg1_s = p.sbx(_stage, pref + "bg1_s", [128, 4], F32)
        bg2_s = p.sbx(_stage, pref + "bg2_s", [128, 4], F32)
        we_s = p.sbx(_stage, pref + "we_s", [128, KT, 4], BF16)
        wo_s = p.sbx(_stage, pref + "wo_s", [128, 4, D], BF16)
        q1c = p.sbx(_stage, pref + "q1c", [128, 4, 128], BF16)
        k1c = p.sbx(_stage, pref + "k1c", [128, 4, 128], BF16)
        q2c = p.sbx(_stage, pref + "q2c", [128, 4, 128], BF16)
        k2c = p.sbx(_stage, pref + "k2c", [128, 4, 128], BF16)
        vc = p.sbx(_stage, pref + "vc", [128, 512], BF16)
        gk1c = p.sbx(_stage, pref + "gk1c", [128, 4, 128], F32)
        gk2c = p.sbx(_stage, pref + "gk2c", [128, 4, 128], F32)
        gke = p.sbx(_stage, pref + "gke", [128, 4, 128], F32)
        egv = p.sbx(_stage, pref + "egv", [128, 4, 128], F32)
        env = p.sbx(_stage, pref + "env", [128, 4, 128], F32)
        elv = p.sbx(_stage, pref + "elv", [128, 4, 128], F32)
        qt_ = p.sbx(_stage, pref + "qt_", [128, 4, 128], BF16)
        kg_ = p.sbx(_stage, pref + "kg_", [128, 4, 128], BF16)
        ko_ = p.sbx(_stage, pref + "ko_", [128, 4, 128], BF16)
        kot = p.sbx(_stage, pref + "kot", [128, 512], BF16)
        pa_ = p.sbx(_stage, pref + "pa_", [128, 8, 128], BF16)
        facb = p.sbx(_stage, pref + "facb", [128, 128], BF16)
        mkb = p.sbx(_stage, pref + "mkb", [128, 128], BF16)
        xl_ = p.sbx(_stage, pref + "xl_", [64, 128], BF16)
        glc = p.sbx(_stage, pref + "glc", [128, 512], BF16)
        rwt = p.sbx(_stage, pref + "rwt", [128, 8], F32)
        ofm = p.sbx(_stage, pref + "ofm", [128, 4, 128], BF16)
        S_ = [p.sbx(_stage, pref + f"S{i}", [128, 4, 64], F32) for i in range(5)]
        Sbf = p.sbx(_stage, pref + "Sbf", [128, 4, 64], BF16)

        pb = [p.psx(_stage, pref + f"pb{i}", [128, 512]) for i in range(6)]
        pbt = p.psx(_stage, pref + "pbt", [128, 512], BF16)

        sync, gps, ten = p.sync, p.gpsd, p.ten
        vec, act = p.vec, p.act

        vec.memset(ones_bf[:, :], 1.0)
        vec.memset(ones_f[:, :], 1.0)
        vec.memset(blk2[:, :], 0.0)
        vec.memset(blk2[0:64, 0:1], 1.0)
        vec.memset(blk2[64:128, 1:2], 1.0)
        for i in range(5):
            vec.memset(S_[i][:, :, :], 0.0)
        vec.memset(Sbf[:, :, :], 0.0)

        def ld(dst, src, rearr=None, ch="ld1"):
            inp = src if rearr is None else src.rearrange(rearr, p=128)
            lw = gps.dma_start(out=dst, in_=inp)
            s.inc(lw, ch, 16)

        ld(wq_s[:, :, :], wq, "(k p) c -> p k c")
        ld(wk_s[:, :, :], wk, "(k p) c -> p k c")
        ld(wv_s[:, :, :], wv, "(k p) c -> p k c")
        ld(wqa_s[:, :, :], wqa, "(k p) c -> p k c")
        ld(wka_s[:, :, :], wka, "(k p) c -> p k c")
        ld(wga_s[:, :, :], wga, "(k p) c -> p k c")
        ld(wqb_s[:, :], wqb)
        ld(wkb_s[:, :], wkb)
        ld(wgb_s[:, :], wgb)
        ld(wg1a_s[:, :, :], wg1a, "(k p) c -> p k c")
        ld(wg2a_s[:, :, :], wg2a, "(k p) c -> p k c")
        ld(wg1b_s[:, :], wg1b)
        ld(wg2b_s[:, :], wg2b)
        ld(we_s[:, :, :], we, "(k p) c -> p k c")
        ld(wo_s[:, :, :], wo, "(k p) c -> p k c")
        ld(tril[:, :], trilm)
        ld(blk2T[:, :], blk2Tc)
        lw = sync.dma_start(out=bg1_s[:, :], in_=bg1r.rearrange("(k p) o -> p k o", p=128),
                            allow_slow_non_contiguous=True)
        s.inc(lw, "xd", 16)
        lw = sync.dma_start(out=bg2_s[:, :], in_=bg2r.rearrange("(k p) o -> p k o", p=128),
                            allow_slow_non_contiguous=True)
        s.inc(lw, "xd", 16)
        lw = sync.dma_start(out=ident[:, :], in_=identf)
        s.inc(lw, "xd", 16)
        lw = sync.dma_start(out=nwb[:, :], in_=normwb)
        s.inc(lw, "xd", 16)

        emit_ln2(p, s, S, io["xfsrc"], xn, xch, xbf, xsq, stats, t1, ones_bf, ones_f,
                pb[0][0:1, :], pb[3][0:1, :], pb[1], pb[2])

        s.wait(ten, "ld1")
        s.wait(ten, "xd")
        s.wait(ten, "v")
        s.wait(vec, "xd")
        vec.tensor_copy(identb[:, :], ident[:, :])
        vec.memset(zer[:, :], 0.0)
        lastm = ten.matmul(pb[5][:, :], ones_f[:, :], nwb[0:1, :],
                           start=True, stop=True)
        s.inc(lastm, "t")
        s.wait(vec, "t")
        vec.tensor_copy(nwb0[:, :], pb[5][:, :])
        s.wait(ten, "v")

        def proj_fm(dst, w_s, csl, ncol=512, scale=None, bias=None, post=None):
            for rt in range(ncol // 128):
                s.wait(ten, "v")
                for kt in range(KT):
                    lastm = ten.matmul(pb[0][:, 0:128], w_s[:, kt, rt * 128:(rt + 1) * 128],
                                       xn[:, kt, csl], start=(kt == 0),
                                       stop=(kt == KT - 1))
                s.inc(lastm, "t")
                s.wait(vec, "t")
                if scale is not None:
                    vec.tensor_scalar(dst[:, rt, :], pb[0][:, 0:128], scale, None,
                                      op0=OP.mult)
                else:
                    vec.tensor_copy(dst[:, rt, :], pb[0][:, 0:128])

        def logsig16(dst, raw_psum4, bias_s):
            for rt in range(4):
                s.wait(vec, "t")
                s.wait(vec, "st")
                vec.tensor_scalar(t1[:, 0:128], raw_psum4[rt], bias_s[:, rt:rt + 1],
                                  None, op0=OP.add)
                s.wait(act, "v")
                act.activation(t2[:, 0:128], t1[:, 0:128], AF.Abs)
                act.activation(t2[:, 0:128], t2[:, 0:128], AF.Exp, scale=-1.0)
                s.wait(vec, "a")
                vec.tensor_scalar(t2[:, 0:128], t2[:, 0:128], 1.0, None, op0=OP.add)
                s.wait(act, "v")
                act.activation(t2[:, 0:128], t2[:, 0:128], AF.Ln)
                s.wait(vec, "a")
                vec.tensor_scalar_min(t1[:, 0:128], t1[:, 0:128], 0.0)
                vec.tensor_sub(t1[:, 0:128], t1[:, 0:128], t2[:, 0:128])
                vec.tensor_scalar(dst[:, rt, :], t1[:, 0:128], 1.0 / 16.0, None,
                                  op0=OP.mult)

        for cc in range(NCH):
            csl = slice(cc * 128, (cc + 1) * 128)
            proj_fm(q1c, wq_s, csl, scale=0.125)
            proj_fm(k1c, wk_s, csl)
            s.wait(ten, "v")
            for kt in range(KT):
                lastm = ten.matmul(pb[0][:, :], xn[:, kt, csl], wv_s[:, kt, :],
                                   start=(kt == 0), stop=(kt == KT - 1))
            s.inc(lastm, "t")
            s.wait(vec, "t")
            vec.tensor_copy(vc[:, :], pb[0][:, :])
            s.wait(ten, "v")
            for kt in range(KT):
                lastm = ten.matmul(pb[0][0:64, 0:128], wqa_s[:, kt, :], xn[:, kt, csl],
                                   start=(kt == 0), stop=(kt == KT - 1))
            s.inc(lastm, "t")
            s.wait(vec, "t")
            vec.tensor_copy(xl_[:, :], pb[0][0:64, 0:128])
            s.wait(ten, "v")
            for rt in range(4):
                lastm = ten.matmul(pb[0][:, 0:128] if rt == 0 else pb[0][:, 0:128],
                                   wqb_s[:, rt * 128:(rt + 1) * 128], xl_[:, :],
                                   start=True, stop=True, skip_group_check=True)
                s.inc(lastm, "t")
                s.wait(vec, "t")
                vec.tensor_scalar(q2c[:, rt, :], pb[0][:, 0:128], 0.125, None,
                                  op0=OP.mult)
                s.wait(ten, "v")
            s.wait(ten, "v")
            for kt in range(KT):
                lastm = ten.matmul(pb[0][0:64, 0:128], wka_s[:, kt, :], xn[:, kt, csl],
                                   start=(kt == 0), stop=(kt == KT - 1))
            s.inc(lastm, "t")
            s.wait(vec, "t")
            vec.tensor_copy(xl_[:, :], pb[0][0:64, 0:128])
            s.wait(ten, "v")
            for rt in range(4):
                lastm = ten.matmul(pb[0][:, 0:128], wkb_s[:, rt * 128:(rt + 1) * 128],
                                   xl_[:, :], start=True, stop=True,
                                   skip_group_check=True)
                s.inc(lastm, "t")
                s.wait(act, "t")
                s.wait(act, "v")
                s.wait(act, "st")
                act.activation(t1[:, 0:128], pb[0][:, 0:128], AF.Exp)
                s.wait(vec, "a")
                vec.tensor_copy(tb[:, 0:128], t1[:, 0:128])
                s.wait(ten, "v")
                lastm = ten.matmul(pb[5][0:2, 0:128], blk2[:, :], tb[:, 0:128],
                                   start=True, stop=True, skip_group_check=True)
                s.inc(lastm, "t")
                s.wait(vec, "t")
                vec.reciprocal(t2[0:2, 0:128], pb[5][0:2, 0:128])
                vec.tensor_copy(xsq[0:2, 0:128], t2[0:2, 0:128])
                s.wait(ten, "v")
                lastm = ten.matmul(pb[5][:, 0:128], blk2T[:, :], xsq[0:2, 0:128],
                                   start=True, stop=True, skip_group_check=True)
                s.inc(lastm, "t")
                s.wait(vec, "t")
                vec.tensor_mul(k2c[:, rt, :], t1[:, 0:128], pb[5][:, 0:128])
                s.wait(ten, "v")
                s.wait(act, "v")
            for (ga_s, gb_s, bgs, dst) in ((wg1a_s, wg1b_s, bg1_s, gk1c),
                                           (wg2a_s, wg2b_s, bg2_s, gk2c)):
                s.wait(ten, "v")
                s.wait(ten, "a")
                for kt in range(KT):
                    lastm = ten.matmul(pb[0][0:16, 0:128], ga_s[:, kt, :],
                                       xn[:, kt, csl], start=(kt == 0),
                                       stop=(kt == KT - 1))
                s.inc(lastm, "t")
                s.wait(vec, "t")
                vec.tensor_copy(xl_[0:16, :], pb[0][0:16, 0:128])
                s.wait(ten, "v")
                for rt in range(4):
                    lastm = ten.matmul(pb[1][:, rt * 128:(rt + 1) * 128],
                                       gb_s[:, rt * 128:(rt + 1) * 128],
                                       xl_[0:16, :], start=(rt == 0), stop=(rt == 3),
                                       skip_group_check=True)
                    s.inc(lastm, "t")
                    s.wait(ten, "t")
                s.wait(vec, "t")
                logsig16(dst, [pb[1][:, rt * 128:(rt + 1) * 128] for rt in range(4)],
                         bgs)
            s.wait(ten, "v")
            s.wait(ten, "a")
            for kt in range(KT):
                lastm = ten.matmul(pb[0][0:64, 0:128], wga_s[:, kt, :], xn[:, kt, csl],
                                   start=(kt == 0), stop=(kt == KT - 1))
            s.inc(lastm, "t")
            s.wait(vec, "t")
            vec.tensor_copy(xl_[:, :], pb[0][0:64, 0:128])
            s.wait(ten, "v")
            lastm = ten.matmul(pb[0][:, :], xl_[:, :], wgb_s[:, :],
                               start=True, stop=True, skip_group_check=True)
            s.inc(lastm, "t")
            s.wait(act, "t")
            s.wait(act, "v")
            act.activation(glc[:, :], pb[0][:, :], AF.Silu)
            s.wait(ten, "a")
            s.wait(ten, "v")
            for kt in range(KT):
                lastm = ten.matmul(pb[0][:, 0:4], xn[:, kt, csl], we_s[:, kt, :],
                                   start=(kt == 0), stop=(kt == KT - 1))
            s.inc(lastm, "t")
            s.wait(act, "t")
            act.activation(t1[:, 0:4], pb[0][:, 0:4], AF.Exp)
            s.wait(vec, "a")
            vec.reduce_sum(t1[:, 4:5], t1[:, 0:4], axis=AX.X)
            vec.reciprocal(t1[:, 5:6], t1[:, 4:5])
            vec.tensor_scalar(t1[:, 0:4], t1[:, 0:4], t1[:, 5:6], None, op0=OP.mult)
            vec.reduce_max(t1[:, 6:7], t1[:, 0:4], axis=AX.X)
            vec.tensor_scalar(rwt[:, 4:8], t1[:, 0:4], t1[:, 6:7], None, op0=OP.is_ge)
            vec.tensor_mul(rwt[:, 0:4], t1[:, 0:4], rwt[:, 4:8])

            for inst in range(5):
                if inst == 0:
                    qsrc, ksrc, gsrc = q1c, k1c, gk1c
                else:
                    e = inst - 1
                    s.wait(ten, "v")
                    lastm = ten.transpose(pb[5][0:1, 0:128], rwt[:, e:e + 1],
                                          ident[:, :])
                    s.inc(lastm, "t")
                    s.wait(vec, "t")
                    vec.tensor_copy(t2[0:1, 0:128], pb[5][0:1, 0:128])
                    s.wait(ten, "v")
                    lastm = ten.transpose(pb[5][0:1, 128:256], rwt[:, 4 + e:5 + e],
                                          ident[:, :])
                    s.inc(lastm, "t")
                    s.wait(vec, "t")
                    vec.tensor_copy(t2[0:1, 128:256], pb[5][0:1, 128:256])
                    s.wait(ten, "v")
                    lastm = ten.matmul(pb[5][:, 0:128], ones_f[:, :],
                                       t2[0:1, 0:128], start=True, stop=False,
                                       skip_group_check=True)
                    s.inc(lastm, "t")
                    s.wait(ten, "t")
                    lastm = ten.matmul(pb[5][:, 128:256], ones_f[:, :],
                                       t2[0:1, 128:256], start=False, stop=True,
                                       skip_group_check=True)
                    s.inc(lastm, "t")
                    s.wait(ten, "t")
                    s.wait(vec, "t")
                    vec.tensor_copy(facb[:, :], pb[5][:, 0:128])
                    vec.tensor_copy(mkb[:, :], pb[5][:, 128:256])
                    s.wait(ten, "v")
                    qsrc, ksrc, gsrc = q2c, k2c, None
                if inst == 0:
                    gin = gk1c
                else:
                    vec.tensor_mul(gke[:, :, :], gk2c[:, :, :],
                                   mkb[:, :].unsqueeze(1).to_broadcast([128, 4, 128]))
                    gin = gke
                for rt in range(4):
                    vec.tensor_tensor_scan(gke[:, rt, :] if inst == 0 else egv[:, rt, :],
                                           zer[:, :], gin[:, rt, :], 0.0,
                                           op0=OP.add, op1=OP.add)
                gcs = gke if inst == 0 else egv
                for rt in range(4):
                    s.wait(act, "v")
                    act.activation(env[:, rt, :], gcs[:, rt, :], AF.Exp, scale=-1.0)
                    act.activation(dec[:, rt:rt + 1], gcs[:, rt, 127:128], AF.Exp)
                    s.wait(vec, "a")
                    vec.tensor_scalar(elv[:, rt, :], gcs[:, rt, :],
                                      gcs[:, rt, 127:128], None, op0=OP.subtract)
                    s.wait(act, "v")
                    act.activation(elv[:, rt, :], elv[:, rt, :], AF.Exp, scale=-1.0)
                    act.activation(egv[:, rt, :] if inst == 0 else t1[:, 0:128],
                                   gcs[:, rt, :], AF.Exp)
                    if inst != 0:
                        s.wait(vec, "a")
                        vec.tensor_copy(gke[:, rt, :], t1[:, 0:128])
                ege = egv if inst == 0 else gke
                s.wait(vec, "t")
                s.wait(vec, "a")
                vec.tensor_copy(Sbf[:, :, :], S_[inst][:, :, :])
                s.wait(vec, "a")
                if inst == 0:
                    vec.tensor_mul(qt_[:, :, :], qsrc[:, :, :], ege[:, :, :])
                    vec.tensor_mul(kg_[:, :, :], ksrc[:, :, :], env[:, :, :])
                    vec.tensor_mul(ko_[:, :, :], ksrc[:, :, :], elv[:, :, :])
                else:
                    fb = facb[:, :].unsqueeze(1).to_broadcast([128, 4, 128])
                    vec.tensor_mul(qt_[:, :, :], qsrc[:, :, :], ege[:, :, :])
                    vec.tensor_mul(qt_[:, :, :], qt_[:, :, :], fb)
                    vec.tensor_mul(kg_[:, :, :], ksrc[:, :, :], env[:, :, :])
                    vec.tensor_mul(kg_[:, :, :], kg_[:, :, :], fb)
                    vec.tensor_mul(ko_[:, :, :], ksrc[:, :, :], elv[:, :, :])
                    vec.tensor_mul(ko_[:, :, :], ko_[:, :, :], fb)
                s.wait(ten, "v")
                for h in range(H):
                    rt, rr = h // 2, 64 * (h % 2)
                    bank = pb[1] if h < 4 else pb[2]
                    lastm = ten.matmul(bank[:, (h % 4) * 128:(h % 4 + 1) * 128],
                                       kg_[rr:rr + 64, rt, :], qt_[rr:rr + 64, rt, :],
                                       start=(h % 4 == 0), stop=(h % 4 == 3),
                                       skip_group_check=True)
                    s.inc(lastm, "t")
                    s.wait(ten, "t")
                s.wait(vec, "t")
                vec.tensor_mul(pa_[:, 0:4, :], pb[1][:, :].rearrange("p (h t) -> p h t", h=4),
                               tril[:, :].unsqueeze(1).to_broadcast([128, 4, 128]))
                vec.tensor_mul(pa_[:, 4:8, :], pb[2][:, :].rearrange("p (h t) -> p h t", h=4),
                               tril[:, :].unsqueeze(1).to_broadcast([128, 4, 128]))
                for rt in range(4):
                    s.wait(ten, "v")
                    lastm = ten.transpose(pbt[:, rt * 128:(rt + 1) * 128],
                                          ko_[:, rt, :], identb[:, :])
                    s.inc(lastm, "t")
                    s.wait(vec, "t")
                    vec.tensor_copy(kot[:, rt * 128:(rt + 1) * 128],
                                    pbt[:, rt * 128:(rt + 1) * 128])
                s.wait(ten, "v")
                for h in range(H):
                    lastm = ten.matmul(pb[3][:, h * 64:(h + 1) * 64],
                                       pa_[:, h, :], vc[:, h * 64:(h + 1) * 64],
                                       start=(inst == 0 and h == 0), stop=False,
                                       skip_group_check=True)
                    s.inc(lastm, "t")
                    s.wait(ten, "t")
                for h in range(H):
                    rt, rr = h // 2, 64 * (h % 2)
                    lastm = ten.matmul(pb[3][:, h * 64:(h + 1) * 64],
                                       qt_[rr:rr + 64, rt, :], Sbf[rr:rr + 64, rt, :],
                                       start=False,
                                       stop=(inst == 4 and h == H - 1),
                                       skip_group_check=True)
                    s.inc(lastm, "t")
                    s.wait(ten, "t")
                s.wait(ten, "v")
                for h in range(H):
                    rt, rr = h // 2, 64 * (h % 2)
                    lastm = ten.matmul(pb[4][rr:rr + 64, rt * 64:(rt + 1) * 64],
                                       kot[:, h * 64:(h + 1) * 64],
                                       vc[:, h * 64:(h + 1) * 64],
                                       start=(h < 2), stop=(h >= H - 2),
                                       skip_group_check=True)
                    s.inc(lastm, "t")
                    s.wait(ten, "t")
                s.wait(vec, "t")
                for rt in range(4):
                    vec.scalar_tensor_tensor(
                        S_[inst][:, rt, :], S_[inst][:, rt, :],
                        dec[:, rt:rt + 1], pb[4][:, rt * 64:(rt + 1) * 64],
                        op0=OP.mult, op1=OP.add)
                s.wait(ten, "v")

            s.wait(vec, "t")
            vec.tensor_copy(t1[:, :], pb[3][:, :])
            vec.tensor_mul(t2[:, :], t1[:, :], t1[:, :])
            vec.reduce_sum(t2[:, 0:8],
                           t2[:, :].rearrange("p (h d) -> p h d", h=8), axis=AX.X)
            vec.tensor_scalar(t2[:, 0:8], t2[:, 0:8], 1.0 / 64, None, op0=OP.mult)
            vec.tensor_scalar(t2[:, 0:8], t2[:, 0:8], EPS, None, op0=OP.add)
            s.wait(act, "v")
            act.activation(t2[:, 8:16], t2[:, 0:8], AF.Ln)
            act.activation(t2[:, 8:16], t2[:, 8:16], AF.Exp, scale=-0.5)
            s.wait(vec, "a")
            vec.tensor_mul(t1[:, :].rearrange("p (h d) -> p h d", h=8),
                           t1[:, :].rearrange("p (h d) -> p h d", h=8),
                           t2[:, 8:16].rearrange("p (h o) -> p h o", h=8)
                           .to_broadcast([128, 8, 64]))
            vec.tensor_mul(t1[:, :], t1[:, :], nwb0[:, :])
            vec.tensor_mul(tb[:, :], t1[:, :], glc[:, :])
            for rt in range(4):
                s.wait(ten, "v")
                lastm = ten.transpose(pbt[:, rt * 128:(rt + 1) * 128],
                                      tb[:, rt * 128:(rt + 1) * 128], identb[:, :])
                s.inc(lastm, "t")
                s.wait(vec, "t")
                vec.tensor_copy(ofm[:, rt, :], pbt[:, rt * 128:(rt + 1) * 128])
            for ot in range(KT):
                s.wait(ten, "v")
                for rt in range(4):
                    lastm = ten.matmul(pb[0][:, 0:128],
                                       wo_s[:, rt, ot * 128:(ot + 1) * 128],
                                       ofm[:, rt, :], start=(rt == 0),
                                       stop=(rt == 3), skip_group_check=True)
                s.inc(lastm, "t")
                s.wait(sync, "v")
                xeng, xch_ = (gps, "xg") if io["xfsrc"].bf16 else (sync, "xd")
                s.wait(xeng, "v")
                last = xeng.dma_start(out=xr[:, :],
                                      in_=io["xfsrc"].ap(slice(ot * 128, (ot + 1) * 128), csl))
                s.inc(last, xch_, 16)
                s.wait(vec, xch_)
                s.wait(vec, "t")
                s.wait(vec, "st")
                vec.tensor_add(t1[:, 0:128], pb[0][:, 0:128], xr[:, :])
                s.wait(sync, "v")
                last = sync.dma_start(out=y[ot * 128:(ot + 1) * 128, csl],
                                      in_=t1[:, 0:128])
                s.inc(last, "st", 16)
        s.wait(sync, "st")
        s.wait(gps, "st")


def tile_wT(wT, IT_out=None):
    ind, outd = wT.shape
    t = wT.reshape(ind // 128, 128, outd // 128, 128)
    return np.ascontiguousarray(t.transpose(2, 0, 1, 3))



def host_moe_inputs(inp, prefix="L0_ffn"):
    Wg_s = inp[f"{prefix}_Wg_s"]; Wu_s = inp[f"{prefix}_Wu_s"]; Wd_s = inp[f"{prefix}_Wd_s"]
    Wg = inp[f"{prefix}_Wg"]; Wu = inp[f"{prefix}_Wu"]; Wd = inp[f"{prefix}_Wd"]
    Wr = inp[f"{prefix}_Wr"]; rb = inp[f"{prefix}_rb"]
    wg = np.stack([tile_wT(np.ascontiguousarray(w.T))
                   for w in [Wg_s] + list(Wg)]).astype(np.float32)
    wu = np.stack([tile_wT(np.ascontiguousarray(w.T))
                   for w in [Wu_s] + list(Wu)]).astype(np.float32)
    wdr = np.stack([tile_wT(np.ascontiguousarray(w.T))
                    for w in [Wd_s] + list(Wd)]).astype(np.float32)
    return {
        "wg": wg, "wu": wu, "wdr": wdr,
        "wr": np.ascontiguousarray(Wr.T).astype(np.float32),
        "rbv": np.asarray(rb).reshape(1, 8).astype(np.float32),
        "identf": np.eye(128, dtype=np.float32),
    }



def host_attn_a_inputs(inp):
    PHI = (1.0 + np.sqrt(5.0)) / 2.0
    S = 4096
    pos = (np.arange(S, dtype=np.float64) * PHI) % 1.0
    dist = np.abs(pos[:, None] - pos[None, :])
    exb_full = (np.exp(-dist) + 1e-8).astype(np.float32)
    import ml_dtypes
    return {
        "wqkv": np.ascontiguousarray(inp["L0_attn_a_w_qkv"]).astype(np.float32),
        "wproj": np.ascontiguousarray(inp["L0_attn_a_w_proj"]).astype(np.float32),
        "bproj": np.asarray(inp["L0_attn_a_b_proj"]).reshape(D, 1).astype(np.float32),
    }, exb_full




def build_fused(nstages=8):
    pass_lite = (nstages == 80)
    p = P("fused")
    nc = p.nc
    with p.ctx:
        E, IT = 9, 16
        dt = nc.dram_tensor
        x_own = dt("x_own", [D, 2048], F32, kind="ExternalInput").ap()
        a0_wqkv = dt("a0_wqkv", [D, 3072], F32, kind="ExternalInput").ap()
        a0_wproj = dt("a0_wproj", [D, D], F32, kind="ExternalInput").ap()
        a0_bproj = dt("a0_bproj", [D, 1], F32, kind="ExternalInput").ap()
        a0_exb = dt("a0_exb", [4096, 2048], BF16, kind="ExternalInput").ap()
        m_wg = dt("m_wg", [E, IT, KT, 128, 128], F32, kind="ExternalInput").ap()
        m_wu = dt("m_wu", [E, IT, KT, 128, 128], F32, kind="ExternalInput").ap()
        m_wdr = dt("m_wdr", [E, KT, IT, 128, 128], F32, kind="ExternalInput").ap()
        m_wr = dt("m_wr", [D, 8], F32, kind="ExternalInput").ap()
        m_rbv = dt("m_rbv", [1, 8], F32, kind="ExternalInput").ap()
        identf = dt("identf", [128, 128], F32, kind="ExternalInput").ap()
        blk2Tc = dt("blk2Tc", [2, 128], F32, kind="ExternalInput").ap()
        trilm = dt("trilm", [128, 128], BF16, kind="ExternalInput").ap()
        b0_names = ["wqw", "wkw", "wvw", "wow"]
        b0w = {n: dt("b0_" + n, [D, D], F32, kind="ExternalInput").ap() for n in b0_names}
        b0_wqb = dt("b0_wqb", [1, D], F32, kind="ExternalInput").ap()
        b0_wkb = dt("b0_wkb", [1, D], F32, kind="ExternalInput").ap()
        b0_wvb = dt("b0_wvb", [1, D], F32, kind="ExternalInput").ap()
        b0_wob = dt("b0_wob", [D, 1], F32, kind="ExternalInput").ap()
        b0_wqiw = dt("b0_wqiw", [D, 64], F32, kind="ExternalInput").ap()
        b0_wkiw = dt("b0_wkiw", [D, 64], F32, kind="ExternalInput").ap()
        b0_wqib = dt("b0_wqib", [64, 1], F32, kind="ExternalInput").ap()
        b0_wkib = dt("b0_wkib", [64, 1], F32, kind="ExternalInput").ap()
        a1_wq = dt("a1_wq", [D, 64], F32, kind="ExternalInput").ap()
        a1_wk = dt("a1_wk", [D, 64], F32, kind="ExternalInput").ap()
        a1_wv = dt("a1_wv", [D, D], F32, kind="ExternalInput").ap()
        a1_cm = dt("a1_cm", [4096, 2048], BF16, kind="ExternalInput").ap()
        b1_wkT = dt("b1_wkT", [D, D], F32, kind="ExternalInput").ap()
        b1_wqT = dt("b1_wqT", [D, D], F32, kind="ExternalInput").ap()
        b1_wvT = dt("b1_wvT", [D, D], F32, kind="ExternalInput").ap()
        b1_woT = dt("b1_woT", [D, D], F32, kind="ExternalInput").ap()
        b1_bo = dt("b1_bo", [D, 1], F32, kind="ExternalInput").ap()
        b1_klng2 = dt("b1_klng2", [128, 1], F32, kind="ExternalInput").ap()
        b1_klnb2 = dt("b1_klnb2", [128, 1], F32, kind="ExternalInput").ap()
        b1_qlng2 = dt("b1_qlng2", [128, 1], F32, kind="ExternalInput").ap()
        b1_qlnb2 = dt("b1_qlnb2", [128, 1], F32, kind="ExternalInput").ap()
        b1_cm = dt("b1_cm", [4096, 4096], BF16, kind="ExternalInput").ap()
        g_wq = dt("g_wq", [D, 512], F32, kind="ExternalInput").ap()
        g_wk = dt("g_wk", [D, 512], F32, kind="ExternalInput").ap()
        g_wv = dt("g_wv", [D, 512], F32, kind="ExternalInput").ap()
        g_wqa = dt("g_wqa", [D, 64], F32, kind="ExternalInput").ap()
        g_wka = dt("g_wka", [D, 64], F32, kind="ExternalInput").ap()
        g_wga = dt("g_wga", [D, 64], F32, kind="ExternalInput").ap()
        g_wqb = dt("g_wqb", [64, 512], F32, kind="ExternalInput").ap()
        g_wkb = dt("g_wkb", [64, 512], F32, kind="ExternalInput").ap()
        g_wgb = dt("g_wgb", [64, 512], F32, kind="ExternalInput").ap()
        g_wg1a = dt("g_wg1a", [D, 16], F32, kind="ExternalInput").ap()
        g_wg2a = dt("g_wg2a", [D, 16], F32, kind="ExternalInput").ap()
        g_wg1b = dt("g_wg1b", [16, 512], F32, kind="ExternalInput").ap()
        g_wg2b = dt("g_wg2b", [16, 512], F32, kind="ExternalInput").ap()
        g_bg1r = dt("g_bg1r", [512, 1], F32, kind="ExternalInput").ap()
        g_bg2r = dt("g_bg2r", [512, 1], F32, kind="ExternalInput").ap()
        g_we = dt("g_we", [D, 4], F32, kind="ExternalInput").ap()
        g_normwb = dt("g_normwb", [1, 512], F32, kind="ExternalInput").ap()
        g_wo = dt("g_wo", [512, D], F32, kind="ExternalInput").ap()
        yout = dt("yout", [D, 4096], F32, kind="ExternalOutput").ap()
        xs = [dt(f"xs{i}", [D, 2048], F32).ap() for i in range(6)]
        y6 = dt("y6", [D, 4096], F32).ap()
        y7 = dt("y7", [D, 4096], F32).ap()
        b0_odram = dt("b0_odram", [D, 2048], BF16).ap()
        b1_odram = dt("b1_odram", [D, 4096], BF16).ap()
        agin = [dt(f"agin{i}", [D, 2048], BF16).ap() for i in range(4)]
        agout = [dt(f"agout{i}", [2 * D, 2048], BF16).ap() for i in range(4)]

        s = p.engines(["xd", "xg", "wl0", "wl1", "wd0", "wd1", "wr0", "eb",
                       "st", "ld1", "cc"])

        emit_ag(p, s, x_own, agin[0], agout[0])
        stage_barrier(p, s)
        io = {"xf": agout[0], "xq": x_own, "xfsrc": AgSrc(agout[0], 2048),
              "wqkv": a0_wqkv, "wproj": a0_wproj, "bproj": a0_bproj,
              "exb": a0_exb, "y": xs[1]}
        with nc.named_scope("st1_a0"):
            emit_a0(p, s, io, "a0_")
        stage_barrier(p, s)
        if nstages < 2:
            s.wait(p.sync, "st"); s.wait(p.gpsd, "st"); return nc
        io = {"x": xs[1], "wg": m_wg, "wu": m_wu, "wdr": m_wdr, "wr": m_wr,
              "rbv": m_rbv, "identf": identf, "y": xs[2]}
        with nc.named_scope("st2_moe"):
            emit_moe(p, s, io, "m1_")
        stage_barrier(p, s)
        if nstages < 3:
            s.wait(p.sync, "st"); s.wait(p.gpsd, "st"); return nc
        emit_ag(p, s, xs[2], agin[1], agout[1])
        stage_barrier(p, s)
        io = {"xf": agout[1], "xq": xs[2], "xfsrc": AgSrc(agout[1], 2048),
              "odram": b0_odram, "identf": identf, "y": xs[3],
              "wqb": b0_wqb, "wkb": b0_wkb, "wvb": b0_wvb, "wob": b0_wob,
              "wqiw": b0_wqiw, "wkiw": b0_wkiw, "wqib": b0_wqib,
              "wkib": b0_wkib}
    
        io.update(b0w)
        with nc.named_scope("st3_b0"):
            emit_b0(p, s, io, "b0_")
        stage_barrier(p, s)
        if nstages < 4:
            s.wait(p.sync, "st"); s.wait(p.gpsd, "st"); return nc
        io = {"x": xs[3], "wg": m_wg, "wu": m_wu, "wdr": m_wdr, "wr": m_wr,
              "rbv": m_rbv, "identf": identf, "y": xs[4]}
        with nc.named_scope("st4_moe"):
            emit_moe(p, s, io, "m2_")
        stage_barrier(p, s)
        if nstages < 5:
            s.wait(p.sync, "st"); s.wait(p.gpsd, "st"); return nc
        emit_ag(p, s, xs[4], agin[2], agout[2])
        stage_barrier(p, s)
        io = {"xf": agout[2], "xq": xs[4], "xfsrc": AgSrc(agout[2], 2048),
              "wq": a1_wq, "wk": a1_wk, "wv": a1_wv, "cm": a1_cm, "y": xs[5]}
        with nc.named_scope("st5_a1"):
            emit_a1(p, s, io, "a1_")
        stage_barrier(p, s)
        if nstages < 6:
            s.wait(p.sync, "st"); s.wait(p.gpsd, "st"); return nc
        emit_ag(p, s, xs[5], agin[3], agout[3])
        stage_barrier(p, s)
        io = {"xf": agout[3], "xfsrc": AgSrc(agout[3], 2048),
              "wq": g_wq, "wk": g_wk, "wv": g_wv, "wqa": g_wqa, "wka": g_wka,
              "wga": g_wga, "wqb": g_wqb, "wkb": g_wkb, "wgb": g_wgb,
              "wg1a": g_wg1a, "wg2a": g_wg2a, "wg1b": g_wg1b, "wg2b": g_wg2b,
              "bg1r": g_bg1r, "bg2r": g_bg2r, "we": g_we, "normwb": g_normwb,
              "wo": g_wo, "trilm": trilm, "identf": identf, "blk2Tc": blk2Tc,
              "y": y6}
    
        with nc.named_scope("st6_gla"):
            emit_gla(p, s, io, "g1_")
        stage_barrier(p, s)
        if nstages < 7:
            s.wait(p.sync, "st"); s.wait(p.gpsd, "st"); return nc
        io = {"xf": y6, "xq": y6, "xfsrc": FlatSrc(y6),
              "wkT": b1_wkT, "wqT": b1_wqT, "wvT": b1_wvT, "woT": b1_woT,
              "bo": b1_bo, "klng2": b1_klng2, "klnb2": b1_klnb2,
              "qlng2": b1_qlng2, "qlnb2": b1_qlnb2, "cm": b1_cm,
              "blk2Tc": blk2Tc, "odram": b1_odram, "y": y7}
        with nc.named_scope("st7_b1"):
            emit_b1(p, s, io, "b1_")
        stage_barrier(p, s)
        if nstages < 8 and nstages != 81:
            s.wait(p.sync, "st"); s.wait(p.gpsd, "st"); return nc
        if nstages == 80:
            lw = p.gpsd.dma_start(out=yout, in_=y7)
            s.inc(lw, "st", 16)
            s.wait(p.sync, "st"); s.wait(p.gpsd, "st"); return nc
        y8 = dt("y8", [D, 4096], F32).ap()
        io = {"xf": y7, "xfsrc": FlatSrc(y7),
              "wq": g_wq, "wk": g_wk, "wv": g_wv, "wqa": g_wqa, "wka": g_wka,
              "wga": g_wga, "wqb": g_wqb, "wkb": g_wkb, "wgb": g_wgb,
              "wg1a": g_wg1a, "wg2a": g_wg2a, "wg1b": g_wg1b, "wg2b": g_wg2b,
              "bg1r": g_bg1r, "bg2r": g_bg2r, "we": g_we, "normwb": g_normwb,
              "wo": g_wo, "trilm": trilm, "identf": identf, "blk2Tc": blk2Tc,
              "y": y8}
        with nc.named_scope("st8_gla"):
            emit_gla(p, s, io, "g2_")
        stage_barrier(p, s)
        lw = p.gpsd.dma_start(out=yout, in_=y8)
        s.inc(lw, "st", 16)
        s.wait(p.sync, "st")
        s.wait(p.gpsd, "st")
    return nc


import ml_dtypes

_FCACHE = {}


def _fprog():
    if "nc" not in _FCACHE:
        _FCACHE["nc"] = build_fused(nstages=8)
    return _FCACHE["nc"]


def kernel(**inp):
    inp = {k: np.asarray(v) for k, v in inp.items()}
    x0 = inp["L0_attn_a_x"].astype(np.float32)
    B, Sn, Dm = x0.shape
    Q = 2048
    wa0 = {
        "a0_wqkv": np.ascontiguousarray(inp["L0_attn_a_w_qkv"]).astype(np.float32),
        "a0_wproj": np.ascontiguousarray(inp["L0_attn_a_w_proj"]).astype(np.float32),
        "a0_bproj": inp["L0_attn_a_b_proj"].reshape(D, 1).astype(np.float32),
    }
    PHI = (1.0 + np.sqrt(5.0)) / 2.0
    pos = (np.arange(Sn, dtype=np.float64) * PHI) % 1.0
    dist = np.abs(pos[:, None] - pos[None, :])
    exbf = (np.exp(-dist) + 1e-8).astype(np.float32)
    wmoe = host_moe_inputs(inp, "L0_ffn")
    ident = np.eye(128, dtype=np.float32)
    blk2Tc = np.kron(np.eye(2), np.ones((1, 64))).astype(np.float32)
    trilc = (np.arange(128)[None, :] >= np.arange(128)[:, None]).astype(ml_dtypes.bfloat16)
    cm_full = (np.arange(Sn)[:, None] <= np.arange(Sn)[None, :])
    wshared = {
        "m_wg": wmoe["wg"], "m_wu": wmoe["wu"], "m_wdr": wmoe["wdr"],
        "m_wr": wmoe["wr"], "m_rbv": wmoe["rbv"],
        "identf": ident, "blk2Tc": blk2Tc, "trilm": trilc,
        "b0_wqw": np.ascontiguousarray(inp["L0_attn_b_wq_w"]).astype(np.float32),
        "b0_wkw": np.ascontiguousarray(inp["L0_attn_b_wk_w"]).astype(np.float32),
        "b0_wvw": np.ascontiguousarray(inp["L0_attn_b_wv_w"]).astype(np.float32),
        "b0_wow": np.ascontiguousarray(inp["L0_attn_b_wo_w"]).astype(np.float32),
        "b0_wqb": inp["L0_attn_b_wq_b"].reshape(1, D).astype(np.float32),
        "b0_wkb": inp["L0_attn_b_wk_b"].reshape(1, D).astype(np.float32),
        "b0_wvb": inp["L0_attn_b_wv_b"].reshape(1, D).astype(np.float32),
        "b0_wob": inp["L0_attn_b_wo_b"].reshape(D, 1).astype(np.float32),
        "b0_wqiw": np.ascontiguousarray(inp["L0_attn_b_wqi_w"]).astype(np.float32),
        "b0_wkiw": np.ascontiguousarray(inp["L0_attn_b_wki_w"]).astype(np.float32),
        "b0_wqib": inp["L0_attn_b_wqi_b"].reshape(64, 1).astype(np.float32),
        "b0_wkib": inp["L0_attn_b_wki_b"].reshape(64, 1).astype(np.float32),
        "a1_wq": np.ascontiguousarray(inp["L1_attn_a_Wq"]).astype(np.float32),
        "a1_wk": np.ascontiguousarray(inp["L1_attn_a_Wk"]).astype(np.float32),
        "a1_wv": np.ascontiguousarray(inp["L1_attn_a_Wv"]).astype(np.float32),
        "b1_wkT": np.ascontiguousarray(inp["L1_attn_b_Wk"].T).astype(np.float32),
        "b1_wqT": np.ascontiguousarray(inp["L1_attn_b_Wq"].T).astype(np.float32),
        "b1_wvT": np.ascontiguousarray(inp["L1_attn_b_Wv"].T).astype(np.float32),
        "b1_woT": np.ascontiguousarray(inp["L1_attn_b_Wo"].T).astype(np.float32),
        "b1_bo": inp["L1_attn_b_bo"].reshape(D, 1).astype(np.float32),
        "b1_klng2": np.tile(inp["L1_attn_b_kln_g"].astype(np.float32), 2).reshape(128, 1),
        "b1_klnb2": np.tile(inp["L1_attn_b_kln_b"].astype(np.float32), 2).reshape(128, 1),
        "b1_qlng2": np.tile(inp["L1_attn_b_qln_g"].astype(np.float32), 2).reshape(128, 1),
        "b1_qlnb2": np.tile(inp["L1_attn_b_qln_b"].astype(np.float32), 2).reshape(128, 1),
        "b1_cm": np.ascontiguousarray(cm_full).astype(ml_dtypes.bfloat16),
        "g_wq": np.ascontiguousarray(inp["L1_ffn_Wq"]).astype(np.float32),
        "g_wk": np.ascontiguousarray(inp["L1_ffn_Wk"]).astype(np.float32),
        "g_wv": np.ascontiguousarray(inp["L1_ffn_Wv"]).astype(np.float32),
        "g_wqa": np.ascontiguousarray(inp["L1_ffn_Wqa"]).astype(np.float32),
        "g_wka": np.ascontiguousarray(inp["L1_ffn_Wka"]).astype(np.float32),
        "g_wga": np.ascontiguousarray(inp["L1_ffn_Wga"]).astype(np.float32),
        "g_wqb": np.ascontiguousarray(inp["L1_ffn_Wqb"]).astype(np.float32),
        "g_wkb": np.ascontiguousarray(inp["L1_ffn_Wkb"]).astype(np.float32),
        "g_wgb": np.ascontiguousarray(inp["L1_ffn_Wgb"]).astype(np.float32),
        "g_wg1a": np.ascontiguousarray(inp["L1_ffn_Wg1a"]).astype(np.float32),
        "g_wg2a": np.ascontiguousarray(inp["L1_ffn_Wg2a"]).astype(np.float32),
        "g_wg1b": np.ascontiguousarray(inp["L1_ffn_Wg1b"]).astype(np.float32),
        "g_wg2b": np.ascontiguousarray(inp["L1_ffn_Wg2b"]).astype(np.float32),
        "g_bg1r": inp["L1_ffn_bg1"].reshape(512, 1).astype(np.float32),
        "g_bg2r": inp["L1_ffn_bg2"].reshape(512, 1).astype(np.float32),
        "g_we": np.ascontiguousarray(inp["L1_ffn_We"]).astype(np.float32),
        "g_normwb": np.tile(inp["L1_ffn_norm_w"].astype(np.float32), 8).reshape(1, 512),
        "g_wo": np.ascontiguousarray(inp["L1_ffn_Wo"]).astype(np.float32),
    }
    wshared.update(wa0)
    nc = _fprog()
    im = []
    for c in range(8):
        b, half = c // 2, c % 2
        own = slice(half * Q, (half + 1) * Q)
        m = {
            "x_own": np.ascontiguousarray(x0[b, own].T),
            "a0_exb": np.ascontiguousarray(exbf[:, own]).astype(ml_dtypes.bfloat16),
            "a1_cm": np.ascontiguousarray(cm_full[:, own]).astype(ml_dtypes.bfloat16),
        }
        m.update(wshared)
        im.append(m)
    res = run_bass_kernel_spmd(nc, im, core_ids=list(range(8)))
    out = np.empty((B, Sn, Dm), np.float32)
    for b in range(B):
        out[b] = res.results[2 * b]["yout"].T
    return out


# version 06 (final)
# speedup vs baseline: 1.5730x; 1.2128x over previous
import numpy as np
from contextlib import ExitStack

import concourse.bass as bass
import concourse.mybir as mybir
from concourse.bass_utils import run_bass_kernel_spmd

F32 = mybir.dt.float32
BF16 = mybir.dt.bfloat16
AF = mybir.ActivationFunctionType
OP = mybir.AluOpType
AX = mybir.AxisListType

D = 1024
KT = 8
EPS = 1e-5

_WRAP = {
    "tensor_copy", "tensor_tensor", "tensor_add", "tensor_sub", "tensor_mul",
    "tensor_max", "tensor_scalar", "tensor_single_scalar", "tensor_scalar_add",
    "tensor_scalar_sub", "tensor_scalar_mul", "tensor_scalar_max",
    "tensor_scalar_min", "tensor_relu", "tensor_reduce", "reduce_max",
    "reduce_sum", "scalar_tensor_tensor", "tensor_tensor_scan", "memset",
    "memzero", "activation", "sign", "square", "sqrt", "mul", "copy", "add",
    "max", "max_index", "match_replace", "select", "copy_predicated",
    "reciprocal", "bn_stats", "bn_aggr", "transpose", "pool", "pool_max",
    "pool_avg", "stream_shuffle", "iota", "affine_select",
}


class Sems:
    def __init__(self, nc, ctx, names):
        self.h = {n: ctx.enter_context(nc.semaphore(n)) for n in names}
        self.c = {n: 0 for n in names}

    def inc(self, inst, name, k=1):
        self.c[name] += k
        inst.then_inc(self.h[name], k)
        return self.c[name]

    def wait(self, eng, name, val=None):
        v = self.c[name] if val is None else val
        if v > 0:
            e = eng.e if isinstance(eng, Eng) else eng
            e.wait_ge(self.h[name], v)

    def mark(self, name):
        return self.c[name]


class Eng:

    def __init__(self, eng, s, ch):
        self.e = eng
        self.s = s
        self.ch = ch

    def __getattr__(self, k):
        f = getattr(self.e, k)
        if k in _WRAP:
            s, ch, e = self.s, self.ch, self.e

            def g(*a, **kw):
                n = s.c[ch]
                if n:
                    e.wait_ge(s.h[ch], n)
                inst = f(*a, **kw)
                s.c[ch] += 1
                inst.then_inc(s.h[ch], 1)
                return inst
            return g
        return f


class P:
    def __init__(self, name):
        self.nc = bass.Bass("TRN2", target_bir_lowering=False)
        self.ctx = ExitStack()
        self.name = name

    def sb(self, name, shape, dtype):
        return self.ctx.enter_context(self.nc.sbuf_tensor(name, shape, dtype))

    def ps(self, name, shape, dtype=F32):
        return self.ctx.enter_context(self.nc.psum_tensor(name, shape, dtype))

    def engines(self, sem_names):
        nc = self.nc
        s = Sems(nc, self.ctx, ["v", "a", "g", "t"] + sem_names)
        self.s = s
        self.blk = self.ctx.enter_context(nc.Block())
        self.sync = nc.sync
        self.gpsd = nc.gpsimd
        self.ten = nc.tensor
        self.vec = Eng(nc.vector, s, "v")
        self.act = Eng(nc.scalar, s, "a")
        self.gpc = Eng(nc.gpsimd, s, "g")
        return s


def _p_sbx(self, st, name, shape, dtype):
    return st.enter_context(self.nc.sbuf_tensor(name, shape, dtype))


def _p_psx(self, st, name, shape, dtype=F32):
    return st.enter_context(self.nc.psum_tensor(name, shape, dtype))


P.sbx = _p_sbx
P.psx = _p_psx


def emit_ln(p, s, T, x_hbm, xn, xch, xbf, xsq, stats, t1, ones_bf, ones_f,
            pb_s, pb_ss, pb_mb, pb_rb, xd="xd"):
    sync, ten, vec, act = p.sync, p.ten, p.vec, p.act
    for c in range(T // 512):
        sl = slice(c * 512, (c + 1) * 512)
        s.wait(sync, "v")
        last = sync.dma_start(out=xch[:, :, :],
                              in_=x_hbm[:, sl].rearrange("(k p) t -> p k t", p=128))
        s.inc(last, xd, 16)
        s.wait(vec, xd)
        s.wait(vec, "t")
        for kt in range(KT):
            vec.tensor_copy(xbf[:, :], xch[:, kt, :])
            vec.tensor_mul(xsq[:, :], xbf[:, :], xbf[:, :])
            s.wait(ten, "v")
            ten.matmul(pb_s[0:1, :], ones_bf[:, :], xbf[:, :],
                       start=(kt == 0), stop=(kt == KT - 1))
            last = ten.matmul(pb_ss[0:1, :], ones_bf[:, :], xsq[:, :],
                              start=(kt == 0), stop=(kt == KT - 1))
            s.inc(last, "t")
            s.wait(vec, "t")
        vec.tensor_scalar(stats[0:1, 2, :], pb_s[0:1, :], 1.0 / D, None, op0=OP.mult)
        vec.tensor_scalar(stats[0:1, 1, :], pb_ss[0:1, :], 1.0 / D, None, op0=OP.mult)
        vec.tensor_mul(stats[0:1, 3, :], stats[0:1, 2, :], stats[0:1, 2, :])
        vec.tensor_sub(stats[0:1, 3, :], stats[0:1, 1, :], stats[0:1, 3, :])
        vec.tensor_scalar(stats[0:1, 3, :], stats[0:1, 3, :], EPS, None, op0=OP.add)
        s.wait(act, "v")
        act.activation(stats[0:1, 4, :], stats[0:1, 3, :], AF.Ln)
        act.activation(stats[0:1, 4, :], stats[0:1, 4, :], AF.Exp, scale=-0.5)
        s.wait(ten, "a")
        s.wait(ten, "v")
        ten.matmul(pb_mb[:, :], ones_f[:, :], stats[0:1, 2, :], start=True, stop=True)
        last = ten.matmul(pb_rb[:, :], ones_f[:, :], stats[0:1, 4, :], start=True, stop=True)
        s.inc(last, "t")
        s.wait(vec, "t")
        for kt in range(KT):
            vec.tensor_sub(t1[:, :], xch[:, kt, :], pb_mb[:, :])
            vec.tensor_mul(xn[:, kt, sl], t1[:, :], pb_rb[:, :])


class FlatSrc:
    bf16 = False

    def __init__(self, t):
        self.t = t

    def ap(self, rows, cols):
        return self.t[rows, cols]


class AgSrc:
    bf16 = True

    def __init__(self, t, half):
        self.t = t
        self.half = half

    def ap(self, rows, cols):
        blk = cols.start // self.half
        off = blk * D
        return self.t[off + rows.start:off + rows.stop,
                      cols.start - blk * self.half:cols.stop - blk * self.half]


def emit_ln2(p, s, T, xsrc, xn, xch, xbf, xsq, stats, t1, ones_bf, ones_f,
             pb_s, pb_ss, pb_mb, pb_rb):
    nc = p.nc
    sync, gps, ten, vec, act = p.sync, p.gpsd, p.ten, p.vec, p.act
    ldeng, ldch = (gps, "xg") if xsrc.bf16 else (sync, "xd")
    for c in range(T // 512):
        sl = slice(c * 512, (c + 1) * 512)
        s.wait(ldeng, "v")
        last = ldeng.dma_start(
            out=xch[:, :, :],
            in_=xsrc.ap(slice(0, D), sl).rearrange("(k p) t -> p k t", p=128))
        s.inc(last, ldch, 16)
        s.wait(vec, ldch)
        s.wait(vec, "t")
        for kt in range(KT):
            vec.tensor_copy(xbf[:, :], xch[:, kt, :])
            vec.tensor_mul(xsq[:, :], xbf[:, :], xbf[:, :])
            s.wait(ten, "v")
            ten.matmul(pb_s[0:1, :], ones_bf[:, :], xbf[:, :],
                       start=(kt == 0), stop=(kt == KT - 1))
            last = ten.matmul(pb_ss[0:1, :], ones_bf[:, :], xsq[:, :],
                              start=(kt == 0), stop=(kt == KT - 1))
            s.inc(last, "t")
            s.wait(vec, "t")
        vec.tensor_scalar(stats[0:1, 2, :], pb_s[0:1, :], 1.0 / D, None, op0=OP.mult)
        vec.tensor_scalar(stats[0:1, 1, :], pb_ss[0:1, :], 1.0 / D, None, op0=OP.mult)
        vec.tensor_mul(stats[0:1, 3, :], stats[0:1, 2, :], stats[0:1, 2, :])
        vec.tensor_sub(stats[0:1, 3, :], stats[0:1, 1, :], stats[0:1, 3, :])
        vec.tensor_scalar(stats[0:1, 3, :], stats[0:1, 3, :], EPS, None, op0=OP.add)
        s.wait(act, "v")
        act.activation(stats[0:1, 4, :], stats[0:1, 3, :], AF.Ln)
        act.activation(stats[0:1, 4, :], stats[0:1, 4, :], AF.Exp, scale=-0.5)
        s.wait(ten, "a")
        s.wait(ten, "v")
        ten.matmul(pb_mb[:, :], ones_f[:, :], stats[0:1, 2, :], start=True, stop=True)
        last = ten.matmul(pb_rb[:, :], ones_f[:, :], stats[0:1, 4, :], start=True, stop=True)
        s.inc(last, "t")
        s.wait(vec, "t")
        for kt in range(KT):
            vec.tensor_sub(t1[:, :], xch[:, kt, :], pb_mb[:, :])
            vec.tensor_mul(xn[:, kt, sl], t1[:, :], pb_rb[:, :])


def stage_barrier(p, s):
    engs = [p.sync, p.gpsd, p.ten, p.vec, p.act]
    for e in engs:
        for ch in s.c:
            s.wait(e, ch)


def emit_ag(p, s, src_f32, agin, agout):
    gps = p.gpsd
    lw = gps.dma_start(out=agin, in_=src_f32)
    s.inc(lw, "xg", 16)
    s.wait(gps, "xg")
    cc = gps.collective_compute(
        "AllGather", OP.bypass,
        replica_groups=[[0, 1], [2, 3], [4, 5], [6, 7]],
        ins=[agin.opt()], outs=[agout.opt()])
    s.inc(cc, "cc", 1)
    s.wait(gps, "cc")


def emit_moe(p, s, io, pref, T=2048, TH=1024, E=9, I=2048):
    IT = I // 128
    NH = T // TH
    nc = p.nc
    from contextlib import ExitStack as _ES0
    with _ES0() as _stage:
        x = io["x"]
        wg = io["wg"]
        wu = io["wu"]
        wdr = io["wdr"]
        wr = io["wr"]
        rbv = io["rbv"]
        identf = io["identf"]
        y = io["y"]

        xn = p.sbx(_stage, pref + "xn", [128, KT, T], BF16)
        xch = p.sbx(_stage, pref + "xch", [128, KT, 512], F32)
        xr = p.sbx(_stage, pref + "xr", [128, 512], F32)
        xbf = p.sbx(_stage, pref + "xbf", [128, 512], BF16)
        xsq = p.sbx(_stage, pref + "xsq", [128, 512], BF16)
        stats = p.sbx(_stage, pref + "stats", [1, 5, 512], F32)
        t1 = p.sbx(_stage, pref + "t1", [128, 512], F32)
        ones_bf = p.sbx(_stage, pref + "ones_bf", [128, 1], BF16)
        ones_f = p.sbx(_stage, pref + "ones_f", [1, 128], F32)
        ones_bf1 = p.sbx(_stage, pref + "ones_bf1", [1, 128], BF16)
        ident = p.sbx(_stage, pref + "ident", [128, 128], F32)
        rbt = p.sbx(_stage, pref + "rbt", [128, 8], F32)
        etmp = p.sbx(_stage, pref + "etmp", [128, 8], F32)
        probs = p.sbx(_stage, pref + "probs", [128, 8], F32)
        m8 = p.sbx(_stage, pref + "m8", [128, 8], F32)
        cwt = p.sbx(_stage, pref + "cwt", [128, 8], F32)
        cwT = p.sbx(_stage, pref + "cwT", [1, 8, T], BF16)
        cwbc = p.sbx(_stage, pref + "cwbc", [128, 512], BF16)
        wrsb = p.sbx(_stage, pref + "wrsb", [128, KT, 8], BF16)
        xe = p.sbx(_stage, pref + "xe", [128, KT, TH], BF16)
        wgsb = [p.sbx(_stage, pref + f"wgsb{i}", [128, KT, 128], BF16) for i in range(2)]
        wusb = [p.sbx(_stage, pref + f"wusb{i}", [128, KT, 128], BF16) for i in range(2)]
        wdsb = [p.sbx(_stage, pref + f"wdsb{i}", [128, IT, 128], BF16) for i in range(2)]
        sg = p.sbx(_stage, pref + "sg", [128, 2, 512], BF16)
        gu = p.sbx(_stage, pref + "gu", [128, IT, TH], BF16)
        yac = p.sbx(_stage, pref + "yac", [128, KT, TH], F32)

        pb = [p.psx(_stage, pref + f"pb{i}", [128, 512]) for i in range(6)]

        sync, gps, ten = p.sync, p.gpsd, p.ten
        vec, act = p.vec, p.act

        vec.memset(ones_bf[:, :], 1.0)
        vec.memset(ones_f[:, :], 1.0)
        vec.memset(ones_bf1[:, :], 1.0)
        last = sync.dma_start(out=ident[:, :], in_=identf)
        s.inc(last, "xd", 16)
        last = sync.dma_start(out=rbt[:, :], in_=rbv.partition_broadcast(128))
        s.inc(last, "xd", 16)
        last = gps.dma_start(out=wrsb[:, :, :],
                             in_=wr.rearrange("(k p) e -> p k e", p=128))
        s.inc(last, "wr0", 16)

        emit_ln2(p, s, T, FlatSrc(x), xn, xch, xbf, xsq, stats, t1, ones_bf, ones_f,
                pb[0][0:1, :], pb[3][0:1, :], pb[1], pb[2])

        s.wait(ten, "wr0")
        s.wait(ten, "xd")
        s.wait(ten, "v")
        for tt in range(T // 128):
            tsl = slice(tt * 128, (tt + 1) * 128)
            for kt in range(KT):
                lastm = ten.matmul(pb[3][:, 0:8], xn[:, kt, tsl], wrsb[:, kt, :],
                                   start=(kt == 0), stop=(kt == KT - 1))
            s.inc(lastm, "t")
            s.wait(vec, "t")
            vec.tensor_add(etmp[:, :], pb[3][:, 0:8], rbt[:, :])
            s.wait(act, "v")
            act.activation(etmp[:, :], etmp[:, :], AF.Exp, scale=-1.0)
            s.wait(vec, "a")
            vec.tensor_scalar(etmp[:, :], etmp[:, :], 1.0, None, op0=OP.add)
            vec.reciprocal(probs[:, :], etmp[:, :])
            vec.max(out=m8[:, :], in_=probs[:, :])
            vec.tensor_scalar(cwt[:, :], probs[:, :], m8[:, 1:2], None, op0=OP.is_ge)
            vec.tensor_mul(cwt[:, :], probs[:, :], cwt[:, :])
            s.wait(ten, "v")
            for half8 in range(2):
                for e8 in range(4):
                    ee = half8 * 4 + e8
                    last = ten.transpose(pb[4][0:1, e8 * 128:(e8 + 1) * 128],
                                         cwt[:, ee:ee + 1], ident[:, :])
                s.inc(last, "t")
                s.wait(vec, "t")
                for e8 in range(4):
                    ee = half8 * 4 + e8
                    vec.tensor_copy(cwT[0:1, ee, tsl],
                                    pb[4][0:1, e8 * 128:(e8 + 1) * 128])
                s.wait(ten, "v")

        wslot_mm = [0, 0]
        wd_slot_mm = [0, 0]
        xe_last_mm = 0
        for h in range(NH):
            s.wait(vec, "t")
            vec.memset(yac[:, :, :], 0.0)
            for e in range(E):
                if e > 0:
                    s.wait(vec, "t", xe_last_mm)
                    for c in range(TH // 512):
                        csl = slice(h * TH + c * 512, h * TH + (c + 1) * 512)
                        osl = slice(c * 512, (c + 1) * 512)
                        s.wait(ten, "v")
                        lastm = ten.matmul(pb[5][:, :], ones_bf1[:, :],
                                           cwT[0:1, e - 1, csl], start=True, stop=True)
                        s.inc(lastm, "t")
                        s.wait(vec, "t")
                        vec.tensor_copy(cwbc[:, :], pb[5][:, :])
                        for kt in range(KT):
                            vec.tensor_mul(xe[:, kt, osl], xn[:, kt, csl], cwbc[:, :])
                    s.wait(ten, "v")
                silu_mark = [0, 0]
                gu_mark = [0, 0]
                jj = 0
                for ii in range(IT):
                    sl_i = ii % 2
                    ch = f"wl{sl_i}"
                    s.wait(gps, "t", wslot_mm[sl_i])
                    lw = gps.dma_start(out=wgsb[sl_i][:, :, :],
                                       in_=wg[e, ii].rearrange("k p c -> p k c"))
                    s.inc(lw, ch, 16)
                    lw = gps.dma_start(out=wusb[sl_i][:, :, :],
                                       in_=wu[e, ii].rearrange("k p c -> p k c"))
                    s.inc(lw, ch, 16)
                    s.wait(ten, ch)
                    for c in range(TH // 512):
                        osl = slice(c * 512, (c + 1) * 512)
                        csl = slice(h * TH + c * 512, h * TH + (c + 1) * 512)
                        par = jj % 2
                        hgb = pb[0] if par == 0 else pb[1]
                        hub = pb[2] if par == 0 else pb[3]
                        s.wait(ten, "v", gu_mark[par])
                        s.wait(ten, "a", silu_mark[par])
                        for kt in range(KT):
                            ten.matmul(hgb[:, :], wgsb[sl_i][:, kt, :],
                                       xn[:, kt, csl],
                                       start=(kt == 0), stop=(kt == KT - 1))
                        for kt in range(KT):
                            rv = xn[:, kt, csl] if e == 0 else xe[:, kt, osl]
                            lastm = ten.matmul(hub[:, :], wusb[sl_i][:, kt, :], rv,
                                               start=(kt == 0), stop=(kt == KT - 1))
                        s.inc(lastm, "t")
                        s.wait(act, "t")
                        s.wait(act, "v", gu_mark[par])
                        act.activation(sg[:, par, :], hgb[:, :], AF.Silu)
                        silu_mark[par] = s.mark("a")
                        s.wait(vec, "a", silu_mark[par])
                        vec.tensor_mul(gu[:, ii, osl], sg[:, par, :], hub[:, :])
                        gu_mark[par] = s.mark("v")
                        jj += 1
                    wslot_mm[sl_i] = s.mark("t")
                xe_last_mm = s.mark("t")
                yev_mark = [0, 0]
                jjb = 0
                for oo in range(KT):
                    sl_o = oo % 2
                    ch = f"wd{sl_o}"
                    s.wait(gps, "t", wd_slot_mm[sl_o])
                    lw = gps.dma_start(out=wdsb[sl_o][:, :, :],
                                       in_=wdr[e, oo].rearrange("i p c -> p i c"))
                    s.inc(lw, ch, 16)
                    s.wait(ten, ch)
                    if oo == 0:
                        s.wait(ten, "v")
                    for c in range(TH // 512):
                        osl = slice(c * 512, (c + 1) * 512)
                        par = jjb % 2
                        yb = pb[4] if par == 0 else pb[5]
                        s.wait(ten, "v", yev_mark[par])
                        for ii in range(IT):
                            lastm = ten.matmul(
                                yb[:, :], wdsb[sl_o][:, ii, :], gu[:, ii, osl],
                                start=(ii == 0), stop=(ii == IT - 1))
                        s.inc(lastm, "t")
                        s.wait(vec, "t")
                        vec.tensor_add(yac[:, oo, osl], yac[:, oo, osl], yb[:, :])
                        yev_mark[par] = s.mark("v")
                        jjb += 1
                    wd_slot_mm[sl_o] = s.mark("t")
            for kt in range(KT):
                for c in range(TH // 512):
                    osl = slice(c * 512, (c + 1) * 512)
                    csl = slice(h * TH + c * 512, h * TH + (c + 1) * 512)
                    s.wait(sync, "v")
                    last = sync.dma_start(out=xr[:, :],
                                          in_=x[kt * 128:(kt + 1) * 128, csl])
                    s.inc(last, "xd", 16)
                    s.wait(vec, "xd")
                    s.wait(vec, "st")
                    vec.tensor_add(t1[:, :], xr[:, :], yac[:, kt, osl])
                    s.wait(sync, "v")
                    last = sync.dma_start(out=y[kt * 128:(kt + 1) * 128, csl],
                                          in_=t1[:, :])
                    s.inc(last, "st", 16)
        s.wait(sync, "st")
        s.wait(gps, "st")




def emit_a0(p, s, io, pref, S=4096, Q=2048):
    H = 16
    NKT2 = S // 128
    NQC = Q // 512
    nc = p.nc
    from contextlib import ExitStack as _ES0
    with _ES0() as _stage:
        xf = io["xf"]
        wqkv = io["wqkv"]
        wproj = io["wproj"]
        bproj = io["bproj"]
        exb = io["exb"]
        y = io["y"]

        xq = io["xq"]
        xn = p.sbx(_stage, pref + "xn", [128, KT, S], BF16)
        xnq = p.sbx(_stage, pref + "xnq", [128, KT, Q], BF16)
        xr = p.sbx(_stage, pref + "xr", [128, 512], F32)
        xbf = p.sbx(_stage, pref + "xbf", [128, 512], BF16)
        xsq = p.sbx(_stage, pref + "xsq", [128, 512], BF16)
        stats = p.sbx(_stage, pref + "stats", [1, 5, 512], F32)
        t1 = p.sbx(_stage, pref + "t1", [128, 512], F32)
        ones_bf = p.sbx(_stage, pref + "ones_bf", [128, 1], BF16)
        ones_f = p.sbx(_stage, pref + "ones_f", [1, 128], F32)
        wsb = [p.sbx(_stage, pref + f"wsb{i}", [128, KT, 128], BF16) for i in range(2)]
        kfm = p.sbx(_stage, pref + "kfm", [128, S], BF16)
        qfm = p.sbx(_stage, pref + "qfm", [128, Q], BF16)
        vtm = p.sbx(_stage, pref + "vtm", [128, NKT2, 130], BF16)
        ptile = [p.sbx(_stage, pref + f"ptile{i}", [128, 512], BF16) for i in range(2)]

        pb = [p.psx(_stage, pref + f"pb{i}", [128, 512]) for i in range(6)]

        sync, gps, ten = p.sync, p.gpsd, p.ten
        vec, act = p.vec, p.act

        vec.memset(ones_bf[:, :], 1.0)
        vec.memset(ones_f[:, :], 1.0)
        vec.memset(vtm[:, :, 64:65], 1.0)
        vec.memset(vtm[:, :, 129:130], 1.0)

        from contextlib import ExitStack as _ESL
        with _ESL() as _lnctx:
            xch = p.sbx(_lnctx, pref + "xch", [128, KT, 512], F32)
            emit_ln2(p, s, S, io["xfsrc"], xn, xch, xbf, xsq, stats, t1, ones_bf, ones_f,
                     pb[0][0:1, :], pb[3][0:1, :], pb[1], pb[2])
            emit_ln2(p, s, Q, FlatSrc(xq), xnq, xch, xbf, xsq, stats, t1, ones_bf, ones_f,
                     pb[0][0:1, :], pb[3][0:1, :], pb[1], pb[2])
        exbs = p.sbx(_stage, pref + "exbs", [128, NKT2, 512], BF16)
        ofm = p.sbx(_stage, pref + "ofm", [128, KT, Q], BF16)
        osb = p.sbx(_stage, pref + "osb", [128, 512], BF16)
        rd1 = p.sbx(_stage, pref + "rd1", [1, 512], F32)

        wslot_mm = [0, 0]

        def load_w(cols_off, slot, ch):
            s.wait(gps, "t", wslot_mm[slot])
            lw = gps.dma_start(
                out=wsb[slot][:, :, :],
                in_=wqkv[:, cols_off:cols_off + 128].rearrange(
                    "(k p) c -> p k c", p=128))
            s.inc(lw, ch, 16)

        s.wait(ten, "v")
        for hp in range(8):
            load_w(1024 + hp * 128, 0, "wl0")
            s.wait(ten, "wl0")
            s.wait(ten, "a")
            s.wait(ten, "v")
            for c in range(S // 512):
                sl = slice(c * 512, (c + 1) * 512)
                s.wait(ten, "v")
                for kt in range(KT):
                    lastm = ten.matmul(pb[4][:, :], wsb[0][:, kt, :], xn[:, kt, sl],
                                       start=(kt == 0), stop=(kt == KT - 1))
                s.inc(lastm, "t")
                s.wait(vec, "t")
                vec.tensor_copy(kfm[:, sl], pb[4][:, :])
            wslot_mm[0] = s.mark("t")
            load_w(hp * 128, 1, "wl1")
            s.wait(ten, "wl1")
            s.wait(ten, "v")
            for c in range(NQC):
                sl = slice(c * 512, (c + 1) * 512)
                s.wait(ten, "v")
                for kt in range(KT):
                    lastm = ten.matmul(pb[5][:, :], wsb[1][:, kt, :], xnq[:, kt, sl],
                                       start=(kt == 0), stop=(kt == KT - 1))
                s.inc(lastm, "t")
                s.wait(vec, "t")
                vec.tensor_copy(qfm[:, sl], pb[5][:, :])
            wslot_mm[1] = s.mark("t")
            load_w(2048 + hp * 128, 0, "wl0")
            s.wait(ten, "wl0")
            s.wait(ten, "v")
            for tt in range(NKT2):
                tsl = slice(tt * 128, (tt + 1) * 128)
                s.wait(ten, "v")
                for kt in range(KT):
                    lastm = ten.matmul(pb[4][:, 0:128], xn[:, kt, tsl],
                                       wsb[0][:, kt, :],
                                       start=(kt == 0), stop=(kt == KT - 1))
                s.inc(lastm, "t")
                s.wait(vec, "t")
                vec.tensor_copy(vtm[:, tt, 0:64], pb[4][:, 0:64])
                vec.tensor_copy(vtm[:, tt, 65:129], pb[4][:, 64:128])
            wslot_mm[0] = s.mark("t")
            s.wait(ten, "v")

            for qc in range(NQC):
                qsl = slice(qc * 512, (qc + 1) * 512)
                s.wait(sync, "v")
                last = sync.dma_start(
                    out=exbs[:, :, :],
                    in_=exb[:, qsl].rearrange("(kt p) q -> p kt q", p=128))
                s.inc(last, "eb", 16)
                s.wait(vec, "eb")
                for h2 in range(2):
                    hb = 64 * h2
                    opb = pb[2]
                    pv_done_v = [0, 0]
                    for kt2 in range(NKT2 + 1):
                        if kt2 < NKT2:
                            sb_ = kt2 % 2
                            s.wait(ten, "a", s.mark("a") if kt2 >= 2 else 0)
                            lastm = ten.matmul(
                                pb[sb_][:, :],
                                kfm[hb:hb + 64, kt2 * 128:(kt2 + 1) * 128],
                                qfm[hb:hb + 64, qsl],
                                start=True, stop=True, skip_group_check=True)
                            s.inc(lastm, "t")
                            s.wait(act, "t")
                            s.wait(act, "v", pv_done_v[sb_])
                            act.activation(ptile[sb_][:, :], pb[sb_][:, :],
                                           AF.Exp, scale=0.125)
                            s.wait(vec, "a")
                            vec.tensor_mul(ptile[sb_][:, :], ptile[sb_][:, :],
                                           exbs[:, kt2, :])
                            pv_done_v[sb_] = s.mark("v")
                        if kt2 > 0:
                            kk = kt2 - 1
                            sb_ = kk % 2
                            s.wait(ten, "v", pv_done_v[sb_])
                            lastm = ten.matmul(
                                opb[0:65, :],
                                vtm[:, kk, 65 * h2:65 * h2 + 65],
                                ptile[sb_][:, :],
                                start=(kk == 0), stop=(kk == NKT2 - 1),
                                skip_group_check=True)
                            s.inc(lastm, "t")
                    s.wait(vec, "t")
                    vec.reciprocal(rd1[0:1, :], opb[64:65, :])
                    s.wait(ten, "v")
                    lastm = ten.matmul(pb[3][0:64, :], ones_f[:, 0:64],
                                       rd1[0:1, :], start=True, stop=True)
                    s.inc(lastm, "t")
                    s.wait(vec, "t")
                    vec.tensor_copy(osb[0:64, :], opb[0:64, :])
                    h_global = hp * 2 + h2
                    otile = h_global // 2
                    orow = 64 * (h_global % 2)
                    vec.tensor_mul(ofm[orow:orow + 64, otile, qsl],
                                   osb[0:64, :], pb[3][0:64, :])
                    s.wait(ten, "v")

        bsb = p.sbx(_stage, pref + "bsb", [128, KT, 1], F32)
        last = sync.dma_start(out=bsb[:, :, :],
                              in_=bproj.rearrange("(k p) o -> p k o", p=128),
                              allow_slow_non_contiguous=True)
        s.inc(last, "xd", 16)
        wslot2 = list(wslot_mm)
        for ot in range(KT):
            slot = ot % 2
            ch = f"wl{slot}"
            s.wait(gps, "t", wslot2[slot])
            lw = gps.dma_start(
                out=wsb[slot][:, :, :],
                in_=wproj[:, ot * 128:(ot + 1) * 128].rearrange(
                    "(k p) c -> p k c", p=128))
            s.inc(lw, ch, 16)
            s.wait(ten, ch)
            for c in range(NQC):
                sl = slice(c * 512, (c + 1) * 512)
                s.wait(ten, "v")
                for kt in range(KT):
                    lastm = ten.matmul(pb[4][:, :], wsb[slot][:, kt, :],
                                       ofm[:, kt, sl], start=(kt == 0),
                                       stop=(kt == KT - 1))
                s.inc(lastm, "t")
                s.wait(sync, "v")
                last = sync.dma_start(out=xr[:, :],
                                      in_=xq[ot * 128:(ot + 1) * 128, sl])
                s.inc(last, "xd", 16)
                s.wait(vec, "xd")
                s.wait(vec, "t")
                s.wait(vec, "st")
                vec.tensor_scalar(t1[:, :], pb[4][:, :], bsb[:, ot, 0:1], None,
                                  op0=OP.add)
                vec.tensor_add(t1[:, :], t1[:, :], xr[:, :])
                s.wait(sync, "v")
                last = sync.dma_start(out=y[ot * 128:(ot + 1) * 128, sl],
                                      in_=t1[:, :])
                s.inc(last, "st", 16)
            wslot2[slot] = s.mark("t")
        s.wait(sync, "st")
        s.wait(gps, "st")



def emit_b0(p, s, io, pref, S=4096, Q=2048, A=3584, W=512, TOPK=64):
    NKT2 = S // 128
    NA = A // 128
    NQC = Q // 512
    nc = p.nc
    from contextlib import ExitStack as _ES0
    with _ES0() as _stage:
        xf = io["xf"]
        xq = io["xq"]
        wqw = io["wqw"]
        wkw = io["wkw"]
        wvw = io["wvw"]
        wow = io["wow"]
        wqb = io["wqb"]
        wkb = io["wkb"]
        wvb = io["wvb"]
        wob = io["wob"]
        wqiw = io["wqiw"]
        wkiw = io["wkiw"]
        wqib = io["wqib"]
        wkib = io["wkib"]
        identf = io["identf"]
        y = io["y"]
        odram = io["odram"]

        xn = p.sbx(_stage, pref + "xn", [128, KT, S], BF16)
        xnq = p.sbx(_stage, pref + "xnq", [128, KT, Q], BF16)
        xr = p.sbx(_stage, pref + "xr", [128, 512], F32)
        xbf = p.sbx(_stage, pref + "xbf", [128, 512], BF16)
        xsq = p.sbx(_stage, pref + "xsq", [128, 512], BF16)
        stats = p.sbx(_stage, pref + "stats", [1, 5, 512], F32)
        t1 = p.sbx(_stage, pref + "t1", [128, 512], F32)
        ones_bf = p.sbx(_stage, pref + "ones_bf", [128, 1], BF16)
        ones_f = p.sbx(_stage, pref + "ones_f", [1, 128], F32)
        ident = p.sbx(_stage, pref + "ident", [128, 128], F32)
        wsb = [p.sbx(_stage, pref + f"wsb{i}", [128, KT, 128], BF16) for i in range(2)]
        wisb = p.sbx(_stage, pref + "wisb", [128, KT, 128], BF16)
        ibia = p.sbx(_stage, pref + "ibia", [128, 2], F32)
        qidx = p.sbx(_stage, pref + "qidx", [64, Q], BF16)
        kidx = p.sbx(_stage, pref + "kidx", [64, A], BF16)
        thrr = p.sbx(_stage, pref + "thrr", [1, Q], F32)
        thrb = p.sbx(_stage, pref + "thrb", [128, 512], BF16)
        m8 = p.sbx(_stage, pref + "m8", [128, 8], F32)
        bcv = p.sbx(_stage, pref + "bcv", [128, 128], F32)
        kfm = p.sbx(_stage, pref + "kfm", [128, S], BF16)
        qfm = p.sbx(_stage, pref + "qfm", [128, Q], BF16)
        vtm = p.sbx(_stage, pref + "vtm", [128, NKT2, 130], BF16)
        ptile = p.sbx(_stage, pref + "ptile", [128, 2, 512], BF16)
        osb = p.sbx(_stage, pref + "osb", [128, 512], BF16)
        rd1 = p.sbx(_stage, pref + "rd1", [1, 512], F32)
        bsb = p.sbx(_stage, pref + "bsb", [128, KT, 1], F32)
        bqk = p.sbx(_stage, pref + "bqk", [128, 2], F32)
        wvb_sb = p.sbx(_stage, pref + "wvb_sb", [1, D], F32)

        pb = [p.psx(_stage, pref + f"pb{i}", [128, 512]) for i in range(6)]

        sync, gps, ten = p.sync, p.gpsd, p.ten
        vec, act = p.vec, p.act

        vec.memset(ones_bf[:, :], 1.0)
        vec.memset(ones_f[:, :], 1.0)
        vec.memset(vtm[:, :, 64:65], 1.0)
        vec.memset(vtm[:, :, 129:130], 1.0)
        last = sync.dma_start(out=ident[:, :], in_=identf)
        s.inc(last, "xd", 16)
        last = sync.dma_start(out=bsb[:, :, :],
                              in_=wob.rearrange("(k p) o -> p k o", p=128),
                              allow_slow_non_contiguous=True)
        s.inc(last, "xd", 16)
        last = sync.dma_start(out=ibia[0:64, 0:1], in_=wqib)
        s.inc(last, "xd", 16)
        last = sync.dma_start(out=ibia[0:64, 1:2], in_=wkib)
        s.inc(last, "xd", 16)
        last = sync.dma_start(out=wvb_sb[:, :], in_=wvb)
        s.inc(last, "xd", 16)
        lw = gps.dma_start(out=wisb[:, :, 0:64],
                           in_=wqiw.rearrange("(k p) c -> p k c", p=128))
        s.inc(lw, "ld1", 16)
        lw = gps.dma_start(out=wisb[:, :, 64:128],
                           in_=wkiw.rearrange("(k p) c -> p k c", p=128))
        s.inc(lw, "ld1", 16)

        from contextlib import ExitStack as _ES
        with _ES() as sub:
            xch = sub.enter_context(nc.sbuf_tensor(pref + "xch", [128, KT, 512], F32))
            emit_ln2(p, s, S, io["xfsrc"], xn, xch, xbf, xsq, stats, t1, ones_bf, ones_f,
                    pb[0][0:1, :], pb[3][0:1, :], pb[1], pb[2])
            emit_ln2(p, s, Q, FlatSrc(xq), xnq, xch, xbf, xsq, stats, t1, ones_bf, ones_f,
                    pb[0][0:1, :], pb[3][0:1, :], pb[1], pb[2])

        s.wait(ten, "ld1")
        s.wait(ten, "xd")
        s.wait(ten, "v")
        for c in range(NQC):
            sl = slice(c * 512, (c + 1) * 512)
            s.wait(ten, "v")
            for kt in range(KT):
                lastm = ten.matmul(pb[4][0:64, :], wisb[:, kt, 0:64],
                                   xnq[:, kt, sl], start=(kt == 0),
                                   stop=(kt == KT - 1))
            s.inc(lastm, "t")
            s.wait(vec, "t")
            vec.tensor_scalar(qidx[:, sl], pb[4][0:64, :], ibia[0:64, 0:1], None,
                              op0=OP.add)
        for c in range(A // 512):
            sl = slice(c * 512, (c + 1) * 512)
            s.wait(ten, "v")
            for kt in range(KT):
                lastm = ten.matmul(pb[4][0:64, :], wisb[:, kt, 64:128],
                                   xn[:, kt, sl], start=(kt == 0),
                                   stop=(kt == KT - 1))
            s.inc(lastm, "t")
            s.wait(vec, "t")
            vec.tensor_scalar(kidx[:, sl], pb[4][0:64, :], ibia[0:64, 1:2], None,
                              op0=OP.add)

        with _ES() as sub:
            scr = sub.enter_context(nc.sbuf_tensor(pref + "scr", [128, A], F32))
            for qt in range(Q // 128):
                tsl = slice(qt * 128, (qt + 1) * 128)
                for kc in range(A // 512):
                    ksl = slice(kc * 512, (kc + 1) * 512)
                    s.wait(ten, "a")
                    s.wait(ten, "v")
                    lastm = ten.matmul(pb[4][:, :], qidx[:, tsl], kidx[:, ksl],
                                       start=True, stop=True,
                                       skip_group_check=True)
                    s.inc(lastm, "t")
                    s.wait(act, "t")
                    s.wait(act, "v")
                    act.activation(scr[:, ksl], pb[4][:, :], AF.Relu)
                for r in range(TOPK // 8):
                    s.wait(vec, "a")
                    vec.max(out=m8[:, :], in_=scr[:, :])
                    if r < TOPK // 8 - 1:
                        vec.match_replace(out=scr[:, :], in_to_replace=m8[:, :],
                                          in_values=scr[:, :], imm_value=-1e30)
                s.wait(ten, "v")
                lastm = ten.transpose(pb[5][0:1, 0:128], m8[:, 7:8], ident[:, :])
                s.inc(lastm, "t")
                s.wait(vec, "t")
                vec.tensor_copy(thrr[0:1, tsl], pb[5][0:1, 0:128])
                s.wait(ten, "v")
                s.wait(act, "v")

        msks = p.sbx(_stage, pref + "msks", [128, NA, 512], BF16)
        osb2 = p.sbx(_stage, pref + "osb2", [64, 512], BF16)
        ofm_sb = p.sbx(_stage, pref + "ofm_sb", [128, KT, 512], BF16)

        wslot_mm = [0, 0]

        def load_w(src, cols_off, slot, ch):
            s.wait(gps, "t", wslot_mm[slot])
            lw = gps.dma_start(
                out=wsb[slot][:, :, :],
                in_=src[:, cols_off:cols_off + 128].rearrange(
                    "(k p) c -> p k c", p=128))
            s.inc(lw, ch, 16)

        for hp in range(8):
            csl128 = slice(hp * 128, (hp + 1) * 128)
            s.wait(sync, "v")
            last = sync.dma_start(out=bqk[:, 0:1],
                                  in_=wqb[0:1, csl128].rearrange("o p -> p o"))
            s.inc(last, "xd", 16)
            last = sync.dma_start(out=bqk[:, 1:2],
                                  in_=wkb[0:1, csl128].rearrange("o p -> p o"))
            s.inc(last, "xd", 16)
            s.wait(ten, "xd")
            s.wait(ten, "v")
            lastm = ten.matmul(pb[5][:, 0:128], ones_f[:, :], wvb_sb[0:1, csl128],
                               start=True, stop=True, skip_group_check=True)
            s.inc(lastm, "t")
            s.wait(vec, "t")
            vec.tensor_copy(bcv[:, :], pb[5][:, 0:128])

            load_w(wkw, hp * 128, 0, "wl0")
            s.wait(ten, "wl0")
            s.wait(ten, "a")
            s.wait(ten, "v")
            for c in range(S // 512):
                sl = slice(c * 512, (c + 1) * 512)
                s.wait(ten, "v")
                for kt in range(KT):
                    lastm = ten.matmul(pb[4][:, :], wsb[0][:, kt, :], xn[:, kt, sl],
                                       start=(kt == 0), stop=(kt == KT - 1))
                s.inc(lastm, "t")
                s.wait(vec, "t")
                vec.tensor_scalar(kfm[:, sl], pb[4][:, :], bqk[:, 1:2], None,
                                  op0=OP.add)
            wslot_mm[0] = s.mark("t")
            load_w(wqw, hp * 128, 1, "wl1")
            s.wait(ten, "wl1")
            s.wait(ten, "v")
            for c in range(NQC):
                sl = slice(c * 512, (c + 1) * 512)
                s.wait(ten, "v")
                for kt in range(KT):
                    lastm = ten.matmul(pb[5][:, :], wsb[1][:, kt, :], xnq[:, kt, sl],
                                       start=(kt == 0), stop=(kt == KT - 1))
                s.inc(lastm, "t")
                s.wait(vec, "t")
                vec.tensor_scalar(qfm[:, sl], pb[5][:, :], bqk[:, 0:1], None,
                                  op0=OP.add)
            wslot_mm[1] = s.mark("t")
            load_w(wvw, hp * 128, 0, "wl0")
            s.wait(ten, "wl0")
            s.wait(ten, "v")
            for tt in range(NKT2):
                tsl = slice(tt * 128, (tt + 1) * 128)
                s.wait(ten, "v")
                for kt in range(KT):
                    lastm = ten.matmul(pb[4][:, 0:128], xn[:, kt, tsl],
                                       wsb[0][:, kt, :],
                                       start=(kt == 0), stop=(kt == KT - 1))
                s.inc(lastm, "t")
                s.wait(vec, "t")
                vec.tensor_add(t1[:, 0:128], pb[4][:, 0:128], bcv[:, :])
                vec.tensor_copy(vtm[:, tt, 0:64], t1[:, 0:64])
                vec.tensor_copy(vtm[:, tt, 65:129], t1[:, 64:128])
            wslot_mm[0] = s.mark("t")
            s.wait(ten, "v")

            for qc in range(NQC):
                qsl = slice(qc * 512, (qc + 1) * 512)
                s.wait(ten, "v")
                lastm = ten.matmul(pb[5][:, :], ones_f[:, :], thrr[0:1, qsl],
                                   start=True, stop=True, skip_group_check=True)
                s.inc(lastm, "t")
                s.wait(vec, "t")
                vec.tensor_copy(thrb[:, :], pb[5][:, :])
                for ka in range(NA):
                    s.wait(ten, "v")
                    lastm = ten.matmul(pb[5][:, :],
                                       kidx[:, ka * 128:(ka + 1) * 128],
                                       qidx[:, qsl], start=True, stop=True,
                                       skip_group_check=True)
                    s.inc(lastm, "t")
                    s.wait(vec, "t")
                    vec.tensor_tensor(msks[:, ka, :], pb[5][:, :], thrb[:, :],
                                      OP.is_ge)
                s.wait(ten, "v")
                for h2 in range(2):
                    hb = 64 * h2
                    opb = pb[2 + h2]
                    exp_mark = [0, 0]
                    p_mark = [("a", 0), ("a", 0)]
                    for j in range(NKT2 + 1):
                        if j < NKT2:
                            kt2 = j
                            sb_ = j % 2
                            s.wait(ten, "a", exp_mark[sb_])
                            lastm = ten.matmul(
                                pb[sb_][:, :],
                                kfm[hb:hb + 64, kt2 * 128:(kt2 + 1) * 128],
                                qfm[hb:hb + 64, qsl],
                                start=True, stop=True, skip_group_check=True)
                            s.inc(lastm, "t")
                            s.wait(act, "t")
                            if kt2 < NA:
                                s.wait(act, "v")
                                act.activation(osb[:, :], pb[sb_][:, :],
                                               AF.Exp, scale=0.125)
                                exp_mark[sb_] = s.mark("a")
                                s.wait(vec, "a")
                                vec.tensor_mul(ptile[:, sb_, :], osb[:, :],
                                               msks[:, kt2, :])
                                p_mark[sb_] = ("v", s.mark("v"))
                            else:
                                act.activation(ptile[:, sb_, :], pb[sb_][:, :],
                                               AF.Exp, scale=0.125)
                                exp_mark[sb_] = s.mark("a")
                                p_mark[sb_] = ("a", s.mark("a"))
                        if j > 0:
                            kk = j - 1
                            sb_ = kk % 2
                            ch_, mk_ = p_mark[sb_]
                            s.wait(ten, ch_, mk_)
                            lastm = ten.matmul(
                                opb[0:65, :],
                                vtm[:, kk, 65 * h2:65 * h2 + 65],
                                ptile[:, sb_, :],
                                start=(kk == 0), stop=(kk == NKT2 - 1),
                                skip_group_check=True)
                            s.inc(lastm, "t")
                for h2 in range(2):
                    opb = pb[2 + h2]
                    s.wait(vec, "t")
                    vec.reciprocal(rd1[0:1, :], opb[64:65, :])
                    s.wait(ten, "v")
                    lastm = ten.matmul(pb[5][0:64, :], ones_f[:, 0:64],
                                       rd1[0:1, :], start=True, stop=True,
                                       skip_group_check=True)
                    s.inc(lastm, "t")
                    s.wait(vec, "t")
                    vec.tensor_copy(osb[0:64, :], opb[0:64, :])
                    h_global = hp * 2 + h2
                    orow = h_global * 64
                    s.wait(vec, "st")
                    vec.tensor_mul(osb2[0:64, :], osb[0:64, :], pb[5][0:64, :])
                    s.wait(sync, "v")
                    last = sync.dma_start(out=odram[orow:orow + 64, qsl],
                                          in_=osb2[0:64, :])
                    s.inc(last, "st", 16)
                    s.wait(ten, "v")

        s.wait(sync, "st")
        wslot2 = list(wslot_mm)
        for c in range(NQC):
            sl = slice(c * 512, (c + 1) * 512)
            s.wait(sync, "t", 0 if c == 0 else s.mark("t"))
            last = sync.dma_start(
                out=ofm_sb[:, :, :],
                in_=odram[:, sl].rearrange("(k p) q -> p k q", p=128))
            s.inc(last, "eb", 16)
            for ot in range(KT):
                slot = ot % 2
                ch = f"wl{slot}"
                s.wait(gps, "t", wslot2[slot])
                lw = gps.dma_start(
                    out=wsb[slot][:, :, :],
                    in_=wow[:, ot * 128:(ot + 1) * 128].rearrange(
                        "(k p) c -> p k c", p=128))
                s.inc(lw, ch, 16)
                s.wait(ten, ch)
                s.wait(ten, "eb")
                s.wait(ten, "v")
                for kt in range(KT):
                    lastm = ten.matmul(pb[4][:, :], wsb[slot][:, kt, :],
                                       ofm_sb[:, kt, :], start=(kt == 0),
                                       stop=(kt == KT - 1))
                s.inc(lastm, "t")
                wslot2[slot] = s.mark("t")
                s.wait(sync, "v")
                last = sync.dma_start(out=xr[:, :],
                                      in_=xq[ot * 128:(ot + 1) * 128, sl])
                s.inc(last, "xd", 16)
                s.wait(vec, "xd")
                s.wait(vec, "t")
                s.wait(vec, "st")
                vec.tensor_scalar(t1[:, :], pb[4][:, :], bsb[:, ot, 0:1], None,
                                  op0=OP.add)
                vec.tensor_add(t1[:, :], t1[:, :], xr[:, :])
                s.wait(sync, "v")
                last = sync.dma_start(out=y[ot * 128:(ot + 1) * 128, sl],
                                      in_=t1[:, :])
                s.inc(last, "st", 16)
        s.wait(sync, "st")
        s.wait(gps, "st")



def emit_a1(p, s, io, pref, S=4096, Q=2048):
    NKT2 = S // 128
    NQC = Q // 512
    nc = p.nc
    from contextlib import ExitStack as _ES0
    with _ES0() as _stage:
        xf = io["xf"]
        xq = io["xq"]
        wq = io["wq"]
        wk = io["wk"]
        wv = io["wv"]
        cm = io["cm"]
        y = io["y"]

        xn = p.sbx(_stage, pref + "xn", [128, KT, S], BF16)
        xnq = p.sbx(_stage, pref + "xnq", [128, KT, Q], BF16)
        xch = p.sbx(_stage, pref + "xch", [128, KT, 512], F32)
        xr = p.sbx(_stage, pref + "xr", [128, 512], F32)
        xbf = p.sbx(_stage, pref + "xbf", [128, 512], BF16)
        xsq = p.sbx(_stage, pref + "xsq", [128, 512], BF16)
        stats = p.sbx(_stage, pref + "stats", [1, 5, 512], F32)
        t1 = p.sbx(_stage, pref + "t1", [128, 512], F32)
        ones_bf = p.sbx(_stage, pref + "ones_bf", [128, 1], BF16)
        ones_f = p.sbx(_stage, pref + "ones_f", [1, 128], F32)
        wqk_sb = p.sbx(_stage, pref + "wqk_sb", [128, KT, 128], BF16)
        wv_sb = p.sbx(_stage, pref + "wv_sb", [128, KT, 256], BF16)
        kfm = p.sbx(_stage, pref + "kfm", [64, S], BF16)
        qfm = p.sbx(_stage, pref + "qfm", [64, Q], BF16)
        vtm = p.sbx(_stage, pref + "vtm", [128, NKT2, 256], BF16)
        cms = p.sbx(_stage, pref + "cms", [128, NKT2, 512], BF16)
        rdq = p.sbx(_stage, pref + "rdq", [1, NQC, 512], F32)
        osb = p.sbx(_stage, pref + "osb", [128, 512], BF16)

        pb = [p.psx(_stage, pref + f"pb{i}", [128, 512]) for i in range(6)]

        sync, gps, ten = p.sync, p.gpsd, p.ten
        vec, act = p.vec, p.act

        vec.memset(ones_bf[:, :], 1.0)
        vec.memset(ones_f[:, :], 1.0)

        emit_ln2(p, s, S, io["xfsrc"], xn, xch, xbf, xsq, stats, t1, ones_bf, ones_f,
                pb[0][0:1, :], pb[3][0:1, :], pb[1], pb[2])
        emit_ln2(p, s, Q, FlatSrc(xq), xnq, xch, xbf, xsq, stats, t1, ones_bf, ones_f,
                pb[0][0:1, :], pb[3][0:1, :], pb[1], pb[2])

        lw = gps.dma_start(out=wqk_sb[:, :, 0:64],
                           in_=wq.rearrange("(k p) c -> p k c", p=128))
        s.inc(lw, "wl0", 16)
        lw = gps.dma_start(out=wqk_sb[:, :, 64:128],
                           in_=wk.rearrange("(k p) c -> p k c", p=128))
        s.inc(lw, "wl0", 16)
        s.wait(ten, "wl0")
        s.wait(ten, "v")
        for c in range(NQC):
            sl = slice(c * 512, (c + 1) * 512)
            s.wait(ten, "v")
            for kt in range(KT):
                lastm = ten.matmul(pb[4][0:64, :], wqk_sb[:, kt, 0:64],
                                   xnq[:, kt, sl], start=(kt == 0),
                                   stop=(kt == KT - 1))
            s.inc(lastm, "t")
            s.wait(vec, "t")
            vec.tensor_copy(qfm[:, sl], pb[4][0:64, :])
        for c in range(S // 512):
            sl = slice(c * 512, (c + 1) * 512)
            s.wait(ten, "v")
            for kt in range(KT):
                lastm = ten.matmul(pb[4][0:64, :], wqk_sb[:, kt, 64:128],
                                   xn[:, kt, sl], start=(kt == 0),
                                   stop=(kt == KT - 1))
            s.inc(lastm, "t")
            s.wait(vec, "t")
            vec.tensor_copy(kfm[:, sl], pb[4][0:64, :])

        vload_mark = 0
        for vh in range(4):
            s.wait(gps, "t", vload_mark)
            lw = gps.dma_start(
                out=wv_sb[:, :, :],
                in_=wv[:, vh * 256:(vh + 1) * 256].rearrange(
                    "(k p) c -> p k c", p=128))
            s.inc(lw, "wl1", 16)
            s.wait(ten, "wl1")
            for tt in range(NKT2):
                tsl = slice(tt * 128, (tt + 1) * 128)
                s.wait(ten, "v")
                for kt in range(KT):
                    lastm = ten.matmul(pb[4][:, 0:256], xn[:, kt, tsl],
                                       wv_sb[:, kt, :], start=(kt == 0),
                                       stop=(kt == KT - 1))
                s.inc(lastm, "t")
                s.wait(vec, "t")
                vec.tensor_copy(vtm[:, tt, :], pb[4][:, 0:256])
            vload_mark = s.mark("t")
            for qc in range(NQC):
                qsl = slice(qc * 512, (qc + 1) * 512)
                s.wait(sync, "v")
                last = sync.dma_start(
                    out=cms[:, :, :],
                    in_=cm[:, qsl].rearrange("(kt p) q -> p kt q", p=128))
                s.inc(last, "eb", 16)
                s.wait(vec, "eb")
                for kt2 in range(NKT2):
                    s.wait(ten, "a")
                    lastm = ten.matmul(pb[0][:, :], kfm[:, kt2 * 128:(kt2 + 1) * 128],
                                       qfm[:, qsl], start=True, stop=True,
                                       skip_group_check=True)
                    s.inc(lastm, "t")
                    s.wait(act, "t")
                    s.wait(act, "v")
                    act.activation(osb[:, :], pb[0][:, :], AF.Exp, scale=0.125)
                    s.wait(vec, "a")
                    vec.tensor_mul(cms[:, kt2, :], osb[:, :], cms[:, kt2, :])
                    s.wait(ten, "v")
                    lastm = ten.matmul(pb[1][0:1, :], ones_bf[:, :],
                                       cms[:, kt2, :], start=(kt2 == 0),
                                       stop=(kt2 == NKT2 - 1),
                                       skip_group_check=True)
                    s.inc(lastm, "t")
                if vh == 0:
                    s.wait(vec, "t")
                    vec.reciprocal(rdq[0:1, qc, :], pb[1][0:1, :])
                for dvt in range(2):
                    s.wait(ten, "v")
                    for kt2 in range(NKT2):
                        lastm = ten.matmul(
                            pb[2][:, :], vtm[:, kt2, dvt * 128:(dvt + 1) * 128],
                            cms[:, kt2, :], start=(kt2 == 0),
                            stop=(kt2 == NKT2 - 1), skip_group_check=True)
                    s.inc(lastm, "t")
                    s.wait(ten, "v")
                    lastm = ten.matmul(pb[3][:, :], ones_f[:, :], rdq[0:1, qc, :],
                                       start=True, stop=True, skip_group_check=True)
                    s.inc(lastm, "t")
                    s.wait(vec, "t")
                    vec.tensor_copy(osb[:, :], pb[2][:, :])
                    ot = vh * 2 + dvt
                    s.wait(sync, "v")
                    last = sync.dma_start(out=xr[:, :],
                                          in_=xq[ot * 128:(ot + 1) * 128, qsl])
                    s.inc(last, "xd", 16)
                    s.wait(vec, "xd")
                    s.wait(vec, "st")
                    vec.tensor_mul(t1[:, :], osb[:, :], pb[3][:, :])
                    vec.tensor_add(t1[:, :], t1[:, :], xr[:, :])
                    s.wait(sync, "v")
                    last = sync.dma_start(out=y[ot * 128:(ot + 1) * 128, qsl],
                                          in_=t1[:, :])
                    s.inc(last, "st", 16)
                    s.wait(ten, "v")
        s.wait(sync, "st")
        s.wait(gps, "st")



def emit_b1(p, s, io, pref, S=4096, Q=4096):
    NKT2 = S // 128
    NQC = Q // 512
    nc = p.nc
    from contextlib import ExitStack as _ES0
    with _ES0() as _stage:
        xf = io["xf"]
        xq = io["xq"]
        wkT = io["wkT"]
        wqT = io["wqT"]
        wvT = io["wvT"]
        woT = io["woT"]
        bo = io["bo"]
        klng2 = io["klng2"]
        klnb2 = io["klnb2"]
        qlng2 = io["qlng2"]
        qlnb2 = io["qlnb2"]
        cm = io["cm"]
        blk2Tc = io["blk2Tc"]
        y = io["y"]

        xn = p.sbx(_stage, pref + "xn", [128, KT, S], BF16)
        qn_all = p.sbx(_stage, pref + "qn_all", [128, KT, Q], BF16)
        xr = p.sbx(_stage, pref + "xr", [128, 512], F32)
        xbf = p.sbx(_stage, pref + "xbf", [128, 512], BF16)
        xsq = p.sbx(_stage, pref + "xsq", [128, 512], BF16)
        stats = p.sbx(_stage, pref + "stats", [1, 5, 512], F32)
        hstats = p.sbx(_stage, pref + "hstats", [2, 5, 512], F32)
        t1 = p.sbx(_stage, pref + "t1", [128, 512], F32)
        t2 = p.sbx(_stage, pref + "t2", [128, 512], F32)
        ones_bf = p.sbx(_stage, pref + "ones_bf", [128, 1], BF16)
        ones_f = p.sbx(_stage, pref + "ones_f", [1, 128], F32)
        blk2 = p.sbx(_stage, pref + "blk2", [128, 2], BF16)
        blk2T = p.sbx(_stage, pref + "blk2T", [2, 128], BF16)
        gln = p.sbx(_stage, pref + "gln", [128, 4], F32)
        wsb = [p.sbx(_stage, pref + f"wsb{i}", [128, KT, 128], BF16) for i in range(2)]
        kfm = p.sbx(_stage, pref + "kfm", [128, S], BF16)
        vtm = p.sbx(_stage, pref + "vtm", [128, NKT2, 130], BF16)
        ptile = p.sbx(_stage, pref + "ptile", [128, 2, 512], BF16)
        osb = p.sbx(_stage, pref + "osb", [128, 512], BF16)
        rd1 = p.sbx(_stage, pref + "rd1", [1, 512], F32)
        bsb = p.sbx(_stage, pref + "bsb", [128, KT, 1], F32)

        pb = [p.psx(_stage, pref + f"pb{i}", [128, 512]) for i in range(6)]

        sync, gps, ten = p.sync, p.gpsd, p.ten
        vec, act = p.vec, p.act

        vec.memset(ones_bf[:, :], 1.0)
        vec.memset(ones_f[:, :], 1.0)
        vec.memset(blk2[:, :], 0.0)
        vec.memset(blk2[0:64, 0:1], 1.0)
        vec.memset(blk2[64:128, 1:2], 1.0)
        lw = gps.dma_start(out=blk2T[:, :], in_=blk2Tc)
        s.inc(lw, "ld1", 16)
        s.wait(ten, "ld1")
        vec.memset(vtm[:, :, 64:65], 1.0)
        vec.memset(vtm[:, :, 129:130], 1.0)
        for i, t in enumerate([klng2, klnb2, qlng2, qlnb2]):
            last = sync.dma_start(out=gln[:, i:i + 1], in_=t)
            s.inc(last, "xd", 16)
        last = sync.dma_start(out=bsb[:, :, :],
                              in_=bo.rearrange("(k p) o -> p k o", p=128),
                              allow_slow_non_contiguous=True)
        s.inc(last, "xd", 16)

        wslot_mm = [0, 0]

        def load_w(src, cols_off, slot, ch):
            s.wait(gps, "t", wslot_mm[slot])
            lw = gps.dma_start(
                out=wsb[slot][:, :, :],
                in_=src[:, cols_off:cols_off + 128].rearrange(
                    "(k p) c -> p k c", p=128))
            s.inc(lw, ch, 16)

        def headln(dst, src_fm, ncols, gcol, bcol):
            for c in range(ncols // 512):
                sl = slice(c * 512, (c + 1) * 512)
                s.wait(vec, "t")
                vec.tensor_copy(xbf[:, :], src_fm[:, sl])
                vec.tensor_mul(xsq[:, :], xbf[:, :], xbf[:, :])
                s.wait(ten, "v")
                ten.matmul(pb[4][0:2, :], blk2[:, :], xbf[:, :],
                           start=True, stop=True, skip_group_check=True)
                lastm = ten.matmul(pb[5][0:2, :], blk2[:, :], xsq[:, :],
                                   start=True, stop=True, skip_group_check=True)
                s.inc(lastm, "t")
                s.wait(vec, "t")
                vec.tensor_scalar(hstats[0:2, 2, :], pb[4][0:2, :], 1.0 / 64, None,
                                  op0=OP.mult)
                vec.tensor_scalar(hstats[0:2, 1, :], pb[5][0:2, :], 1.0 / 64, None,
                                  op0=OP.mult)
                vec.tensor_mul(hstats[0:2, 3, :], hstats[0:2, 2, :], hstats[0:2, 2, :])
                vec.tensor_sub(hstats[0:2, 3, :], hstats[0:2, 1, :], hstats[0:2, 3, :])
                vec.tensor_scalar(hstats[0:2, 3, :], hstats[0:2, 3, :], EPS, None,
                                  op0=OP.add)
                s.wait(act, "v")
                act.activation(hstats[0:2, 4, :], hstats[0:2, 3, :], AF.Ln)
                act.activation(hstats[0:2, 4, :], hstats[0:2, 4, :], AF.Exp,
                               scale=-0.5)
                s.wait(vec, "a")
                vec.tensor_copy(xbf[0:2, 0:512], hstats[0:2, 2, :])
                vec.tensor_copy(xsq[0:2, 0:512], hstats[0:2, 4, :])
                s.wait(ten, "v")
                ten.matmul(pb[4][:, :], blk2T[:, :], xbf[0:2, 0:512],
                           start=True, stop=True, skip_group_check=True)
                lastm = ten.matmul(pb[5][:, :], blk2T[:, :], xsq[0:2, 0:512],
                                   start=True, stop=True, skip_group_check=True)
                s.inc(lastm, "t")
                s.wait(vec, "t")
                vec.tensor_sub(t2[:, :], src_fm[:, sl], pb[4][:, :])
                vec.tensor_mul(t2[:, :], t2[:, :], pb[5][:, :])
                vec.tensor_scalar(dst[:, sl], t2[:, :], gln[:, gcol:gcol + 1],
                                  gln[:, bcol:bcol + 1], op0=OP.mult, op1=OP.add)
                s.wait(ten, "v")

        from contextlib import ExitStack as _ES
        with _ES() as sub:
            xch = sub.enter_context(nc.sbuf_tensor(pref + "xch", [128, KT, 512], F32))
            xnq = xn
            emit_ln2(p, s, S, io["xfsrc"], xn, xch, xbf, xsq, stats, t1, ones_bf, ones_f,
                    pb[0][0:1, :], pb[3][0:1, :], pb[1], pb[2])
            for hp in range(8):
                load_w(wqT, hp * 128, 1, "wl1")
                s.wait(ten, "wl1")
                for c in range(NQC):
                    sl = slice(c * 512, (c + 1) * 512)
                    s.wait(ten, "v")
                    for kt in range(KT):
                        lastm = ten.matmul(pb[5][:, :], wsb[1][:, kt, :],
                                           xnq[:, kt, sl], start=(kt == 0),
                                           stop=(kt == KT - 1))
                    s.inc(lastm, "t")
                    s.wait(vec, "t")
                    vec.tensor_copy(qn_all[:, hp, sl], pb[5][:, :])
                wslot_mm[1] = s.mark("t")
                headln(qn_all[:, hp, :], qn_all[:, hp, :], Q, 2, 3)
        cms = p.sbx(_stage, pref + "cms", [128, 16, 512], BF16)
        odram = io["odram"]
        osb2 = p.sbx(_stage, pref + "osb2", [64, 512], BF16)
        ofm_sb = p.sbx(_stage, pref + "ofm_sb", [128, KT, 512], BF16)

        for hp in range(8):
            load_w(wkT, hp * 128, 0, "wl0")
            s.wait(ten, "wl0")
            s.wait(ten, "a")
            s.wait(ten, "v")
            for c in range(S // 512):
                sl = slice(c * 512, (c + 1) * 512)
                s.wait(ten, "v")
                for kt in range(KT):
                    lastm = ten.matmul(pb[4][:, :], wsb[0][:, kt, :], xn[:, kt, sl],
                                       start=(kt == 0), stop=(kt == KT - 1))
                s.inc(lastm, "t")
                s.wait(vec, "t")
                vec.tensor_copy(kfm[:, sl], pb[4][:, :])
            wslot_mm[0] = s.mark("t")
            headln(kfm, kfm, S, 0, 1)
            load_w(wvT, hp * 128, 0, "wl0")
            s.wait(ten, "wl0")
            s.wait(ten, "v")
            for tt in range(NKT2):
                tsl = slice(tt * 128, (tt + 1) * 128)
                s.wait(ten, "v")
                for kt in range(KT):
                    lastm = ten.matmul(pb[4][:, 0:128], xn[:, kt, tsl],
                                       wsb[0][:, kt, :],
                                       start=(kt == 0), stop=(kt == KT - 1))
                s.inc(lastm, "t")
                s.wait(vec, "t")
                vec.tensor_copy(vtm[:, tt, 0:64], pb[4][:, 0:64])
                vec.tensor_copy(vtm[:, tt, 65:129], pb[4][:, 64:128])
            wslot_mm[0] = s.mark("t")
            s.wait(ten, "v")

            for qc in range(NQC):
                qsl = slice(qc * 512, (qc + 1) * 512)
                kts = [k for k in range(NKT2) if 128 * k <= 512 * qc + 511]
                dset = set(range(4 * qc, 4 * qc + 4))
                s.wait(sync, "v")
                last = sync.dma_start(
                    out=cms[:, 0:4, :],
                    in_=cm[qc * 512:(qc + 1) * 512, qsl].rearrange(
                        "(kt p) q -> p kt q", p=128))
                s.inc(last, "eb", 16)
                s.wait(vec, "eb")
                s.wait(act, "eb")
                for h2 in range(2):
                    hb = 64 * h2
                    opb = pb[2 + h2]
                    exp_mark = [0, 0]
                    p_mark = [0, 0]
                    n = len(kts)
                    for j in range(n + 1):
                        if j < n:
                            kt2 = kts[j]
                            sb_ = j % 2
                            s.wait(ten, "a", exp_mark[sb_])
                            lastm = ten.matmul(
                                pb[sb_][:, :],
                                kfm[hb:hb + 64, kt2 * 128:(kt2 + 1) * 128],
                                qn_all[hb:hb + 64, hp, qsl],
                                start=True, stop=True, skip_group_check=True)
                            s.inc(lastm, "t")
                            s.wait(act, "t")
                            if kt2 in dset:
                                s.wait(act, "v")
                                act.activation(osb[:, :], pb[sb_][:, :],
                                               AF.Exp, scale=1.0 / 32.0)
                                exp_mark[sb_] = s.mark("a")
                                s.wait(vec, "a")
                                vec.tensor_mul(ptile[:, sb_, :], osb[:, :],
                                               cms[:, kt2 - 4 * qc, :])
                                p_mark[sb_] = ("v", s.mark("v"))
                            else:
                                act.activation(ptile[:, sb_, :], pb[sb_][:, :],
                                               AF.Exp, scale=1.0 / 32.0)
                                exp_mark[sb_] = s.mark("a")
                                p_mark[sb_] = ("a", s.mark("a"))
                        if j > 0:
                            kk = kts[j - 1]
                            sb_ = (j - 1) % 2
                            ch_, mk_ = p_mark[sb_]
                            s.wait(ten, ch_, mk_)
                            lastm = ten.matmul(
                                opb[0:65, :],
                                vtm[:, kk, 65 * h2:65 * h2 + 65],
                                ptile[:, sb_, :],
                                start=(j - 1 == 0), stop=(j - 1 == n - 1),
                                skip_group_check=True)
                            s.inc(lastm, "t")
                for h2 in range(2):
                    opb = pb[2 + h2]
                    s.wait(vec, "t")
                    vec.reciprocal(rd1[0:1, :], opb[64:65, :])
                    s.wait(ten, "v")
                    lastm = ten.matmul(pb[5][0:64, :], ones_f[:, 0:64],
                                       rd1[0:1, :], start=True, stop=True,
                                       skip_group_check=True)
                    s.inc(lastm, "t")
                    s.wait(vec, "t")
                    vec.tensor_copy(osb[0:64, :], opb[0:64, :])
                    h_global = hp * 2 + h2
                    orow = h_global * 64
                    s.wait(vec, "st")
                    vec.tensor_mul(osb2[0:64, :], osb[0:64, :], pb[5][0:64, :])
                    s.wait(sync, "v")
                    last = sync.dma_start(out=odram[orow:orow + 64, qsl],
                                          in_=osb2[0:64, :])
                    s.inc(last, "st", 16)
                    s.wait(ten, "v")

        s.wait(sync, "st")
        wslot2 = list(wslot_mm)
        for c in range(NQC):
            sl = slice(c * 512, (c + 1) * 512)
            s.wait(sync, "t", 0 if c == 0 else s.mark("t"))
            last = sync.dma_start(
                out=ofm_sb[:, :, :],
                in_=odram[:, sl].rearrange("(k p) q -> p k q", p=128))
            s.inc(last, "eb", 16)
            for ot in range(KT):
                slot = ot % 2
                ch = f"wl{slot}"
                s.wait(gps, "t", wslot2[slot])
                lw = gps.dma_start(
                    out=wsb[slot][:, :, :],
                    in_=woT[:, ot * 128:(ot + 1) * 128].rearrange(
                        "(k p) c -> p k c", p=128))
                s.inc(lw, ch, 16)
                s.wait(ten, ch)
                s.wait(ten, "eb")
                s.wait(ten, "v")
                for kt in range(KT):
                    lastm = ten.matmul(pb[4][:, :], wsb[slot][:, kt, :],
                                       ofm_sb[:, kt, :], start=(kt == 0),
                                       stop=(kt == KT - 1))
                s.inc(lastm, "t")
                wslot2[slot] = s.mark("t")
                s.wait(sync, "v")
                last = sync.dma_start(out=xr[:, :],
                                      in_=xq[ot * 128:(ot + 1) * 128, sl])
                s.inc(last, "xd", 16)
                s.wait(vec, "xd")
                s.wait(vec, "t")
                s.wait(vec, "st")
                vec.tensor_scalar(t1[:, :], pb[4][:, :], bsb[:, ot, 0:1], None,
                                  op0=OP.add)
                vec.tensor_add(t1[:, :], t1[:, :], xr[:, :])
                s.wait(sync, "v")
                last = sync.dma_start(out=y[ot * 128:(ot + 1) * 128, sl],
                                      in_=t1[:, :])
                s.inc(last, "st", 16)
        s.wait(sync, "st")
        s.wait(gps, "st")


GLA_SAFE_O = False
GLA_SPLIT3D = False


def emit_gla(p, s, io, pref, S=4096):
    NCH = S // 128
    NG = NCH // 4
    nc = p.nc
    H, DK = 8, 64
    from contextlib import ExitStack as _ES0
    with _ES0() as _stage:
        wq = io["wq"]
        wk = io["wk"]
        wv = io["wv"]
        wqa = io["wqa"]
        wka = io["wka"]
        wga = io["wga"]
        wqb = io["wqb"]
        wkb = io["wkb"]
        wgb = io["wgb"]
        wg1a = io["wg1a"]
        wg2a = io["wg2a"]
        wg1b = io["wg1b"]
        wg2b = io["wg2b"]
        bg1r = io["bg1r"]
        bg2r = io["bg2r"]
        we = io["we"]
        normwb = io["normwb"]
        wo = io["wo"]
        trilm = io["trilm"]
        identf = io["identf"]
        blk2Tc = io["blk2Tc"]
        eselc = io["esel"]
        xfsrc = io["xfsrc"]
        y = io["y"]

        xng = p.sbx(_stage, pref + "xng", [128, KT, 512], BF16)
        xch = p.sbx(_stage, pref + "xch", [128, KT, 512], F32)
        xr = p.sbx(_stage, pref + "xr", [128, 512], F32)
        xbf = p.sbx(_stage, pref + "xbf", [128, 512], BF16)
        xsq = p.sbx(_stage, pref + "xsq", [128, 512], BF16)
        stats = p.sbx(_stage, pref + "stats", [1, 5, 512], F32)
        t1 = p.sbx(_stage, pref + "t1", [128, 512], F32)
        t2 = p.sbx(_stage, pref + "t2", [128, 512], F32)
        tb = p.sbx(_stage, pref + "tb", [128, 512], BF16)
        ones_bf = p.sbx(_stage, pref + "ones_bf", [128, 1], BF16)
        ones_f = p.sbx(_stage, pref + "ones_f", [1, 128], F32)
        ident = p.sbx(_stage, pref + "ident", [128, 128], F32)
        identb = p.sbx(_stage, pref + "identb", [128, 128], BF16)
        tril = p.sbx(_stage, pref + "tril", [128, 128], BF16)
        nwb0 = p.sbx(_stage, pref + "nwb0", [128, 512], F32)
        zer = p.sbx(_stage, pref + "zer", [128, 128], F32)
        blk2 = p.sbx(_stage, pref + "blk2", [128, 2], BF16)
        blk2T = p.sbx(_stage, pref + "blk2T", [2, 128], BF16)
        nwb = p.sbx(_stage, pref + "nwb", [1, 512], F32)
        esel = p.sbx(_stage, pref + "esel", [8, 1024], BF16)
        wq_s = p.sbx(_stage, pref + "wq_s", [128, KT, 512], BF16)
        wk_s = p.sbx(_stage, pref + "wk_s", [128, KT, 512], BF16)
        wv_s = p.sbx(_stage, pref + "wv_s", [128, KT, 512], BF16)
        wqa_s = p.sbx(_stage, pref + "wqa_s", [128, KT, 64], BF16)
        wka_s = p.sbx(_stage, pref + "wka_s", [128, KT, 64], BF16)
        wga_s = p.sbx(_stage, pref + "wga_s", [128, KT, 64], BF16)
        wqb_s = p.sbx(_stage, pref + "wqb_s", [64, 512], BF16)
        wkb_s = p.sbx(_stage, pref + "wkb_s", [64, 512], BF16)
        wgb_s = p.sbx(_stage, pref + "wgb_s", [64, 512], BF16)
        wg1a_s = p.sbx(_stage, pref + "wg1a_s", [128, KT, 16], BF16)
        wg2a_s = p.sbx(_stage, pref + "wg2a_s", [128, KT, 16], BF16)
        wg1b_s = p.sbx(_stage, pref + "wg1b_s", [16, 512], BF16)
        wg2b_s = p.sbx(_stage, pref + "wg2b_s", [16, 512], BF16)
        bg1_s = p.sbx(_stage, pref + "bg1_s", [128, 4], F32)
        bg2_s = p.sbx(_stage, pref + "bg2_s", [128, 4], F32)
        we_s = p.sbx(_stage, pref + "we_s", [128, KT, 4], BF16)
        wo_s = p.sbx(_stage, pref + "wo_s", [128, 4, D], BF16)
        pa_ = p.sbx(_stage, pref + "pa_", [128, 8, 128], BF16)
        kot = p.sbx(_stage, pref + "kot", [128, 512], BF16)
        Sbf = p.sbx(_stage, pref + "Sbf", [128, 4, 64], BF16)
        S_ = [p.sbx(_stage, pref + f"S{i}", [128, 4, 64], F32) for i in range(5)]
        rwg = p.sbx(_stage, pref + "rwg", [128, 4, 8], F32)
        tr8 = p.sbx(_stage, pref + "tr8", [8, 512], BF16)
        xl512 = p.sbx(_stage, pref + "xl512", [64, 512], BF16)

        pb = [p.psx(_stage, pref + f"pb{i}", [128, 512]) for i in range(7)]
        pbt = p.psx(_stage, pref + "pbt", [128, 512], BF16)
        OB = [pb[3], pb[0], pb[5], pb[6]]

        sync, gps, ten = p.sync, p.gpsd, p.ten
        vec, act = p.vec, p.act

        vec.memset(ones_bf[:, :], 1.0)
        vec.memset(ones_f[:, :], 1.0)
        vec.memset(blk2[:, :], 0.0)
        vec.memset(blk2[0:64, 0:1], 1.0)
        vec.memset(blk2[64:128, 1:2], 1.0)
        for i in range(5):
            vec.memset(S_[i][:, :, :], 0.0)

        def ld(dst, src, rearr=None, ch="ld1"):
            inp = src if rearr is None else src.rearrange(rearr, p=128)
            lw = gps.dma_start(out=dst, in_=inp)
            s.inc(lw, ch, 16)

        ld(wq_s[:, :, :], wq, "(k p) c -> p k c")
        ld(wk_s[:, :, :], wk, "(k p) c -> p k c")
        ld(wv_s[:, :, :], wv, "(k p) c -> p k c")
        ld(wqa_s[:, :, :], wqa, "(k p) c -> p k c")
        ld(wka_s[:, :, :], wka, "(k p) c -> p k c")
        ld(wga_s[:, :, :], wga, "(k p) c -> p k c")
        ld(wqb_s[:, :], wqb)
        ld(wkb_s[:, :], wkb)
        ld(wgb_s[:, :], wgb)
        ld(wg1a_s[:, :, :], wg1a, "(k p) c -> p k c")
        ld(wg2a_s[:, :, :], wg2a, "(k p) c -> p k c")
        ld(wg1b_s[:, :], wg1b)
        ld(wg2b_s[:, :], wg2b)
        ld(we_s[:, :, :], we, "(k p) c -> p k c")
        ld(wo_s[:, :, :], wo, "(k p) c -> p k c")
        ld(tril[:, :], trilm)
        ld(blk2T[:, :], blk2Tc)
        ld(esel[:, :], eselc)
        lw = sync.dma_start(out=bg1_s[:, :], in_=bg1r.rearrange("(k p) o -> p k o", p=128),
                            allow_slow_non_contiguous=True)
        s.inc(lw, "xd", 16)
        lw = sync.dma_start(out=bg2_s[:, :], in_=bg2r.rearrange("(k p) o -> p k o", p=128),
                            allow_slow_non_contiguous=True)
        s.inc(lw, "xd", 16)
        lw = sync.dma_start(out=ident[:, :], in_=identf)
        s.inc(lw, "xd", 16)
        lw = sync.dma_start(out=nwb[:, :], in_=normwb)
        s.inc(lw, "xd", 16)

        q1g = p.sbx(_stage, pref + "q1g", [128, 4, 512], BF16)
        k1g = p.sbx(_stage, pref + "k1g", [128, 4, 512], BF16)
        q2g = p.sbx(_stage, pref + "q2g", [128, 4, 512], BF16)
        k2g = p.sbx(_stage, pref + "k2g", [128, 4, 512], BF16)
        vg = p.sbx(_stage, pref + "vg", [128, 4, 512], BF16)
        glg = p.sbx(_stage, pref + "glg", [128, 4, 512], BF16)
        qtg = p.sbx(_stage, pref + "qtg", [128, 4, 512], BF16)
        kgg = p.sbx(_stage, pref + "kgg", [128, 4, 512], BF16)
        gk2g = p.sbx(_stage, pref + "gk2g", [128, 4, 512], F32)
        ga = p.sbx(_stage, pref + "ga", [128, 4, 512], F32)
        gb = p.sbx(_stage, pref + "gb", [128, 4, 512], F32)
        gc = p.sbx(_stage, pref + "gc", [128, 4, 512], F32)
        facg = p.sbx(_stage, pref + "facg", [128, 4, 512], BF16)
        mskg = p.sbx(_stage, pref + "mskg", [128, 4, 512], BF16)
        og = p.sbx(_stage, pref + "og", [128, 4, 512], F32)
        ofm_g = p.sbx(_stage, pref + "ofm_g", [128, 4, 512], BF16)

        s.wait(ten, "ld1")
        s.wait(ten, "xd")
        s.wait(ten, "v")
        s.wait(vec, "xd")
        vec.tensor_copy(identb[:, :], ident[:, :])
        vec.memset(zer[:, :], 0.0)
        lastm = ten.matmul(pb[5][:, :], ones_f[:, :], nwb[0:1, :],
                           start=True, stop=True)
        s.inc(lastm, "t")
        s.wait(vec, "t")
        vec.tensor_copy(nwb0[:, :], pb[5][:, :])
        s.wait(ten, "v")

        def mini_bar():
            for e in (ten, vec, act):
                for ch in ("t", "v", "a"):
                    s.wait(e, ch)

        ldeng, ldch = (gps, "xg") if xfsrc.bf16 else (sync, "xd")
        for gg in range(NG):
            gsl = slice(gg * 512, (gg + 1) * 512)
            mini_bar()
            s.wait(ldeng, "v")
            last = ldeng.dma_start(
                out=xch[:, :, :],
                in_=xfsrc.ap(slice(0, D), gsl).rearrange("(k p) t -> p k t",
                                                         p=128))
            s.inc(last, ldch, 16)
            s.wait(vec, ldch)
            s.wait(vec, "t")
            for kt in range(KT):
                vec.tensor_copy(xbf[:, :], xch[:, kt, :])
                vec.tensor_mul(xsq[:, :], xbf[:, :], xbf[:, :])
                s.wait(ten, "v")
                ten.matmul(pb[0][0:1, :], ones_bf[:, :], xbf[:, :],
                           start=(kt == 0), stop=(kt == KT - 1))
                lastm = ten.matmul(pb[3][0:1, :], ones_bf[:, :], xsq[:, :],
                                   start=(kt == 0), stop=(kt == KT - 1))
                s.inc(lastm, "t")
                s.wait(vec, "t")
            vec.tensor_scalar(stats[0:1, 2, :], pb[0][0:1, :], 1.0 / D, None,
                              op0=OP.mult)
            vec.tensor_scalar(stats[0:1, 1, :], pb[3][0:1, :], 1.0 / D, None,
                              op0=OP.mult)
            vec.tensor_mul(stats[0:1, 3, :], stats[0:1, 2, :], stats[0:1, 2, :])
            vec.tensor_sub(stats[0:1, 3, :], stats[0:1, 1, :], stats[0:1, 3, :])
            vec.tensor_scalar(stats[0:1, 3, :], stats[0:1, 3, :], EPS, None,
                              op0=OP.add)
            s.wait(act, "v")
            act.activation(stats[0:1, 4, :], stats[0:1, 3, :], AF.Ln)
            act.activation(stats[0:1, 4, :], stats[0:1, 4, :], AF.Exp,
                           scale=-0.5)
            s.wait(ten, "a")
            s.wait(ten, "v")
            ten.matmul(pb[1][:, :], ones_f[:, :], stats[0:1, 2, :], start=True,
                       stop=True)
            lastm = ten.matmul(pb[2][:, :], ones_f[:, :], stats[0:1, 4, :],
                               start=True, stop=True)
            s.inc(lastm, "t")
            s.wait(vec, "t")
            for kt in range(KT):
                vec.tensor_sub(t1[:, :], xch[:, kt, :], pb[1][:, :])
                vec.tensor_mul(xng[:, kt, :], t1[:, :], pb[2][:, :])
            pjm = [s.mark("v"), s.mark("v")]
            pji = [0]

            def pmm(dst, pairs, rows=128, scale=None, vmin=0):
                par = pji[0] % 2
                bank = pb[0] if par == 0 else pb[5]
                s.wait(ten, "v", max(pjm[par], vmin))
                n = len(pairs)
                for i, (lh, rh) in enumerate(pairs):
                    lastm = ten.matmul(bank[0:rows, :], lh, rh, start=(i == 0),
                                       stop=(i == n - 1), skip_group_check=True)
                s.inc(lastm, "t")
                s.wait(vec, "t")
                if scale is not None:
                    vec.tensor_scalar(dst, bank[0:rows, :], scale, None,
                                      op0=OP.mult)
                else:
                    vec.tensor_copy(dst, bank[0:rows, :])
                pjm[par] = s.mark("v")
                pji[0] += 1
                return pjm[par]

            for rt in range(4):
                pmm(q1g[:, rt, :],
                    [(wq_s[:, kt, rt * 128:(rt + 1) * 128], xng[:, kt, :])
                     for kt in range(KT)], scale=0.125)
            for rt in range(4):
                pmm(k1g[:, rt, :],
                    [(wk_s[:, kt, rt * 128:(rt + 1) * 128], xng[:, kt, :])
                     for kt in range(KT)])
            for c4 in range(4):
                csl = slice(c4 * 128, (c4 + 1) * 128)
                pmm(vg[:, c4, :],
                    [(xng[:, kt, csl], wv_s[:, kt, :]) for kt in range(KT)])
            vm = pmm(xl512[0:64, :],
                     [(wqa_s[:, kt, :], xng[:, kt, :]) for kt in range(KT)],
                     rows=64)
            for rt in range(4):
                pmm(q2g[:, rt, :],
                    [(wqb_s[:, rt * 128:(rt + 1) * 128], xl512[0:64, :])],
                    scale=0.125, vmin=vm)
            pmm(xl512[0:64, :],
                [(wka_s[:, kt, :], xng[:, kt, :]) for kt in range(KT)],
                rows=64)
            for rt in range(4):
                s.wait(ten, "v")
                s.wait(ten, "a")
                lastm = ten.matmul(pb[1][:, :], wkb_s[:, rt * 128:(rt + 1) * 128],
                                   xl512[0:64, :], start=True, stop=True,
                                   skip_group_check=True)
                s.inc(lastm, "t")
                s.wait(act, "t")
                s.wait(act, "v")
                act.activation(t1[:, :], pb[1][:, :], AF.Exp)
                s.wait(vec, "a")
                vec.tensor_copy(tb[:, :], t1[:, :])
                s.wait(ten, "v")
                lastm = ten.matmul(pb[4][0:2, :], blk2[:, :], tb[:, :],
                                   start=True, stop=True, skip_group_check=True)
                s.inc(lastm, "t")
                s.wait(vec, "t")
                vec.reciprocal(t2[0:2, :], pb[4][0:2, :])
                vec.tensor_copy(xsq[0:2, :], t2[0:2, :])
                s.wait(ten, "v")
                lastm = ten.matmul(pb[4][:, :], blk2T[:, :], xsq[0:2, :],
                                   start=True, stop=True, skip_group_check=True)
                s.inc(lastm, "t")
                s.wait(vec, "t")
                vec.tensor_mul(k2g[:, rt, :], t1[:, :], pb[4][:, :])
            for (ga_w, gb_w, bgs, dst) in ((wg1a_s, wg1b_s, bg1_s, ga),
                                           (wg2a_s, wg2b_s, bg2_s, gk2g)):
                pmm(xl512[0:16, :],
                    [(ga_w[:, kt, :], xng[:, kt, :]) for kt in range(KT)],
                    rows=16)
                for rt in range(4):
                    s.wait(ten, "v")
                    s.wait(ten, "a")
                    lastm = ten.matmul(pb[2][:, :],
                                       gb_w[:, rt * 128:(rt + 1) * 128],
                                       xl512[0:16, :], start=True, stop=True,
                                       skip_group_check=True)
                    s.inc(lastm, "t")
                    s.wait(vec, "t")
                    vec.tensor_scalar(t1[:, :], pb[2][:, :], bgs[:, rt:rt + 1],
                                      None, op0=OP.add)
                    s.wait(act, "v")
                    act.activation(t2[:, :], t1[:, :], AF.Abs)
                    act.activation(t2[:, :], t2[:, :], AF.Exp, scale=-1.0)
                    s.wait(vec, "a")
                    vec.tensor_scalar(t2[:, :], t2[:, :], 1.0, None, op0=OP.add)
                    s.wait(act, "v")
                    act.activation(t2[:, :], t2[:, :], AF.Ln)
                    s.wait(vec, "a")
                    vec.tensor_scalar_min(t1[:, :], t1[:, :], 0.0)
                    vec.tensor_sub(t1[:, :], t1[:, :], t2[:, :])
                    vec.tensor_scalar(dst[:, rt, :], t1[:, :], 1.0 / 16.0, None,
                                      op0=OP.mult)
            pmm(xl512[0:64, :],
                [(wga_s[:, kt, :], xng[:, kt, :]) for kt in range(KT)],
                rows=64)
            for c4 in range(4):
                s.wait(ten, "v")
                s.wait(ten, "a")
                lastm = ten.matmul(pb[1][:, :], xl512[0:64, c4 * 128:(c4 + 1) * 128],
                                   wgb_s[:, :], start=True, stop=True,
                                   skip_group_check=True)
                s.inc(lastm, "t")
                s.wait(act, "t")
                s.wait(act, "v")
                act.activation(glg[:, c4, :], pb[1][:, :], AF.Silu)
            for c4 in range(4):
                csl = slice(c4 * 128, (c4 + 1) * 128)
                bank = pb[3] if c4 % 2 == 0 else pb[6]
                s.wait(ten, "v")
                s.wait(ten, "a")
                for kt in range(KT):
                    lastm = ten.matmul(bank[:, 0:4], xng[:, kt, csl],
                                       we_s[:, kt, :], start=(kt == 0),
                                       stop=(kt == KT - 1), skip_group_check=True)
                s.inc(lastm, "t")
                s.wait(act, "t")
                act.activation(t1[:, 0:4], bank[:, 0:4], AF.Exp)
                s.wait(vec, "a")
                vec.reduce_sum(t1[:, 4:5], t1[:, 0:4], axis=AX.X)
                vec.reciprocal(t1[:, 5:6], t1[:, 4:5])
                vec.tensor_scalar(t1[:, 0:4], t1[:, 0:4], t1[:, 5:6], None,
                                  op0=OP.mult)
                vec.reduce_max(t1[:, 6:7], t1[:, 0:4], axis=AX.X)
                vec.tensor_scalar(rwg[:, c4, 4:8], t1[:, 0:4], t1[:, 6:7], None,
                                  op0=OP.is_ge)
                vec.tensor_mul(rwg[:, c4, 0:4], t1[:, 0:4], rwg[:, c4, 4:8])
            for c4 in range(4):
                s.wait(ten, "v")
                lastm = ten.transpose(pb[4][0:8, 0:128], rwg[:, c4, :],
                                      ident[:, :])
                s.inc(lastm, "t")
                s.wait(vec, "t")
                vec.tensor_copy(tr8[0:8, c4 * 128:(c4 + 1) * 128],
                                pb[4][0:8, 0:128])
            vm = s.mark("v")
            for e in range(4):
                pmm(facg[:, e, :], [(esel[0:8, e * 128:(e + 1) * 128],
                                     tr8[0:8, :])], vmin=vm)
                pmm(mskg[:, e, :], [(esel[0:8, (4 + e) * 128:(5 + e) * 128],
                                     tr8[0:8, :])], vmin=vm)

            for inst in range(5):
                e = inst - 1
                s.wait(vec, "t")
                s.wait(vec, "a")
                if inst == 0:
                    pass
                else:
                    if GLA_SPLIT3D:
                        for rt in range(4):
                            vec.tensor_mul(ga[:, rt, :], gk2g[:, rt, :],
                                           mskg[:, e, :])
                    else:
                        vec.tensor_mul(ga[:, :, :], gk2g[:, :, :],
                                       mskg[:, e, :].unsqueeze(1)
                                       .to_broadcast([128, 4, 512]))
                for rt in range(4):
                    for c4 in range(4):
                        vec.tensor_tensor_scan(
                            gb[:, rt, c4 * 128:(c4 + 1) * 128], zer[:, :],
                            ga[:, rt, c4 * 128:(c4 + 1) * 128], 0.0,
                            op0=OP.add, op1=OP.add)
                s.wait(act, "v")
                if GLA_SPLIT3D:
                    for rt in range(4):
                        act.activation(gc[:, rt, :], gb[:, rt, :], AF.Exp,
                                       scale=-1.0)
                        act.activation(ga[:, rt, :], gb[:, rt, :], AF.Exp)
                else:
                    act.activation(gc[:, :, :], gb[:, :, :], AF.Exp, scale=-1.0)
                    act.activation(ga[:, :, :], gb[:, :, :], AF.Exp)
                s.wait(vec, "a")
                if GLA_SPLIT3D:
                    for rt in range(4):
                        if inst == 0:
                            vec.tensor_mul(qtg[:, rt, :], q1g[:, rt, :],
                                           ga[:, rt, :])
                            vec.tensor_mul(kgg[:, rt, :], k1g[:, rt, :],
                                           gc[:, rt, :])
                        else:
                            vec.tensor_mul(qtg[:, rt, :], q2g[:, rt, :],
                                           ga[:, rt, :])
                            vec.tensor_mul(qtg[:, rt, :], qtg[:, rt, :],
                                           facg[:, e, :])
                            vec.tensor_mul(kgg[:, rt, :], k2g[:, rt, :],
                                           gc[:, rt, :])
                            vec.tensor_mul(kgg[:, rt, :], kgg[:, rt, :],
                                           facg[:, e, :])
                elif inst == 0:
                    vec.tensor_mul(qtg[:, :, :], q1g[:, :, :], ga[:, :, :])
                    vec.tensor_mul(kgg[:, :, :], k1g[:, :, :], gc[:, :, :])
                else:
                    fb = facg[:, e, :].unsqueeze(1).to_broadcast([128, 4, 512])
                    vec.tensor_mul(qtg[:, :, :], q2g[:, :, :], ga[:, :, :])
                    vec.tensor_mul(qtg[:, :, :], qtg[:, :, :], fb)
                    vec.tensor_mul(kgg[:, :, :], k2g[:, :, :], gc[:, :, :])
                    vec.tensor_mul(kgg[:, :, :], kgg[:, :, :], fb)
                for c4 in range(4):
                    cof = c4 * 128
                    vec.tensor_copy(Sbf[:, :, :], S_[inst][:, :, :])
                    s.wait(ten, "v")
                    _ord = (0, 2, 4, 6, 1, 3, 5, 7)
                    for i in range(8):
                        hi = _ord[i]
                        rt, rr = hi // 2, 64 * (hi % 2)
                        ten.matmul(OB[hi // 2][:, rr:rr + 64],
                                   qtg[rr:rr + 64, rt, cof:cof + 128],
                                   Sbf[rr:rr + 64, rt, :],
                                   start=(hi % 2 == 0), stop=False,
                                   skip_group_check=True)
                        h = i
                        rt, rr = h // 2, 64 * (h % 2)
                        bank = pb[1] if h < 4 else pb[2]
                        lastm = ten.matmul(
                            bank[:, (h % 4) * 128:(h % 4 + 1) * 128],
                            kgg[rr:rr + 64, rt, cof:cof + 128],
                            qtg[rr:rr + 64, rt, cof:cof + 128],
                            start=(h % 4 == 0), stop=(h % 4 == 3),
                            skip_group_check=True)
                        s.inc(lastm, "t")
                        s.wait(ten, "t")
                    s.wait(vec, "t")
                    vec.tensor_mul(pa_[:, 0:4, :],
                                   pb[1][:, :].rearrange("p (h t) -> p h t", h=4),
                                   tril[:, :].unsqueeze(1)
                                   .to_broadcast([128, 4, 128]))
                    vec.tensor_mul(pa_[:, 4:8, :],
                                   pb[2][:, :].rearrange("p (h t) -> p h t", h=4),
                                   tril[:, :].unsqueeze(1)
                                   .to_broadcast([128, 4, 128]))
                    for rt in range(4):
                        s.wait(ten, "v")
                        lastm = ten.transpose(pbt[:, rt * 128:(rt + 1) * 128],
                                              kgg[:, rt, cof:cof + 128],
                                              identb[:, :])
                        s.inc(lastm, "t")
                        s.wait(vec, "t")
                        vec.tensor_copy(kot[:, rt * 128:(rt + 1) * 128],
                                        pbt[:, rt * 128:(rt + 1) * 128])
                    s.wait(ten, "v")
                    for i in range(8):
                        hi = _ord[i]
                        lastm = ten.matmul(OB[hi // 2][:, 64 * (hi % 2):
                                              64 * (hi % 2) + 64],
                                           pa_[:, hi, :],
                                           vg[:, c4, hi * 64:(hi + 1) * 64],
                                           start=False, stop=(hi % 2 == 1),
                                           skip_group_check=True)
                        h = i
                        rt, rr = h // 2, 64 * (h % 2)
                        lastm = ten.matmul(pb[4][rr:rr + 64, rt * 64:(rt + 1) * 64],
                                           kot[:, h * 64:(h + 1) * 64],
                                           vg[:, c4, h * 64:(h + 1) * 64],
                                           start=(h < 2), stop=(h >= H - 2),
                                           skip_group_check=True)
                        s.inc(lastm, "t")
                        s.wait(ten, "t")
                    o_mark = s.mark("t")
                    s.wait(vec, "t", o_mark)
                    for b in range(4):
                        if inst == 0:
                            vec.tensor_copy(og[:, c4, b * 128:(b + 1) * 128],
                                            OB[b][:, 0:128])
                        else:
                            vec.tensor_add(og[:, c4, b * 128:(b + 1) * 128],
                                           og[:, c4, b * 128:(b + 1) * 128],
                                           OB[b][:, 0:128])
                    s.wait(vec, "t")
                    for rt in range(4):
                        vec.tensor_add(t1[:, rt * 64:(rt + 1) * 64],
                                       S_[inst][:, rt, :],
                                       pb[4][:, rt * 64:(rt + 1) * 64])
                    for rt in range(4):
                        vec.tensor_scalar(S_[inst][:, rt, :],
                                          t1[:, rt * 64:(rt + 1) * 64],
                                          ga[:, rt, cof + 127:cof + 128], None,
                                          op0=OP.mult)
                    s.wait(ten, "v")

            mini_bar()
            if GLA_SPLIT3D:
                for c4 in range(4):
                    vec.tensor_mul(gb[:, c4, :], og[:, c4, :], og[:, c4, :])
            else:
                vec.tensor_mul(gb[:, :, :], og[:, :, :], og[:, :, :])
            for c4 in range(4):
                vec.reduce_sum(t1[:, c4 * 8:(c4 + 1) * 8],
                               gb[:, c4, :].rearrange("p (h d) -> p h d", h=8),
                               axis=AX.X)
            vec.tensor_scalar(t1[:, 0:32], t1[:, 0:32], 1.0 / 64, None,
                              op0=OP.mult)
            vec.tensor_scalar(t1[:, 0:32], t1[:, 0:32], EPS, None, op0=OP.add)
            s.wait(act, "v")
            act.activation(t1[:, 32:64], t1[:, 0:32], AF.Ln)
            act.activation(t1[:, 32:64], t1[:, 32:64], AF.Exp, scale=-0.5)
            s.wait(vec, "a")
            for c4 in range(4):
                ogv = og[:, c4, :].rearrange("p (h d) -> p h d", h=8)
                vec.tensor_mul(ogv, ogv,
                               t1[:, 32 + c4 * 8:32 + (c4 + 1) * 8]
                               .rearrange("p (h o) -> p h o", h=8)
                               .to_broadcast([128, 8, 64]))
                vec.tensor_mul(og[:, c4, :], og[:, c4, :], nwb0[:, :])
                s.wait(vec, "t")
                vec.tensor_mul(tb[:, :], og[:, c4, :], glg[:, c4, :])
                for rt in range(4):
                    s.wait(ten, "v")
                    lastm = ten.transpose(pbt[:, rt * 128:(rt + 1) * 128],
                                          tb[:, rt * 128:(rt + 1) * 128],
                                          identb[:, :])
                    s.inc(lastm, "t")
                    s.wait(vec, "t")
                    vec.tensor_copy(ofm_g[:, rt, c4 * 128:(c4 + 1) * 128],
                                    pbt[:, rt * 128:(rt + 1) * 128])
            wom = [s.mark("v"), s.mark("v")]
            for ot in range(KT):
                par = ot % 2
                bank = pb[3] if par == 0 else pb[6]
                s.wait(ten, "v", wom[par])
                for rt in range(4):
                    lastm = ten.matmul(bank[:, :],
                                       wo_s[:, rt, ot * 128:(ot + 1) * 128],
                                       ofm_g[:, rt, :], start=(rt == 0),
                                       stop=(rt == 3), skip_group_check=True)
                s.inc(lastm, "t")
                xeng, xch_ = (gps, "xg") if xfsrc.bf16 else (sync, "xd")
                s.wait(xeng, "v")
                last = xeng.dma_start(
                    out=xr[:, :],
                    in_=xfsrc.ap(slice(ot * 128, (ot + 1) * 128), gsl))
                s.inc(last, xch_, 16)
                s.wait(vec, xch_)
                s.wait(vec, "t")
                s.wait(vec, "st")
                vec.tensor_add(t2[:, :], bank[:, :], xr[:, :])
                wom[par] = s.mark("v")
                s.wait(sync, "v")
                last = sync.dma_start(out=y[ot * 128:(ot + 1) * 128, gsl],
                                      in_=t2[:, :])
                s.inc(last, "st", 16)
        s.wait(sync, "st")
        s.wait(gps, "st")


def tile_wT(wT, IT_out=None):
    ind, outd = wT.shape
    t = wT.reshape(ind // 128, 128, outd // 128, 128)
    return np.ascontiguousarray(t.transpose(2, 0, 1, 3))



def host_moe_inputs(inp, prefix="L0_ffn"):
    Wg_s = inp[f"{prefix}_Wg_s"]; Wu_s = inp[f"{prefix}_Wu_s"]; Wd_s = inp[f"{prefix}_Wd_s"]
    Wg = inp[f"{prefix}_Wg"]; Wu = inp[f"{prefix}_Wu"]; Wd = inp[f"{prefix}_Wd"]
    Wr = inp[f"{prefix}_Wr"]; rb = inp[f"{prefix}_rb"]
    wg = np.stack([tile_wT(np.ascontiguousarray(w.T))
                   for w in [Wg_s] + list(Wg)]).astype(np.float32)
    wu = np.stack([tile_wT(np.ascontiguousarray(w.T))
                   for w in [Wu_s] + list(Wu)]).astype(np.float32)
    wdr = np.stack([tile_wT(np.ascontiguousarray(w.T))
                    for w in [Wd_s] + list(Wd)]).astype(np.float32)
    return {
        "wg": wg, "wu": wu, "wdr": wdr,
        "wr": np.ascontiguousarray(Wr.T).astype(np.float32),
        "rbv": np.asarray(rb).reshape(1, 8).astype(np.float32),
        "identf": np.eye(128, dtype=np.float32),
    }



def host_attn_a_inputs(inp):
    PHI = (1.0 + np.sqrt(5.0)) / 2.0
    S = 4096
    pos = (np.arange(S, dtype=np.float64) * PHI) % 1.0
    dist = np.abs(pos[:, None] - pos[None, :])
    exb_full = (np.exp(-dist) + 1e-8).astype(np.float32)
    import ml_dtypes
    return {
        "wqkv": np.ascontiguousarray(inp["L0_attn_a_w_qkv"]).astype(np.float32),
        "wproj": np.ascontiguousarray(inp["L0_attn_a_w_proj"]).astype(np.float32),
        "bproj": np.asarray(inp["L0_attn_a_b_proj"]).reshape(D, 1).astype(np.float32),
    }, exb_full




def build_fused(nstages=8):
    pass_lite = (nstages == 80)
    p = P("fused")
    nc = p.nc
    with p.ctx:
        E, IT = 9, 16
        dt = nc.dram_tensor
        x_own = dt("x_own", [D, 2048], F32, kind="ExternalInput").ap()
        a0_wqkv = dt("a0_wqkv", [D, 3072], F32, kind="ExternalInput").ap()
        a0_wproj = dt("a0_wproj", [D, D], F32, kind="ExternalInput").ap()
        a0_bproj = dt("a0_bproj", [D, 1], F32, kind="ExternalInput").ap()
        a0_exb = dt("a0_exb", [4096, 2048], BF16, kind="ExternalInput").ap()
        m_wg = dt("m_wg", [E, IT, KT, 128, 128], F32, kind="ExternalInput").ap()
        m_wu = dt("m_wu", [E, IT, KT, 128, 128], F32, kind="ExternalInput").ap()
        m_wdr = dt("m_wdr", [E, KT, IT, 128, 128], F32, kind="ExternalInput").ap()
        m_wr = dt("m_wr", [D, 8], F32, kind="ExternalInput").ap()
        m_rbv = dt("m_rbv", [1, 8], F32, kind="ExternalInput").ap()
        identf = dt("identf", [128, 128], F32, kind="ExternalInput").ap()
        blk2Tc = dt("blk2Tc", [2, 128], F32, kind="ExternalInput").ap()
        trilm = dt("trilm", [128, 128], BF16, kind="ExternalInput").ap()
        b0_names = ["wqw", "wkw", "wvw", "wow"]
        b0w = {n: dt("b0_" + n, [D, D], F32, kind="ExternalInput").ap() for n in b0_names}
        b0_wqb = dt("b0_wqb", [1, D], F32, kind="ExternalInput").ap()
        b0_wkb = dt("b0_wkb", [1, D], F32, kind="ExternalInput").ap()
        b0_wvb = dt("b0_wvb", [1, D], F32, kind="ExternalInput").ap()
        b0_wob = dt("b0_wob", [D, 1], F32, kind="ExternalInput").ap()
        b0_wqiw = dt("b0_wqiw", [D, 64], F32, kind="ExternalInput").ap()
        b0_wkiw = dt("b0_wkiw", [D, 64], F32, kind="ExternalInput").ap()
        b0_wqib = dt("b0_wqib", [64, 1], F32, kind="ExternalInput").ap()
        b0_wkib = dt("b0_wkib", [64, 1], F32, kind="ExternalInput").ap()
        a1_wq = dt("a1_wq", [D, 64], F32, kind="ExternalInput").ap()
        a1_wk = dt("a1_wk", [D, 64], F32, kind="ExternalInput").ap()
        a1_wv = dt("a1_wv", [D, D], F32, kind="ExternalInput").ap()
        a1_cm = dt("a1_cm", [4096, 2048], BF16, kind="ExternalInput").ap()
        b1_wkT = dt("b1_wkT", [D, D], F32, kind="ExternalInput").ap()
        b1_wqT = dt("b1_wqT", [D, D], F32, kind="ExternalInput").ap()
        b1_wvT = dt("b1_wvT", [D, D], F32, kind="ExternalInput").ap()
        b1_woT = dt("b1_woT", [D, D], F32, kind="ExternalInput").ap()
        b1_bo = dt("b1_bo", [D, 1], F32, kind="ExternalInput").ap()
        b1_klng2 = dt("b1_klng2", [128, 1], F32, kind="ExternalInput").ap()
        b1_klnb2 = dt("b1_klnb2", [128, 1], F32, kind="ExternalInput").ap()
        b1_qlng2 = dt("b1_qlng2", [128, 1], F32, kind="ExternalInput").ap()
        b1_qlnb2 = dt("b1_qlnb2", [128, 1], F32, kind="ExternalInput").ap()
        b1_cm = dt("b1_cm", [4096, 4096], BF16, kind="ExternalInput").ap()
        g_wq = dt("g_wq", [D, 512], F32, kind="ExternalInput").ap()
        g_wk = dt("g_wk", [D, 512], F32, kind="ExternalInput").ap()
        g_wv = dt("g_wv", [D, 512], F32, kind="ExternalInput").ap()
        g_wqa = dt("g_wqa", [D, 64], F32, kind="ExternalInput").ap()
        g_wka = dt("g_wka", [D, 64], F32, kind="ExternalInput").ap()
        g_wga = dt("g_wga", [D, 64], F32, kind="ExternalInput").ap()
        g_wqb = dt("g_wqb", [64, 512], F32, kind="ExternalInput").ap()
        g_wkb = dt("g_wkb", [64, 512], F32, kind="ExternalInput").ap()
        g_wgb = dt("g_wgb", [64, 512], F32, kind="ExternalInput").ap()
        g_wg1a = dt("g_wg1a", [D, 16], F32, kind="ExternalInput").ap()
        g_wg2a = dt("g_wg2a", [D, 16], F32, kind="ExternalInput").ap()
        g_wg1b = dt("g_wg1b", [16, 512], F32, kind="ExternalInput").ap()
        g_wg2b = dt("g_wg2b", [16, 512], F32, kind="ExternalInput").ap()
        g_bg1r = dt("g_bg1r", [512, 1], F32, kind="ExternalInput").ap()
        g_bg2r = dt("g_bg2r", [512, 1], F32, kind="ExternalInput").ap()
        g_we = dt("g_we", [D, 4], F32, kind="ExternalInput").ap()
        g_normwb = dt("g_normwb", [1, 512], F32, kind="ExternalInput").ap()
        g_esel = dt("g_esel", [8, 1024], F32, kind="ExternalInput").ap()
        g_wo = dt("g_wo", [512, D], F32, kind="ExternalInput").ap()
        yout = dt("yout", [D, 4096], F32, kind="ExternalOutput").ap()
        xs = [dt(f"xs{i}", [D, 2048], F32).ap() for i in range(6)]
        y6 = dt("y6", [D, 4096], F32).ap()
        y7 = dt("y7", [D, 4096], F32).ap()
        b0_odram = dt("b0_odram", [D, 2048], BF16).ap()
        b1_odram = dt("b1_odram", [D, 4096], BF16).ap()
        agin = [dt(f"agin{i}", [D, 2048], BF16).ap() for i in range(4)]
        agout = [dt(f"agout{i}", [2 * D, 2048], BF16).ap() for i in range(4)]

        s = p.engines(["xd", "xg", "wl0", "wl1", "wd0", "wd1", "wr0", "eb",
                       "st", "ld1", "cc"])

        emit_ag(p, s, x_own, agin[0], agout[0])
        stage_barrier(p, s)
        io = {"xf": agout[0], "xq": x_own, "xfsrc": AgSrc(agout[0], 2048),
              "wqkv": a0_wqkv, "wproj": a0_wproj, "bproj": a0_bproj,
              "exb": a0_exb, "y": xs[1]}
        with nc.named_scope("st1_a0"):
            emit_a0(p, s, io, "a0_")
        stage_barrier(p, s)
        if nstages < 2:
            s.wait(p.sync, "st"); s.wait(p.gpsd, "st"); return nc
        io = {"x": xs[1], "wg": m_wg, "wu": m_wu, "wdr": m_wdr, "wr": m_wr,
              "rbv": m_rbv, "identf": identf, "y": xs[2]}
        with nc.named_scope("st2_moe"):
            emit_moe(p, s, io, "m1_")
        stage_barrier(p, s)
        if nstages < 3:
            s.wait(p.sync, "st"); s.wait(p.gpsd, "st"); return nc
        emit_ag(p, s, xs[2], agin[1], agout[1])
        stage_barrier(p, s)
        io = {"xf": agout[1], "xq": xs[2], "xfsrc": AgSrc(agout[1], 2048),
              "odram": b0_odram, "identf": identf, "y": xs[3],
              "wqb": b0_wqb, "wkb": b0_wkb, "wvb": b0_wvb, "wob": b0_wob,
              "wqiw": b0_wqiw, "wkiw": b0_wkiw, "wqib": b0_wqib,
              "wkib": b0_wkib}
    
        io.update(b0w)
        with nc.named_scope("st3_b0"):
            emit_b0(p, s, io, "b0_")
        stage_barrier(p, s)
        if nstages < 4:
            s.wait(p.sync, "st"); s.wait(p.gpsd, "st"); return nc
        io = {"x": xs[3], "wg": m_wg, "wu": m_wu, "wdr": m_wdr, "wr": m_wr,
              "rbv": m_rbv, "identf": identf, "y": xs[4]}
        with nc.named_scope("st4_moe"):
            emit_moe(p, s, io, "m2_")
        stage_barrier(p, s)
        if nstages < 5:
            s.wait(p.sync, "st"); s.wait(p.gpsd, "st"); return nc
        emit_ag(p, s, xs[4], agin[2], agout[2])
        stage_barrier(p, s)
        io = {"xf": agout[2], "xq": xs[4], "xfsrc": AgSrc(agout[2], 2048),
              "wq": a1_wq, "wk": a1_wk, "wv": a1_wv, "cm": a1_cm, "y": xs[5]}
        with nc.named_scope("st5_a1"):
            emit_a1(p, s, io, "a1_")
        stage_barrier(p, s)
        if nstages < 6:
            s.wait(p.sync, "st"); s.wait(p.gpsd, "st"); return nc
        emit_ag(p, s, xs[5], agin[3], agout[3])
        stage_barrier(p, s)
        io = {"xf": agout[3], "xfsrc": AgSrc(agout[3], 2048),
              "wq": g_wq, "wk": g_wk, "wv": g_wv, "wqa": g_wqa, "wka": g_wka,
              "wga": g_wga, "wqb": g_wqb, "wkb": g_wkb, "wgb": g_wgb,
              "wg1a": g_wg1a, "wg2a": g_wg2a, "wg1b": g_wg1b, "wg2b": g_wg2b,
              "bg1r": g_bg1r, "bg2r": g_bg2r, "we": g_we, "normwb": g_normwb,
              "wo": g_wo, "trilm": trilm, "identf": identf, "blk2Tc": blk2Tc,
              "esel": g_esel, "y": y6}
    
        with nc.named_scope("st6_gla"):
            emit_gla(p, s, io, "g1_")
        stage_barrier(p, s)
        if nstages == 60:
            lw = p.gpsd.dma_start(out=yout, in_=y6)
            s.inc(lw, "st", 16)
            s.wait(p.sync, "st"); s.wait(p.gpsd, "st"); return nc
        if nstages < 7:
            s.wait(p.sync, "st"); s.wait(p.gpsd, "st"); return nc
        io = {"xf": y6, "xq": y6, "xfsrc": FlatSrc(y6),
              "wkT": b1_wkT, "wqT": b1_wqT, "wvT": b1_wvT, "woT": b1_woT,
              "bo": b1_bo, "klng2": b1_klng2, "klnb2": b1_klnb2,
              "qlng2": b1_qlng2, "qlnb2": b1_qlnb2, "cm": b1_cm,
              "blk2Tc": blk2Tc, "odram": b1_odram, "y": y7}
        with nc.named_scope("st7_b1"):
            emit_b1(p, s, io, "b1_")
        stage_barrier(p, s)
        if nstages < 8 and nstages != 81:
            s.wait(p.sync, "st"); s.wait(p.gpsd, "st"); return nc
        if nstages == 80:
            lw = p.gpsd.dma_start(out=yout, in_=y7)
            s.inc(lw, "st", 16)
            s.wait(p.sync, "st"); s.wait(p.gpsd, "st"); return nc
        y8 = dt("y8", [D, 4096], F32).ap()
        io = {"xf": y7, "xfsrc": FlatSrc(y7),
              "wq": g_wq, "wk": g_wk, "wv": g_wv, "wqa": g_wqa, "wka": g_wka,
              "wga": g_wga, "wqb": g_wqb, "wkb": g_wkb, "wgb": g_wgb,
              "wg1a": g_wg1a, "wg2a": g_wg2a, "wg1b": g_wg1b, "wg2b": g_wg2b,
              "bg1r": g_bg1r, "bg2r": g_bg2r, "we": g_we, "normwb": g_normwb,
              "wo": g_wo, "trilm": trilm, "identf": identf, "blk2Tc": blk2Tc,
              "esel": g_esel, "y": y8}
        with nc.named_scope("st8_gla"):
            emit_gla(p, s, io, "g2_")
        stage_barrier(p, s)
        lw = p.gpsd.dma_start(out=yout, in_=y8)
        s.inc(lw, "st", 16)
        s.wait(p.sync, "st")
        s.wait(p.gpsd, "st")
    return nc


import ml_dtypes

_FCACHE = {}


def _fprog():
    if "nc" not in _FCACHE:
        _FCACHE["nc"] = build_fused(nstages=8)
    return _FCACHE["nc"]


def kernel(**inp):
    inp = {k: np.asarray(v) for k, v in inp.items()}
    x0 = inp["L0_attn_a_x"].astype(np.float32)
    B, Sn, Dm = x0.shape
    Q = 2048
    wa0 = {
        "a0_wqkv": np.ascontiguousarray(inp["L0_attn_a_w_qkv"]).astype(np.float32),
        "a0_wproj": np.ascontiguousarray(inp["L0_attn_a_w_proj"]).astype(np.float32),
        "a0_bproj": inp["L0_attn_a_b_proj"].reshape(D, 1).astype(np.float32),
    }
    PHI = (1.0 + np.sqrt(5.0)) / 2.0
    pos = (np.arange(Sn, dtype=np.float64) * PHI) % 1.0
    dist = np.abs(pos[:, None] - pos[None, :])
    exbf = (np.exp(-dist) + 1e-8).astype(np.float32)
    wmoe = host_moe_inputs(inp, "L0_ffn")
    ident = np.eye(128, dtype=np.float32)
    blk2Tc = np.kron(np.eye(2), np.ones((1, 64))).astype(np.float32)
    trilc = (np.arange(128)[None, :] >= np.arange(128)[:, None]).astype(ml_dtypes.bfloat16)
    cm_full = (np.arange(Sn)[:, None] <= np.arange(Sn)[None, :])
    wshared = {
        "m_wg": wmoe["wg"], "m_wu": wmoe["wu"], "m_wdr": wmoe["wdr"],
        "m_wr": wmoe["wr"], "m_rbv": wmoe["rbv"],
        "identf": ident, "blk2Tc": blk2Tc, "trilm": trilc,
        "b0_wqw": np.ascontiguousarray(inp["L0_attn_b_wq_w"]).astype(np.float32),
        "b0_wkw": np.ascontiguousarray(inp["L0_attn_b_wk_w"]).astype(np.float32),
        "b0_wvw": np.ascontiguousarray(inp["L0_attn_b_wv_w"]).astype(np.float32),
        "b0_wow": np.ascontiguousarray(inp["L0_attn_b_wo_w"]).astype(np.float32),
        "b0_wqb": inp["L0_attn_b_wq_b"].reshape(1, D).astype(np.float32),
        "b0_wkb": inp["L0_attn_b_wk_b"].reshape(1, D).astype(np.float32),
        "b0_wvb": inp["L0_attn_b_wv_b"].reshape(1, D).astype(np.float32),
        "b0_wob": inp["L0_attn_b_wo_b"].reshape(D, 1).astype(np.float32),
        "b0_wqiw": np.ascontiguousarray(inp["L0_attn_b_wqi_w"]).astype(np.float32),
        "b0_wkiw": np.ascontiguousarray(inp["L0_attn_b_wki_w"]).astype(np.float32),
        "b0_wqib": inp["L0_attn_b_wqi_b"].reshape(64, 1).astype(np.float32),
        "b0_wkib": inp["L0_attn_b_wki_b"].reshape(64, 1).astype(np.float32),
        "a1_wq": np.ascontiguousarray(inp["L1_attn_a_Wq"]).astype(np.float32),
        "a1_wk": np.ascontiguousarray(inp["L1_attn_a_Wk"]).astype(np.float32),
        "a1_wv": np.ascontiguousarray(inp["L1_attn_a_Wv"]).astype(np.float32),
        "b1_wkT": np.ascontiguousarray(inp["L1_attn_b_Wk"].T).astype(np.float32),
        "b1_wqT": np.ascontiguousarray(inp["L1_attn_b_Wq"].T).astype(np.float32),
        "b1_wvT": np.ascontiguousarray(inp["L1_attn_b_Wv"].T).astype(np.float32),
        "b1_woT": np.ascontiguousarray(inp["L1_attn_b_Wo"].T).astype(np.float32),
        "b1_bo": inp["L1_attn_b_bo"].reshape(D, 1).astype(np.float32),
        "b1_klng2": np.tile(inp["L1_attn_b_kln_g"].astype(np.float32), 2).reshape(128, 1),
        "b1_klnb2": np.tile(inp["L1_attn_b_kln_b"].astype(np.float32), 2).reshape(128, 1),
        "b1_qlng2": np.tile(inp["L1_attn_b_qln_g"].astype(np.float32), 2).reshape(128, 1),
        "b1_qlnb2": np.tile(inp["L1_attn_b_qln_b"].astype(np.float32), 2).reshape(128, 1),
        "b1_cm": np.ascontiguousarray(cm_full).astype(ml_dtypes.bfloat16),
        "g_wq": np.ascontiguousarray(inp["L1_ffn_Wq"]).astype(np.float32),
        "g_wk": np.ascontiguousarray(inp["L1_ffn_Wk"]).astype(np.float32),
        "g_wv": np.ascontiguousarray(inp["L1_ffn_Wv"]).astype(np.float32),
        "g_wqa": np.ascontiguousarray(inp["L1_ffn_Wqa"]).astype(np.float32),
        "g_wka": np.ascontiguousarray(inp["L1_ffn_Wka"]).astype(np.float32),
        "g_wga": np.ascontiguousarray(inp["L1_ffn_Wga"]).astype(np.float32),
        "g_wqb": np.ascontiguousarray(inp["L1_ffn_Wqb"]).astype(np.float32),
        "g_wkb": np.ascontiguousarray(inp["L1_ffn_Wkb"]).astype(np.float32),
        "g_wgb": np.ascontiguousarray(inp["L1_ffn_Wgb"]).astype(np.float32),
        "g_wg1a": np.ascontiguousarray(inp["L1_ffn_Wg1a"]).astype(np.float32),
        "g_wg2a": np.ascontiguousarray(inp["L1_ffn_Wg2a"]).astype(np.float32),
        "g_wg1b": np.ascontiguousarray(inp["L1_ffn_Wg1b"]).astype(np.float32),
        "g_wg2b": np.ascontiguousarray(inp["L1_ffn_Wg2b"]).astype(np.float32),
        "g_bg1r": inp["L1_ffn_bg1"].reshape(512, 1).astype(np.float32),
        "g_bg2r": inp["L1_ffn_bg2"].reshape(512, 1).astype(np.float32),
        "g_we": np.ascontiguousarray(inp["L1_ffn_We"]).astype(np.float32),
        "g_normwb": np.tile(inp["L1_ffn_norm_w"].astype(np.float32), 8).reshape(1, 512),
        "g_wo": np.ascontiguousarray(inp["L1_ffn_Wo"]).astype(np.float32),
        "g_esel": np.kron(np.eye(8), np.ones((1, 128))).astype(np.float32),
    }
    wshared.update(wa0)
    nc = _fprog()
    im = []
    for c in range(8):
        b, half = c // 2, c % 2
        own = slice(half * Q, (half + 1) * Q)
        m = {
            "x_own": np.ascontiguousarray(x0[b, own].T),
            "a0_exb": np.ascontiguousarray(exbf[:, own]).astype(ml_dtypes.bfloat16),
            "a1_cm": np.ascontiguousarray(cm_full[:, own]).astype(ml_dtypes.bfloat16),
        }
        m.update(wshared)
        im.append(m)
    res = run_bass_kernel_spmd(nc, im, core_ids=list(range(8)))
    out = np.empty((B, Sn, Dm), np.float32)
    for b in range(B):
        out[b] = res.results[2 * b]["yout"].T
    return out
